# version 09
# baseline (speedup 1.0000x reference)
import numpy as np
import ml_dtypes
import concourse.bass as bass
import concourse.mybir as mybir
from concourse.bass_utils import run_bass_kernel_spmd
from concourse.bass_core_rust import IndirectOffsetOnAxis

F32 = mybir.dt.float32
BF16 = mybir.dt.bfloat16
I32 = mybir.dt.int32
U32 = mybir.dt.uint32
AF = mybir.ActivationFunctionType
ALU = mybir.AluOpType
AX = mybir.AxisListType

B, L, D = 4, 4096, 2048
INNER, HC, NB = 4096, 2048, 16
NST, DTR = 16, 128
T = 512
NCH = L // T
E, CAP, F = 8, 512, 8192
NEXP = 8
NTOK = B * L
TOKPC = NTOK // 8
TBL = E * CAP + 128
OOBV = 100000.0

ENG = {"pe": "tensor", "act": "scalar", "dve": "vector", "gp": "gpsimd", "qS": "sync", "qG": "gpsimd"}


class Rec:
    def __init__(self, nc):
        self.nc = nc
        self.streams = {k: [] for k in ["sync", "scalar", "vector", "tensor", "gpsimd"]}
        self.cnt = {"pe": 0, "act": 0, "dve": 0, "gp": 0, "qS": 0, "qG": 0}
        self.semh = {}
        self.ccn = 0
        self._ctx = None
        self.famstream = {}
        self.hwm = {k: {} for k in self.streams}

    def _filt(self, stream, aft):
        best = {}
        for (s, v) in aft:
            if v > best.get(s, 0):
                best[s] = v
        h = self.hwm[stream]
        out = []
        for s, v in best.items():
            if v > h.get(s, 0):
                h[s] = v
                out.append((s, v))
        return out

    def init_sems(self, ctx):
        self._ctx = ctx
        for name in ["pe", "act", "dve", "gp", "qS", "qG"]:
            self.semh[name] = ctx.enter_context(self.nc.semaphore("sem_" + name))

    def op(self, eng, fn, after=()):
        self.cnt[eng] += 1
        tgt = self.cnt[eng]
        aft = [x for x in after if x is not None]
        if eng in ("act", "dve", "gp"):
            aft.append((eng, tgt - 1))
        aft = self._filt(ENG[eng], aft)
        sem = self.semh

        def emit(E, aft=tuple(aft), fn=fn, eng=eng):
            for (s, v) in aft:
                if v > 0:
                    E.wait_ge(sem[s], v)
            fn(E).then_inc(sem[eng], 1)
        self.streams[ENG[eng]].append(emit)
        return (eng, tgt)

    def raw(self, eng, fn, after=()):
        aft = self._filt(ENG[eng], [x for x in after if x is not None])
        sem = self.semh

        def emit(E, aft=tuple(aft), fn=fn):
            for (s, v) in aft:
                if v > 0:
                    E.wait_ge(sem[s], v)
            fn(E)
        self.streams[ENG[eng]].append(emit)

    def dma(self, q, fam, fn, after=()):
        if fam not in self.cnt:
            self.cnt[fam] = 0
            self.semh[fam] = self._ctx.enter_context(self.nc.semaphore("sem_" + fam))
            self.famstream[fam] = ENG[q]
        assert self.famstream[fam] == ENG[q], fam
        self.cnt[fam] += 16
        tgt = self.cnt[fam]
        aft = self._filt(ENG[q], [x for x in after if x is not None])
        sem = self.semh

        def emit(E, aft=tuple(aft), fn=fn, fam=fam):
            for (s, v) in aft:
                if v > 0:
                    E.wait_ge(sem[s], v)
            fn(E).then_inc(sem[fam], 16)
        self.streams[ENG[q]].append(emit)
        return (fam, tgt)

    def cc(self, fn, after=()):
        name = "cc%d" % self.ccn
        self.ccn += 1
        self.semh[name] = self._ctx.enter_context(self.nc.semaphore("sem_" + name))
        aft = self._filt("gpsimd", [x for x in after if x is not None])
        sem = self.semh

        def emit(E, aft=tuple(aft), fn=fn, name=name):
            for (s, v) in aft:
                if v > 0:
                    E.wait_ge(sem[s], v)
            fn(E).then_inc(sem[name])
        self.streams["gpsimd"].append(emit)
        return (name, 1)

    def all_cur(self):
        return [(e, v) for (e, v) in self.cnt.items() if v > 0]


class Guard:
    def __init__(self):
        self.evs = []

    def r(self, ev):
        if ev is not None:
            self.evs.append(ev)
        return ev

    def w(self):
        evs, self.evs = self.evs, []
        return evs


def build_program():
    nc = bass.Bass("TRN2", target_bir_lowering=False, dynamic_dma_scratch_size=65536)
    dt = nc.dram_tensor

    xb_h = dt("xb", [L, D], F32, kind="ExternalInput")
    wuT_h = dt("wuT", [D, HC], F32, kind="ExternalInput")
    wzT_h = dt("wzT", [D, HC], F32, kind="ExternalInput")
    cw = dt("cw", [128, NB * 4], F32, kind="ExternalInput").ap()
    cb = dt("cb", [128, NB], F32, kind="ExternalInput").ap()
    xpT_h = dt("xpT", [HC, 160], F32, kind="ExternalInput")
    dtT_h = dt("dtT", [DTR, HC], F32, kind="ExternalInput")
    dtb = dt("dtb", [128, NB], F32, kind="ExternalInput").ap()
    alog = dt("alog", [128, NB * NST], F32, kind="ExternalInput").ap()
    dvec = dt("dvec", [128, NB], F32, kind="ExternalInput").ap()
    opT_h = dt("opT", [INNER, D], F32, kind="ExternalInput")
    rwT_h = dt("rwT", [D, E], F32, kind="ExternalInput")
    w1T_h = dt("w1T", [D, F], F32, kind="ExternalInput")
    b1r = dt("b1r", [128, F // 128], F32, kind="ExternalInput").ap()
    w2T_h = dt("w2T", [F, D], F32, kind="ExternalInput")
    b2r = dt("b2r", [128, D // 128], F32, kind="ExternalInput").ap()
    ident = dt("ident", [128, 128], F32, kind="ExternalInput").ap()
    xown_h = dt("xown", [TOKPC, D], F32, kind="ExternalInput")
    sids = dt("sids", [128, 4], I32, kind="ExternalInput").ap()
    hoff = dt("hoff", [1, 1], I32, kind="ExternalInput").ap()
    coff = dt("coff", [1, 1], I32, kind="ExternalInput").ap()
    epst = dt("epst", [128, 1], F32, kind="ExternalInput").ap()
    dynoff = dt("dynoff", [1, 16], I32, kind="ExternalInput").ap()

    outown = dt("outown", [TOKPC, D], F32, kind="ExternalOutput").ap()

    u16d_h = dt("u16d", [HC, L], BF16)
    z16d_h = dt("z16d", [HC, L], BF16)
    xdblp_h = dt("xdblp", [160, L], F32)
    xdblr_h = dt("xdblr", [160, L], F32)
    bflat_h = dt("bflat", [NST, T], F32)
    cflat_h = dt("cflat", [NST, T], F32)
    ys16d_h = dt("ys16d", [HC, L], BF16)
    ysf_h = dt("ysf", [INNER, L], BF16)
    xf32o_h = dt("xf32o", [TOKPC, D], F32)
    xf16o_h = dt("xf16o", [TOKPC, D], BF16)
    xffull_h = dt("xffull", [NTOK, D], BF16)
    top2o_h = dt("top2o", [TOKPC, 4], F32)
    top2full_h = dt("top2full", [NTOK, 4], F32)
    gtscr_h = dt("gtscr", [1, E], F32)
    ptab_h = dt("ptab", [3, CAP], F32)
    rtab_h = dt("rtab", [E * 3, CAP], F32)
    hfino_h = dt("hfino", [CAP, D], BF16)
    hfull_h = dt("hfull", [E * CAP, D], BF16)

    from contextlib import ExitStack
    ctx = ExitStack()
    R = Rec(nc)
    R.init_sems(ctx)

    s_id16 = ctx.enter_context(nc.sbuf_tensor("s_id16", [128, 128], BF16))
    s_cw = ctx.enter_context(nc.sbuf_tensor("s_cw", [128, NB * 4], F32))
    s_cb = ctx.enter_context(nc.sbuf_tensor("s_cb", [128, NB], F32))
    s_dtb = ctx.enter_context(nc.sbuf_tensor("s_dtb", [128, NB], F32))
    s_negA = ctx.enter_context(nc.sbuf_tensor("s_negA", [128, NB * NST], F32))
    s_dvec = ctx.enter_context(nc.sbuf_tensor("s_dvec", [128, NB], F32))
    s_eps = ctx.enter_context(nc.sbuf_tensor("s_eps", [128, 1], F32))
    s_dyn = ctx.enter_context(nc.sbuf_tensor("s_dyn", [1, 16], I32))
    s_ones = ctx.enter_context(nc.sbuf_tensor("s_ones", [128, 1], F32))

    R.dma("qS", "pre", lambda E: E.dma_start(out=s_cw[:], in_=cw))
    R.dma("qS", "pre", lambda E: E.dma_start(out=s_cb[:], in_=cb))
    R.dma("qS", "pre", lambda E: E.dma_start(out=s_dtb[:], in_=dtb))
    R.dma("qS", "pre", lambda E: E.dma_start(out=s_negA[:], in_=alog))
    R.dma("qS", "pre", lambda E: E.dma_start(out=s_dvec[:], in_=dvec))
    R.dma("qS", "pre", lambda E: E.dma_start(out=s_eps[:], in_=epst))
    lN = R.dma("qS", "pre", lambda E: E.dma_start(out=s_dyn[:], in_=dynoff))
    idld = R.dma("qG", "idg", lambda E: E.dma_start(out=s_id16[:], in_=ident))
    idc = idld
    e1 = R.op("act", lambda E: E.activation(out=s_negA[:], in_=s_negA[:], func=AF.Exp),
               after=[lN])
    negA_done = R.op("dve", lambda E: E.tensor_scalar(out=s_negA[:], in0=s_negA[:], scalar1=-1.0,
                                                      scalar2=None, op0=ALU.mult), after=[e1])
    ones1 = R.op("dve", lambda E: E.memset(s_ones[:], 1.0))
    r_t = nc.gpsimd.alloc_register("r_t")

    with ExitStack() as P:
        EE = P.enter_context
        s_wuT = [EE(nc.sbuf_tensor("s_wuT%d" % k, [128, HC], BF16)) for k in range(16)]
        s_wzT = [EE(nc.sbuf_tensor("s_wzT%d" % k, [128, HC], BF16)) for k in range(16)]
        s_xpT = [EE(nc.sbuf_tensor("s_xpT%d" % i, [128, 160], BF16)) for i in range(NB)]
        s_xc = [EE(nc.sbuf_tensor("s_xc%d" % q, [128, D], BF16)) for q in range(1)]
        s_xT = [EE(nc.sbuf_tensor("s_xTT%d" % k, [128, T], BF16)) for k in range(16)]
        s_upre = EE(nc.sbuf_tensor("s_upre", [128, T + 3], BF16))
        s_halo = EE(nc.sbuf_tensor("s_halo", [128, 3 * NB], BF16))
        s_cvt = EE(nc.sbuf_tensor("s_cvt", [128, T], F32))
        s_u16t = EE(nc.sbuf_tensor("s_u16t", [128, T], BF16))
        s_z16t = s_u16t
        p_tr = [EE(nc.psum_tensor("p_tr%d" % s, [128, 128], BF16)) for s in range(2)]
        p_mm = [EE(nc.psum_tensor("p_mm%d" % s, [128, T], F32)) for s in range(2)]
        p_dt = EE(nc.psum_tensor("p_dt", [128, T], F32))
        p_bc = EE(nc.psum_tensor("p_bc", [32, T], F32))

        for k in range(16):
            R.dma("qG", "wp1", lambda E, k=k: E.dma_start(out=s_wuT[k][:], in_=wuT_h.ap()[k * 128:(k + 1) * 128, :]))
            R.dma("qG", "wp1", lambda E, k=k: E.dma_start(out=s_wzT[k][:], in_=wzT_h.ap()[k * 128:(k + 1) * 128, :]))
        for i in range(NB):
            wl = R.dma("qG", "wp1", lambda E, i=i: E.dma_start(out=s_xpT[i][:], in_=xpT_h.ap()[i * 128:(i + 1) * 128, :]))
        w_done = wl
        halo0 = R.op("dve", lambda E: E.memset(s_halo[:], 0.0))

        g_xc = [Guard() for _ in range(1)]
        g_xT = [Guard() for _ in range(16)]
        g_tr = [Guard() for _ in range(2)]
        g_mm = [Guard() for _ in range(2)]
        g_u16t = Guard(); g_z16t = g_u16t
        g_upre = Guard(); g_halo = Guard()
        g_dtp = Guard(); g_cvt = Guard()

        for k in range(NCH):
            wEs = [g_xT[i].w() for i in range(16)]
            for q in range(4):
                xld = R.dma("qG", "xld", lambda E, q=q, k=k: E.dma_start(
                    out=s_xc[0][:], in_=xb_h.ap()[k * T + q * 128: k * T + (q + 1) * 128, :]),
                    after=g_xc[0].w())
                for i in range(16):
                    slot = i % 2
                    tr = R.op("pe", lambda E, i=i, q=q, slot=slot: E.transpose(
                        out=p_tr[slot][:], in_=s_xc[0][:, i * 128:(i + 1) * 128],
                        identity=s_id16[:]),
                        after=[xld, idc] + g_tr[slot].w()
                        + (wEs[i] if q == 0 else []))
                    g_xc[0].r(tr)
                    ev = R.op("act", lambda E, i=i, q=q, slot=slot: E.activation(
                        out=s_xT[i][:, q * 128:(q + 1) * 128], in_=p_tr[slot][:], func=AF.Copy),
                        after=[tr])
                    g_tr[slot].r(ev)
            xT_all = ("act", R.cnt["act"])

            for j in range(32):
                isu = j < 16
                i = j if isu else j - 16
                wt = s_wuT if isu else s_wzT
                bank = j % 2
                mm = None
                for kk in range(16):
                    aft = []
                    if kk == 0:
                        aft = [xT_all, w_done] + g_mm[bank].w()
                        if j == 0 and k > 0:
                            aft += g_dtp.w()
                    mm = R.op("pe", lambda E, kk=kk, i=i, wt=wt, bank=bank: E.matmul(
                        out=p_mm[bank][:], lhsT=wt[kk][:, i * 128:(i + 1) * 128], rhs=s_xT[kk][:],
                        start=(kk == 0), stop=(kk == 15)), after=aft)
                for kk in range(16):
                    g_xT[kk].r(mm)
                if isu:
                    ev = R.op("act", lambda E, i=i, bank=bank: E.activation(
                        out=s_upre[:, 3:T + 3], in_=p_mm[bank][:], func=AF.Copy),
                        after=[mm] + g_upre.w())
                    g_mm[bank].r(ev)
                    hcp = R.op("dve", lambda E, i=i: E.tensor_copy(
                        out=s_upre[:, 0:3], in_=s_halo[:, 3 * i:3 * i + 3]),
                        after=[halo0] + g_halo.w())
                    c1 = R.op("dve", lambda E, i=i: E.tensor_scalar(
                        out=s_cvt[:], in0=s_upre[:, 0:T], scalar1=s_cw[:, 4 * i:4 * i + 1],
                        scalar2=None, op0=ALU.mult), after=[ev, hcp] + g_cvt.w())
                    for kk in range(1, 4):
                        c1 = R.op("dve", lambda E, i=i, kk=kk: E.scalar_tensor_tensor(
                            out=s_cvt[:], in0=s_upre[:, kk:kk + T],
                            scalar=s_cw[:, 4 * i + kk:4 * i + kk + 1],
                            in1=s_cvt[:], op0=ALU.mult, op1=ALU.add), after=[c1])
                    hal = R.op("dve", lambda E, i=i: E.tensor_copy(
                        out=s_halo[:, 3 * i:3 * i + 3], in_=s_upre[:, T:T + 3]), after=[c1])
                    g_upre.r(hal)
                    g_halo.r(hal)
                    sil = R.op("act", lambda E, i=i: E.activation(
                        out=s_u16t[:], in_=s_cvt[:], func=AF.Silu, bias=s_cb[:, i:i + 1]),
                        after=[c1] + g_u16t.w())
                    g_cvt.r(sil)
                    st_u = R.dma("qS", "stuz", lambda E, i=i, k=k: E.dma_start(
                        out=u16d_h.ap()[i * 128:(i + 1) * 128, k * T:(k + 1) * T], in_=s_u16t[:]),
                        after=[sil])
                    m1 = R.op("pe", lambda E, i=i: E.matmul(
                        out=p_dt[:], lhsT=s_xpT[i][:, 0:128], rhs=s_u16t[:],
                        start=(i == 0), stop=(i == 15)), after=[sil])
                    m2 = R.op("pe", lambda E, i=i: E.matmul(
                        out=p_bc[:], lhsT=s_xpT[i][:, 128:160], rhs=s_u16t[:],
                        start=(i == 0), stop=(i == 15)), after=[sil])
                    g_u16t.r(st_u); g_u16t.r(m1); g_u16t.r(m2)
                else:
                    ev = R.op("act", lambda E, bank=bank: E.activation(
                        out=s_z16t[:], in_=p_mm[bank][:], func=AF.Copy),
                        after=[mm] + g_z16t.w())
                    g_mm[bank].r(ev)
                    st_z = R.dma("qS", "stuz", lambda E, i=i, k=k: E.dma_start(
                        out=z16d_h.ap()[i * 128:(i + 1) * 128, k * T:(k + 1) * T], in_=s_z16t[:]),
                        after=[ev])
                    g_z16t.r(st_z)

            dtev = R.op("dve", lambda E: E.tensor_copy(out=s_cvt[:], in_=p_dt[:]),
                        after=[m1] + g_cvt.w())
            g_dtp.r(dtev)
            st1 = R.dma("qS", "cvt", lambda E, k=k: E.dma_start(
                out=xdblp_h.ap()[0:128, k * T:(k + 1) * T], in_=s_cvt[:]), after=[dtev])
            g_cvt.r(st1)
            bcev = R.op("dve", lambda E: E.tensor_copy(out=s_cvt[0:32, :], in_=p_bc[:]),
                        after=[m2, st1] + g_cvt.w())
            g_dtp.r(bcev)
            st2 = R.dma("qS", "cvt", lambda E, k=k: E.dma_start(
                out=xdblp_h.ap()[128:160, k * T:(k + 1) * T], in_=s_cvt[0:32, :]), after=[bcev])
            g_cvt.r(st2)

        p1_tail = R.all_cur()

    c_xdbl = R.cc(lambda E: E.collective_compute(
        "AllReduce", ALU.add, replica_groups=[[0, 1], [2, 3], [4, 5], [6, 7]],
        ins=[xdblp_h.ap().opt()], outs=[xdblr_h.ap().opt()]), after=p1_tail)

    with ExitStack() as P:
        EE = P.enter_context
        s_dtT = EE(nc.sbuf_tensor("s_dtT", [128, HC], BF16))
        s_carry = EE(nc.sbuf_tensor("s_carry", [128, NB * NST], F32))
        s_dt16 = EE(nc.sbuf_tensor("s_dt16", [128, T], BF16))
        s_Brep = EE(nc.sbuf_tensor("s_Brep", [128, NST * T], BF16))
        s_Crep = EE(nc.sbuf_tensor("s_Crep", [128, NST * T], BF16))
        s_bcfl = EE(nc.sbuf_tensor("s_bcfl", [33, 2048], BF16))
        s_on1 = EE(nc.sbuf_tensor("s_on1", [33, 128], BF16))
        s_dA = EE(nc.sbuf_tensor("s_dA", [128, NST * T], F32))
        s_bt = EE(nc.sbuf_tensor("s_bt", [128, NST * T], BF16))
        s_h = EE(nc.sbuf_tensor("s_h", [128, NST * T], BF16))
        s_del = EE(nc.sbuf_tensor("s_del", [128, T], F32))
        s_t1 = EE(nc.sbuf_tensor("s_t1", [128, T], F32))
        s_u16 = EE(nc.sbuf_tensor("s_u16", [128, T], BF16))
        s_du16 = EE(nc.sbuf_tensor("s_du16", [128, T], BF16))
        s_z16 = EE(nc.sbuf_tensor("s_z16", [128, T], BF16))
        s_y = EE(nc.sbuf_tensor("s_y", [128, T], F32))
        s_sz = EE(nc.sbuf_tensor("s_sz", [128, T], BF16))
        s_ys16 = EE(nc.sbuf_tensor("s_ys16", [128, T], BF16))
        p_d = EE(nc.psum_tensor("p_d", [128, T], F32))
        p_rep = [EE(nc.psum_tensor("p_rep%d" % s, [128, 1024], F32)) for s in range(2)]

        dtT_ld = R.dma("qG", "dtt", lambda E: E.dma_start(out=s_dtT[:], in_=dtT_h.ap()), after=[c_xdbl])
        on1v = R.op("dve", lambda E: E.memset(s_on1[:], 1.0))
        carry0 = R.op("dve", lambda E: E.memset(s_carry[:], 0.0))

        g_dt16 = Guard()
        g_Brep = Guard(); g_Crep = Guard(); g_bflat = Guard(); g_cflat = Guard()
        g_pd = Guard(); g_t1 = Guard(); g_del = Guard()
        g_u16 = Guard(); g_z16 = Guard(); g_dA = Guard()
        g_bt = Guard(); g_h = Guard(); g_y = Guard(); g_sz = Guard(); g_ys = Guard()
        g_du16 = Guard()

        for k in range(NCH):
            dtl = R.dma("qG", "dtl", lambda E, k=k: E.dma_start(
                out=s_dt16[:], in_=xdblr_h.ap()[0:128, k * T:(k + 1) * T]),
                after=[c_xdbl] + g_dt16.w())
            cast_dt = dtl
            bfd = R.dma("qS", "bfd", lambda E, k=k: E.dma_start(
                out=bflat_h.ap(), in_=xdblr_h.ap()[128:144, k * T:(k + 1) * T]),
                after=[c_xdbl] + g_bflat.w())
            cfd = R.dma("qS", "cfd", lambda E, k=k: E.dma_start(
                out=cflat_h.ap(), in_=xdblr_h.ap()[144:160, k * T:(k + 1) * T]),
                after=[c_xdbl] + g_cflat.w())
            g_prep = getattr(R, "_g_prep", None)
            if g_prep is None:
                g_prep = R._g_prep = [Guard(), Guard()]
                R._g_bcfl = [Guard(), Guard()]
            g_bcfl = R._g_bcfl
            brl = crl = None
            for rep_i, (row0, s_rep, flat_ap, dep) in enumerate(
                    [(0, s_Brep, bflat_h, bfd), (32, s_Crep, cflat_h, cfd)]):
                last = None
                for seg in range(NST * T // 1024):
                    par = seg % 2
                    fl_ld = R.dma("qG", "fl%d" % par, lambda E, row0=row0, seg=seg, par=par, flat_ap=flat_ap: E.dma_start(
                        out=s_bcfl[row0:row0 + 1, par * 1024:(par + 1) * 1024],
                        in_=flat_ap.ap().rearrange("n t -> (n t)").unsqueeze(0)
                        [:, seg * 1024:(seg + 1) * 1024]),
                        after=[dep] + g_bcfl[par].w())
                    mmr0 = R.op("pe", lambda E, row0=row0, par=par: E.matmul(
                        out=p_rep[par][:, 0:512], lhsT=s_on1[row0:row0 + 1, :],
                        rhs=s_bcfl[row0:row0 + 1, par * 1024:par * 1024 + 512],
                        start=True, stop=True),
                        after=[fl_ld, on1v] + g_prep[par].w())
                    mmr = R.op("pe", lambda E, row0=row0, par=par: E.matmul(
                        out=p_rep[par][:, 512:1024], lhsT=s_on1[row0:row0 + 1, :],
                        rhs=s_bcfl[row0:row0 + 1, par * 1024 + 512:(par + 1) * 1024],
                        start=True, stop=True))
                    g_bcfl[par].r(mmr)
                    evr = R.op("act", lambda E, s_rep=s_rep, seg=seg, par=par: E.activation(
                        out=s_rep[:, seg * 1024:(seg + 1) * 1024], in_=p_rep[par][:],
                        func=AF.Copy), after=[mmr])
                    g_prep[par].r(evr)
                    last = evr
                if rep_i == 0:
                    brl = last
                    g_bflat.r(fl_ld)
                else:
                    crl = last
                    g_cflat.r(fl_ld)

            for i in range(NB):
                mmd = R.op("pe", lambda E, i=i: E.matmul(
                    out=p_d[:], lhsT=s_dtT[:, i * 128:(i + 1) * 128], rhs=s_dt16[:],
                    start=True, stop=True),
                    after=[cast_dt, dtT_ld] + g_pd.w())
                g_dt16.r(mmd)
                a1 = R.op("act", lambda E, i=i: E.activation(
                    out=s_t1[:], in_=p_d[:], func=AF.Abs, bias=s_dtb[:, i:i + 1]),
                    after=[mmd] + g_t1.w())
                a2 = R.op("act", lambda E: E.activation(out=s_t1[:], in_=s_t1[:], func=AF.Exp, scale=-1.0))
                a3 = R.op("act", lambda E: E.activation(out=s_t1[:], in_=s_t1[:], func=AF.Ln, bias=1.0))
                a4 = R.op("dve", lambda E, i=i: E.tensor_scalar(
                    out=s_del[:], in0=p_d[:], scalar1=s_dtb[:, i:i + 1], scalar2=0.0,
                    op0=ALU.add, op1=ALU.max),
                    after=[mmd] + g_del.w())
                g_pd.r(a1); g_pd.r(a4)
                dl = R.op("dve", lambda E: E.tensor_tensor(out=s_del[:], in0=s_del[:], in1=s_t1[:],
                                                           op=ALU.add), after=[a3, a4])
                g_t1.r(dl)
                ul = R.dma("qS", "ul", lambda E, i=i, k=k: E.dma_start(
                    out=s_u16[:], in_=u16d_h.ap()[i * 128:(i + 1) * 128, k * T:(k + 1) * T]),
                    after=g_u16.w())
                du = R.op("dve", lambda E: E.tensor_tensor(out=s_du16[:], in0=s_del[:], in1=s_u16[:],
                                                           op=ALU.mult),
                          after=[dl, ul] + g_du16.w())
                exps = []
                dAw = g_dA.w()
                for n in range(NST):
                    exps.append(R.op("act", lambda E, i=i, n=n: E.activation(
                        out=s_dA[:, n * T:(n + 1) * T], in_=s_del[:], func=AF.Exp,
                        scale=s_negA[:, i * NST + n: i * NST + n + 1]),
                        after=[dl, negA_done] + (dAw if n == 0 else [])))
                    g_del.r(exps[n])
                du_bc = s_du16[:].unsqueeze(1).broadcast_to([128, NST, T])
                bt = R.op("gp", lambda E, du_bc=du_bc: E.tensor_tensor(
                    out=s_bt.rearrange("p (n t) -> p n t", n=NST), in0=du_bc,
                    in1=s_Brep.rearrange("p (n t) -> p n t", n=NST), op=ALU.mult),
                    after=[du, brl] + g_bt.w())
                g_du16.r(bt); g_Brep.r(bt)
                sc = bt
                hw = g_h.w()
                for n in range(NST):
                    sc = R.op("dve", lambda E, i=i, n=n: E.tensor_tensor_scan(
                        out=s_h[:, n * T:(n + 1) * T], data0=s_dA[:, n * T:(n + 1) * T],
                        data1=s_bt[:, n * T:(n + 1) * T],
                        initial=s_carry[:, i * NST + n: i * NST + n + 1],
                        op0=ALU.mult, op1=ALU.add),
                        after=[exps[n], carry0, bt] + (hw if n == 0 else []))
                    g_dA.r(sc)
                cu = R.op("dve", lambda E, i=i: E.tensor_copy(
                    out=s_carry[:, i * NST:(i + 1) * NST],
                    in_=s_h[:].rearrange("p (n t) -> p n t", n=NST)[:, :, T - 1:T].squeeze(2)),
                    after=[sc])
                hc = R.op("gp", lambda E: E.tensor_tensor(out=s_bt[:], in0=s_h[:], in1=s_Crep[:],
                                                          op=ALU.mult), after=[cu, crl])
                g_h.r(hc); g_Crep.r(hc)
                tr1 = R.op("dve", lambda E: E.tensor_tensor(
                    out=s_bt[:, 0:8 * T], in0=s_bt[:, 0:8 * T], in1=s_bt[:, 8 * T:16 * T],
                    op=ALU.add), after=[hc])
                tr2 = R.op("dve", lambda E: E.tensor_tensor(
                    out=s_bt[:, 0:4 * T], in0=s_bt[:, 0:4 * T], in1=s_bt[:, 4 * T:8 * T],
                    op=ALU.add))
                tr3 = R.op("dve", lambda E: E.tensor_tensor(
                    out=s_bt[:, 0:2 * T], in0=s_bt[:, 0:2 * T], in1=s_bt[:, 2 * T:4 * T],
                    op=ALU.add))
                yr = R.op("dve", lambda E: E.tensor_tensor(
                    out=s_y[:], in0=s_bt[:, 0:T], in1=s_bt[:, T:2 * T],
                    op=ALU.add), after=g_y.w())
                g_bt.r(yr)
                zl = R.dma("qS", "zl", lambda E, i=i, k=k: E.dma_start(
                    out=s_z16[:], in_=z16d_h.ap()[i * 128:(i + 1) * 128, k * T:(k + 1) * T]),
                    after=g_z16.w())
                szv = R.op("act", lambda E: E.activation(out=s_sz[:], in_=s_z16[:], func=AF.Silu),
                           after=[zl] + g_sz.w())
                g_z16.r(szv)
                y2 = R.op("dve", lambda E, i=i: E.scalar_tensor_tensor(
                    out=s_y[:], in0=s_u16[:], scalar=s_dvec[:, i:i + 1], in1=s_y[:],
                    op0=ALU.mult, op1=ALU.add), after=[yr])
                g_u16.r(y2)
                ysv = R.op("dve", lambda E: E.tensor_tensor(out=s_ys16[:], in0=s_y[:], in1=s_sz[:],
                                                            op=ALU.mult), after=[y2, szv] + g_ys.w())
                g_y.r(ysv); g_sz.r(ysv)
                yst = R.dma("qS", "yst", lambda E, i=i, k=k: E.dma_start(
                    out=ys16d_h.ap()[i * 128:(i + 1) * 128, k * T:(k + 1) * T], in_=s_ys16[:]),
                    after=[ysv])
                g_ys.r(yst)
        p2_tail = R.all_cur()

    c_ys = R.cc(lambda E: E.collective_compute(
        "AllGather", ALU.bypass, replica_groups=[[0, 1], [2, 3], [4, 5], [6, 7]],
        ins=[ys16d_h.ap().opt()], outs=[ysf_h.ap().opt()]), after=p2_tail)

    with ExitStack() as P:
        EE = P.enter_context
        s_opk = [EE(nc.sbuf_tensor("s_opk%d" % s, [128, D], BF16)) for s in range(2)]
        s_id32_3 = EE(nc.sbuf_tensor("s_id32p3", [128, 128], F32))
        s_ysall = EE(nc.sbuf_tensor("s_ysall", [128, 32 * 256], BF16))
        s_sT = [EE(nc.sbuf_tensor("s_sTq%d" % q, [128, D], F32)) for q in range(2)]
        s_xrow = EE(nc.sbuf_tensor("s_xrow", [128, D], F32))
        s_mu = EE(nc.sbuf_tensor("s_mu", [128, 1], F32))
        s_mu2 = EE(nc.sbuf_tensor("s_mu2", [128, 1], F32))
        s_m2 = EE(nc.sbuf_tensor("s_m2", [128, 1], F32))
        s_sd = EE(nc.sbuf_tensor("s_sd", [128, 1], F32))
        s_rstd = EE(nc.sbuf_tensor("s_rstd", [128, 1], F32))
        s_xf = EE(nc.sbuf_tensor("s_xf", [128, D], F32))
        s_sq = EE(nc.sbuf_tensor("s_sq", [128, D], F32))
        s_xf16 = EE(nc.sbuf_tensor("s_xf16", [128, D], BF16))
        s_xft = [EE(nc.sbuf_tensor("s_xft%d" % kk, [128, 128], F32)) for kk in range(16)]
        s_rwT = EE(nc.sbuf_tensor("s_rwT", [128, 16 * NEXP], F32))
        s_lg = EE(nc.sbuf_tensor("s_lg", [8, 128], F32))
        s_lgt = EE(nc.sbuf_tensor("s_lgt", [128, 8], F32))
        s_mx = EE(nc.sbuf_tensor("s_mx", [128, 8], F32))
        s_mi = EE(nc.sbuf_tensor("s_mi", [128, 8], U32))
        s_nm = EE(nc.sbuf_tensor("s_nm", [128, 1], F32))
        s_ex = EE(nc.sbuf_tensor("s_ex", [128, 8], F32))
        s_ssum = EE(nc.sbuf_tensor("s_ssum", [128, 1], F32))
        s_rs = EE(nc.sbuf_tensor("s_rs", [128, 1], F32))
        s_t2p = EE(nc.sbuf_tensor("s_t2p", [128, 4], F32))
        p_sw = [EE(nc.psum_tensor("p_sw%d" % s, [128, 2048], F32)) for s in range(2)]

        id32l = R.dma("qS", "id3", lambda E: E.dma_start(out=s_id32_3[:], in_=ident), after=[c_ys])
        rwld = R.dma("qS", "rw", lambda E: E.dma_start(
            out=s_rwT[:].rearrange("p (kk e) -> p kk e", e=NEXP),
            in_=bass.AP(rwT_h, 0, [[NEXP, 128], [128 * NEXP, 16], [1, NEXP]])), after=[c_ys])

        g_ysall = Guard()
        g_opk = [Guard() for _ in range(2)]
        g_psw = [Guard(), Guard()]
        g_sT = [Guard() for _ in range(2)]
        g_xrow = Guard(); g_xf = Guard(); g_sq = Guard(); g_xf16 = Guard()
        g_ptr = [Guard(), Guard()]; g_plg = Guard(); g_ptl = Guard()
        g_xft = [Guard() for _ in range(16)]
        g_lg = Guard(); g_lgt = Guard()
        g_mx = Guard(); g_mi = Guard(); g_nm = Guard(); g_ex = Guard()
        g_ssum = Guard(); g_rs = Guard(); g_t2p = Guard()
        g_mu = Guard(); g_mu2 = Guard(); g_m2 = Guard(); g_sd = Guard(); g_rstd = Guard()

        for k in range(NCH):
            jofs = k
            g_w = g_ysall.w()

            def do_regs(E, jofs=jofs):
                E.reg_load(r_t, s_dyn[0:1, jofs:jofs + 1])
            R.raw("gp", do_regs, after=[lN])
            ysld = R.dma("qG", "ys", lambda E: E.dma_start(
                out=s_ysall[:].rearrange("p (kk t) -> p kk t", t=256),
                in_=bass.AP(ysf_h, r_t, [[L, 128], [128 * L, 32], [1, 256]])),
                after=[c_ys] + g_w)
            ys_all = ysld

            mm = None
            lastq = [None, None]
            for kk in range(32):
                buf = kk % 2
                opl = R.dma("qG", "op%d" % buf, lambda E, kk=kk, buf=buf: E.dma_start(
                    out=s_opk[buf][:], in_=opT_h.ap()[kk * 128:(kk + 1) * 128, :]),
                    after=g_opk[buf].w() + ([c_ys] if k == 0 and kk < 2 else []))
                for q in range(2):
                    for dq in range(4):
                        aft = []
                        if dq == 0:
                            aft = [ys_all, opl]
                            if kk == 0:
                                aft = aft + g_psw[q].w()
                        mm = R.op("pe", lambda E, kk=kk, q=q, dq=dq, buf=buf: E.matmul(
                            out=p_sw[q][:, dq * 512:(dq + 1) * 512],
                            lhsT=s_ysall[:, kk * 256 + q * 128:kk * 256 + (q + 1) * 128],
                            rhs=s_opk[buf][:, dq * 512:(dq + 1) * 512],
                            start=(kk == 0), stop=(kk == 31)), after=aft)
                    lastq[q] = mm
                g_opk[buf].r(mm)
            g_ysall.r(mm)
            for q in range(2):
                ev = R.op("act", lambda E, q=q: E.activation(
                    out=s_sT[q][:], in_=p_sw[q][:], func=AF.Copy),
                    after=[lastq[q]] + g_sT[q].w())
                g_psw[q].r(ev)
            sT_done = ("act", R.cnt["act"])

            for q in range(2):
                row0 = k * 256 + q * 128
                xl = R.dma("qS", "xr3", lambda E, row0=row0: E.dma_start(
                    out=s_xrow[:], in_=xown_h.ap()[row0:row0 + 128, :]), after=g_xrow.w())
                add = R.op("dve", lambda E, q=q: E.tensor_tensor(
                    out=s_sT[q][:], in0=s_sT[q][:], in1=s_xrow[:], op=ALU.add),
                    after=[sT_done, xl])
                g_xrow.r(add)
                mu = R.op("dve", lambda E, q=q: E.tensor_reduce(
                    out=s_mu[:], in_=s_sT[q][:], axis=AX.X, op=ALU.add), after=[add] + g_mu.w())
                mub = R.op("dve", lambda E: E.tensor_scalar(
                    out=s_mu[:], in0=s_mu[:], scalar1=1.0 / D, scalar2=None, op0=ALU.mult))
                sq = R.op("act", lambda E, q=q: E.activation(
                    out=s_sq[:], in_=s_sT[q][:], func=AF.Square), after=[add] + g_sq.w())
                m2a = R.op("dve", lambda E: E.tensor_reduce(
                    out=s_m2[:], in_=s_sq[:], axis=AX.X, op=ALU.add), after=[sq] + g_m2.w())
                g_sq.r(m2a)
                m2b = R.op("dve", lambda E: E.tensor_scalar(
                    out=s_m2[:], in0=s_m2[:], scalar1=1.0 / D, scalar2=None, op0=ALU.mult))
                mu2v = R.op("dve", lambda E: E.tensor_scalar(
                    out=s_mu2[:], in0=s_mu[:], scalar1=s_mu[:, 0:1], scalar2=None, op0=ALU.mult),
                    after=[mub] + g_mu2.w())
                vv = R.op("dve", lambda E: E.tensor_tensor(
                    out=s_m2[:], in0=s_m2[:], in1=s_mu2[:], op=ALU.subtract), after=[m2b, mu2v])
                g_mu2.r(vv)
                sdv = R.op("act", lambda E: E.activation(
                    out=s_sd[:], in_=s_m2[:], func=AF.Sqrt, bias=s_eps[:, 0:1]),
                    after=[vv] + g_sd.w())
                g_m2.r(sdv)
                rst = R.op("dve", lambda E: E.reciprocal(out=s_rstd[:], in_=s_sd[:]),
                           after=[sdv] + g_rstd.w())
                g_sd.r(rst)
                nmu = R.op("dve", lambda E: E.tensor_scalar(
                    out=s_mu[:], in0=s_mu[:], scalar1=-1.0, scalar2=None, op0=ALU.mult),
                    after=[mu2v])
                g_mu.r(nmu)
                xfv = R.op("dve", lambda E, q=q: E.tensor_scalar(
                    out=s_xf[:], in0=s_sT[q][:], scalar1=s_mu[:, 0:1], scalar2=s_rstd[:, 0:1],
                    op0=ALU.add, op1=ALU.mult), after=[nmu, rst] + g_xf.w())
                g_sT[q].r(xfv); g_mu.r(xfv); g_rstd.r(xfv)
                xf16v = R.op("dve", lambda E: E.tensor_copy(out=s_xf16[:], in_=s_xf[:]),
                             after=[xfv] + g_xf16.w())
                st32 = R.dma("qS", "xf32s", lambda E, row0=row0: E.dma_start(
                    out=xf32o_h.ap()[row0:row0 + 128, :], in_=s_xf[:]), after=[xfv])
                st16 = R.dma("qS", "xf16s", lambda E, row0=row0: E.dma_start(
                    out=xf16o_h.ap()[row0:row0 + 128, :], in_=s_xf16[:]), after=[xf16v])
                g_xf16.r(st16)
                g_xf.r(st32)
                for kk in range(16):
                    slot = kk % 2
                    tr = R.op("pe", lambda E, kk=kk, slot=slot: E.transpose(
                        out=p_sw[0][:, slot * 512:slot * 512 + 128],
                        in_=s_xf[:, kk * 128:(kk + 1) * 128],
                        identity=s_id32_3[:]),
                        after=[xfv, id32l] + g_ptr[slot].w())
                    g_xf.r(tr)
                    tev = R.op("act", lambda E, kk=kk, slot=slot: E.activation(
                        out=s_xft[kk][:], in_=p_sw[0][:, slot * 512:slot * 512 + 128],
                        func=AF.Copy),
                        after=[tr] + g_xft[kk].w())
                    g_ptr[slot].r(tev)
                    g_psw[0].r(tev)
                xft_done = ("act", R.cnt["act"])
                mm2 = None
                for kk in range(16):
                    aft = ([xft_done, rwld] + g_plg.w()) if kk == 0 else []
                    mm2 = R.op("pe", lambda E, kk=kk: E.matmul(
                        out=p_sw[1][0:8, 0:128], lhsT=s_rwT[:, kk * NEXP:(kk + 1) * NEXP],
                        rhs=s_xft[kk][:],
                        start=(kk == 0), stop=(kk == 15)), after=aft)
                for kk in range(16):
                    g_xft[kk].r(mm2)
                lgev = R.op("act", lambda E: E.activation(out=s_lg[:], in_=p_sw[1][0:8, 0:128],
                                                          func=AF.Copy),
                            after=[mm2] + g_lg.w())
                g_plg.r(lgev)
                g_psw[1].r(lgev)
                trl = R.op("pe", lambda E: E.transpose(
                    out=p_sw[1][:, 512:520], in_=s_lg[0:8, :], identity=s_id32_3[0:8, 0:8]),
                    after=[lgev] + g_ptl.w())
                g_lg.r(trl)
                ltv = R.op("act", lambda E: E.activation(out=s_lgt[:], in_=p_sw[1][:, 512:520],
                                                         func=AF.Copy), after=[trl] + g_lgt.w())
                g_ptl.r(ltv)
                g_psw[1].r(ltv)
                mx = R.op("dve", lambda E: E.max(out=s_mx[:], in_=s_lgt[:]),
                          after=[ltv] + g_mx.w())
                g_lgt.r(mx)
                mi = R.op("dve", lambda E: E.max_index(out=s_mi[:], in_max=s_mx[:], in_values=s_lgt[:]),
                          after=[mx] + g_mi.w())
                g_lgt.r(mi)
                nm = R.op("dve", lambda E: E.tensor_scalar(
                    out=s_nm[:], in0=s_mx[:, 0:1], scalar1=-1.0, scalar2=None, op0=ALU.mult),
                    after=[mx] + g_nm.w())
                exv = R.op("act", lambda E: E.activation(out=s_ex[:], in_=s_mx[:], func=AF.Exp,
                                                         bias=s_nm[:, 0:1]),
                           after=[nm] + g_ex.w())
                g_mx.r(exv); g_nm.r(exv)
                ssv = R.op("dve", lambda E: E.tensor_reduce(
                    out=s_ssum[:], in_=s_ex[:], axis=AX.X, op=ALU.add), after=[exv] + g_ssum.w())
                rsv = R.op("dve", lambda E: E.reciprocal(out=s_rs[:], in_=s_ssum[:]),
                           after=[ssv] + g_rs.w())
                g_ssum.r(rsv)
                c0 = R.op("dve", lambda E: E.tensor_copy(out=s_t2p[:, 0:1], in_=s_mi[:, 0:1]),
                          after=[mi] + g_t2p.w())
                c1v = R.op("dve", lambda E: E.tensor_copy(out=s_t2p[:, 1:2], in_=s_mi[:, 1:2]))
                g_mi.r(c1v)
                c2 = R.op("dve", lambda E: E.tensor_tensor(out=s_t2p[:, 2:3], in0=s_ex[:, 0:1],
                                                           in1=s_rs[:, 0:1], op=ALU.mult),
                          after=[rsv])
                c3 = R.op("dve", lambda E: E.tensor_tensor(out=s_t2p[:, 3:4], in0=s_ex[:, 1:2],
                                                           in1=s_rs[:, 0:1], op=ALU.mult))
                g_ex.r(c3); g_rs.r(c3)
                stt2 = R.dma("qS", "t2s", lambda E, row0=row0: E.dma_start(
                    out=top2o_h.ap()[row0:row0 + 128, :], in_=s_t2p[:]), after=[c3, c0, c1v, c2])
                g_t2p.r(stt2)
        p3_tail = R.all_cur()

    c_xf = R.cc(lambda E: E.collective_compute(
        "AllGather", ALU.bypass, replica_groups=[list(range(8))],
        ins=[xf16o_h.ap().opt()], outs=[xffull_h.ap().opt()]), after=p3_tail)
    c_t2 = R.cc(lambda E: E.collective_compute(
        "AllGather", ALU.bypass, replica_groups=[list(range(8))],
        ins=[top2o_h.ap().opt()], outs=[top2full_h.ap().opt()]), after=[c_xf])

    with ExitStack() as P:
        EE = P.enter_context
        s_id32_4 = EE(nc.sbuf_tensor("s_id32p4", [128, 128], F32))
        s_b1r = EE(nc.sbuf_tensor("s_b1r", [128, F // 128], F32))
        s_b2r = EE(nc.sbuf_tensor("s_b2r", [128, D // 128], F32))
        s_ssf = EE(nc.sbuf_tensor("s_ssf", [128, 4], F32))
        s_ssi = EE(nc.sbuf_tensor("s_ssi", [128, 4], I32))
        P4r = ExitStack()
        ER = P4r.enter_context
        s_r4 = ER(nc.sbuf_tensor("s_r4", [128, 512], F32))
        s_E1 = ER(nc.sbuf_tensor("s_E1", [128, 128], F32))
        s_E2 = ER(nc.sbuf_tensor("s_E2", [128, 128], F32))
        s_P1 = ER(nc.sbuf_tensor("s_P1", [128, 128], F32))
        s_P2 = ER(nc.sbuf_tensor("s_P2", [128, 128], F32))
        s_oh = ER(nc.sbuf_tensor("s_oh", [128, 8 * 128], F32))
        s_pos = ER(nc.sbuf_tensor("s_pos", [128, 8 * 128], F32))
        s_keep = ER(nc.sbuf_tensor("s_keep", [128, 8 * 128], F32))
        s_cum = ER(nc.sbuf_tensor("s_cum", [128, 128], F32))
        s_tot = ER(nc.sbuf_tensor("s_tot", [128, 8], F32))
        s_TB = ER(nc.sbuf_tensor("s_TB", [8, 128], F32))
        s_TBc = ER(nc.sbuf_tensor("s_TBc", [8, 128], F32))
        s_prefP = ER(nc.sbuf_tensor("s_prefP", [128, 8], F32))
        s_gt8 = ER(nc.sbuf_tensor("s_gt8", [8, 1], F32))
        s_gtb = ER(nc.sbuf_tensor("s_gtb", [128, 8], F32))
        s_ts1 = ER(nc.sbuf_tensor("s_ts1", [128, 128], F32))
        s_ts2v = ER(nc.sbuf_tensor("s_ts2v", [128, 128], F32))
        s_kp = ER(nc.sbuf_tensor("s_kp", [128, 128], F32))
        s_w1t = ER(nc.sbuf_tensor("s_w1t", [128, 128], F32))
        s_w2t = ER(nc.sbuf_tensor("s_w2t", [128, 128], F32))
        s_tmp = ER(nc.sbuf_tensor("s_tmpP4", [128, 128], F32))
        s_iotac = ER(nc.sbuf_tensor("s_iotac", [128, CAP], F32))
        s_cc4 = ER(nc.sbuf_tensor("s_cc4", [128, 1], I32))
        s_cc4f = ER(nc.sbuf_tensor("s_cc4f", [128, 1], F32))
        s_pcol = ER(nc.sbuf_tensor("s_pcol", [128, 1], F32))
        s_jrow = ER(nc.sbuf_tensor("s_jrow", [128, 128], F32))
        s_lhsA = ER(nc.sbuf_tensor("s_lhsA", [128, 384], BF16))
        s_lhsB = ER(nc.sbuf_tensor("s_lhsB", [128, 384], BF16))
        s_ohj = [ER(nc.sbuf_tensor("s_ohj%d" % s, [128, CAP], BF16)) for s in range(2)]
        s_tab4 = ER(nc.sbuf_tensor("s_tab4", [4, CAP], F32))
        s_ssfh = ER(nc.sbuf_tensor("s_ssfh", [128, 4], F32))
        s_ssfl = ER(nc.sbuf_tensor("s_ssfl", [128, 4], F32))
        p_h1 = [EE(nc.psum_tensor("p_h1%d" % s, [128, CAP], F32)) for s in range(2)]
        p_h2 = [EE(nc.psum_tensor("p_h2%d" % s, [128, CAP], F32)) for s in range(2)]
        p_g16 = [EE(nc.psum_tensor("p_g16%d" % s, [128, 128], BF16)) for s in range(2)]
        p_r = EE(nc.psum_tensor("p_r", [128, 128], F32))
        p_tab4 = EE(nc.psum_tensor("p_tab4", [4, CAP], F32))

        id32l4 = R.dma("qS", "id4", lambda E: E.dma_start(out=s_id32_4[:], in_=ident), after=[c_t2])
        b1l = R.dma("qS", "b1", lambda E: E.dma_start(out=s_b1r[:], in_=b1r), after=[c_t2])
        b2l = R.dma("qS", "b2", lambda E: E.dma_start(out=s_b2r[:], in_=b2r))
        r4l = R.dma("qS", "r4", lambda E: E.dma_start(
            out=s_r4[:], in_=top2full_h.ap().rearrange("(p j) c -> p (j c)", p=128)),
            after=[c_t2])
        exE1 = R.op("dve", lambda E: E.tensor_copy(
            out=s_E1[:], in_=s_r4[:].rearrange("p (j c) -> p j c", c=4)[:, :, 0:1].squeeze(2)),
            after=[r4l])
        exE2 = R.op("dve", lambda E: E.tensor_copy(
            out=s_E2[:], in_=s_r4[:].rearrange("p (j c) -> p j c", c=4)[:, :, 1:2].squeeze(2)))
        exP1 = R.op("dve", lambda E: E.tensor_copy(
            out=s_P1[:], in_=s_r4[:].rearrange("p (j c) -> p j c", c=4)[:, :, 2:3].squeeze(2)))
        exP2 = R.op("dve", lambda E: E.tensor_copy(
            out=s_P2[:], in_=s_r4[:].rearrange("p (j c) -> p j c", c=4)[:, :, 3:4].squeeze(2)))

        ones_bc = s_ones[:].unsqueeze(2).broadcast_to([128, 1, 128]).squeeze(1)
        ones8_bc = s_ones[0:8, :].unsqueeze(2).broadcast_to([8, 1, 128]).squeeze(1)
        gtb_done = None
        agg_done = []
        g_tot = Guard(); g_TB = Guard(); g_TBc = Guard(); g_prefP = Guard(); g_pr = Guard()
        g_cumb = Guard()
        for ph in range(2):
            Ed = s_E1 if ph == 0 else s_E2
            Edev = exE1 if ph == 0 else exE2
            for e in range(E):
                oh = R.op("dve", lambda E_, e=e, Ed=Ed: E_.tensor_single_scalar(
                    out=s_oh[:, e * 128:(e + 1) * 128], in_=Ed[:], scalar=float(e), op=ALU.is_equal),
                    after=[Edev])
                cm = R.op("dve", lambda E_, e=e: E_.tensor_tensor_scan(
                    out=s_cum[:], data0=ones_bc, data1=s_oh[:, e * 128:(e + 1) * 128],
                    initial=0.0, op0=ALU.mult, op1=ALU.add), after=[oh, ones1] + g_cumb.w())
                ps = R.op("dve", lambda E_, e=e: E_.tensor_tensor(
                    out=s_pos[:, e * 128:(e + 1) * 128], in0=s_cum[:],
                    in1=s_oh[:, e * 128:(e + 1) * 128], op=ALU.subtract), after=[cm])
                tc = R.op("dve", lambda E_, e=e: E_.tensor_copy(
                    out=s_tot[:, e:e + 1], in_=s_cum[:, 127:128]),
                    after=[cm] + (g_tot.w() if e == 0 else []))
                g_cumb.r(tc)
            tot_done = ("dve", R.cnt["dve"])
            trt = R.op("pe", lambda E_: E_.transpose(out=p_r[0:8, :], in_=s_tot[:, 0:8],
                                                     identity=s_id32_4[:]),
                       after=[tot_done, id32l4] + g_pr.w())
            g_tot.r(trt)
            tbv = R.op("act", lambda E_: E_.activation(out=s_TB[:], in_=p_r[0:8, :], func=AF.Copy),
                       after=[trt] + g_TB.w())
            g_pr.r(tbv)
            cm8 = R.op("dve", lambda E_: E_.tensor_tensor_scan(
                out=s_TBc[:], data0=ones8_bc, data1=s_TB[:], initial=0.0,
                op0=ALU.mult, op1=ALU.add), after=[tbv, ones1] + g_TBc.w())
            if ph == 0:
                gtc = R.op("dve", lambda E_: E_.tensor_copy(out=s_gt8[:], in_=s_TBc[:, 127:128]),
                           after=[cm8])
                gtd = R.dma("qS", "gt", lambda E_: E_.dma_start(
                    out=gtscr_h.ap().rearrange("1 e -> e 1"), in_=s_gt8[:]), after=[gtc])
                gtb_done = R.dma("qG", "gtb", lambda E_: E_.dma_start(
                    out=s_gtb[:], in_=gtscr_h.ap().to_broadcast([128, NEXP])), after=[gtd])
            prf = R.op("dve", lambda E_: E_.tensor_tensor(out=s_TB[:], in0=s_TBc[:], in1=s_TB[:],
                                                          op=ALU.subtract), after=[cm8])
            trb = R.op("pe", lambda E_: E_.transpose(out=p_r[:, 0:8], in_=s_TB[:],
                                                     identity=s_id32_4[0:8, 0:8]),
                       after=[prf] + g_pr.w())
            g_TB.r(trb)
            prp = R.op("act", lambda E_: E_.activation(out=s_prefP[:], in_=p_r[:, 0:8], func=AF.Copy),
                       after=[trb] + g_prefP.w())
            g_pr.r(prp)
            g_TBc.r(prf)
            for e in range(E):
                if ph == 0:
                    pg = R.op("dve", lambda E_, e=e: E_.tensor_scalar(
                        out=s_pos[:, e * 128:(e + 1) * 128], in0=s_pos[:, e * 128:(e + 1) * 128],
                        scalar1=s_prefP[:, e:e + 1], scalar2=None, op0=ALU.add), after=[prp])
                else:
                    pg = R.op("dve", lambda E_, e=e: E_.tensor_scalar(
                        out=s_pos[:, e * 128:(e + 1) * 128], in0=s_pos[:, e * 128:(e + 1) * 128],
                        scalar1=s_prefP[:, e:e + 1], scalar2=s_gtb[:, e:e + 1],
                        op0=ALU.add, op1=ALU.add), after=[prp, gtb_done])
                g_prefP.r(pg)
                kp = R.op("dve", lambda E_, e=e: E_.tensor_single_scalar(
                    out=s_keep[:, e * 128:(e + 1) * 128], in_=s_pos[:, e * 128:(e + 1) * 128],
                    scalar=float(CAP), op=ALU.is_lt), after=[pg])
                t1_ = R.op("dve", lambda E_, e=e: E_.tensor_scalar(
                    out=s_pos[:, e * 128:(e + 1) * 128], in0=s_pos[:, e * 128:(e + 1) * 128],
                    scalar1=float(e * CAP) - OOBV, scalar2=None, op0=ALU.add), after=[kp])
                t2_ = R.op("dve", lambda E_, e=e: E_.tensor_tensor(
                    out=s_pos[:, e * 128:(e + 1) * 128], in0=s_pos[:, e * 128:(e + 1) * 128],
                    in1=s_keep[:, e * 128:(e + 1) * 128], op=ALU.mult), after=[t1_])
                R.op("dve", lambda E_, e=e: E_.tensor_scalar(
                    out=s_pos[:, e * 128:(e + 1) * 128], in0=s_pos[:, e * 128:(e + 1) * 128],
                    scalar1=OOBV, scalar2=None, op0=ALU.add), after=[t2_])
            slotm_done = ("dve", R.cnt["dve"])
            ts_dst = s_ts1 if ph == 0 else s_ts2v
            z1 = R.op("dve", lambda E_, ts_dst=ts_dst: E_.memset(ts_dst[:], 0.0), after=[slotm_done])
            z2 = R.op("dve", lambda E_: E_.memset(s_kp[:], 0.0))
            for e in range(E):
                tm = R.op("dve", lambda E_, e=e: E_.tensor_tensor(
                    out=s_tmp[:], in0=s_oh[:, e * 128:(e + 1) * 128],
                    in1=s_pos[:, e * 128:(e + 1) * 128], op=ALU.mult))
                R.op("dve", lambda E_, ts_dst=ts_dst: E_.tensor_tensor(
                    out=ts_dst[:], in0=ts_dst[:], in1=s_tmp[:], op=ALU.add))
                tm2 = R.op("dve", lambda E_, e=e: E_.tensor_tensor(
                    out=s_tmp[:], in0=s_oh[:, e * 128:(e + 1) * 128],
                    in1=s_keep[:, e * 128:(e + 1) * 128], op=ALU.mult))
                R.op("dve", lambda E_: E_.tensor_tensor(
                    out=s_kp[:], in0=s_kp[:], in1=s_tmp[:], op=ALU.add))
            wdst = s_w1t if ph == 0 else s_w2t
            Pv = s_P1 if ph == 0 else s_P2
            Pev = exP1 if ph == 0 else exP2
            wv = R.op("dve", lambda E_, wdst=wdst, Pv=Pv: E_.tensor_tensor(
                out=wdst[:], in0=Pv[:], in1=s_kp[:], op=ALU.mult), after=[Pev])
            agg_done.append(("dve", R.cnt["dve"]))

        io0 = R.op("gp", lambda E_: E_.iota(s_iotac[:], [[1, CAP]], 0, 0,
                                            allow_small_or_imprecise_dtypes=True))
        cc4l = R.dma("qG", "ccb", lambda E_: E_.dma_start(
            out=s_cc4[:], in_=dynoff[0:1, 13:14].to_broadcast([128, 1])))
        cc4c = R.op("dve", lambda E_: E_.tensor_copy(out=s_cc4f[:], in_=s_cc4[:]),
                    after=[cc4l])
        iov = R.op("dve", lambda E_: E_.tensor_scalar(
            out=s_iotac[:], in0=s_iotac[:], scalar1=s_cc4f[:, 0:1], scalar2=None,
            op0=ALU.add), after=[io0, cc4c])
        iop = R.op("gp", lambda E_: E_.iota(s_pcol[:], [[1, 1]], 0, 1,
                                            allow_small_or_imprecise_dtypes=True))
        ioj = R.op("gp", lambda E_: E_.iota(s_jrow[:], [[1, 128]], 0, 0,
                                            allow_small_or_imprecise_dtypes=True))
        for (dst, col, srcv, aftev) in (
                (s_lhsA, 0, None, [iop]), (s_lhsA, 1, s_jrow, [ioj]),
                (s_lhsA, 2, s_w1t, [agg_done[0]]),
                (s_lhsB, 0, None, [iop]), (s_lhsB, 1, s_jrow, [ioj]),
                (s_lhsB, 2, s_w2t, [agg_done[1]])):
            R.op("dve", lambda E_, dst=dst, col=col, srcv=srcv: E_.tensor_copy(
                out=dst[:].rearrange("p (j c) -> p j c", c=3)[:, :, col:col + 1].squeeze(2),
                in_=(s_pcol[:].broadcast_to([128, 128]) if srcv is None else srcv[:])),
                after=aftev)
        lhs_done = ("dve", R.cnt["dve"])
        g_ohj = [Guard(), Guard()]
        mmt = None
        for ph in range(2):
            slt = s_ts1 if ph == 0 else s_ts2v
            lhs = s_lhsA if ph == 0 else s_lhsB
            for j in range(128):
                buf = j % 2
                oh = R.op("dve", lambda E_, slt=slt, j=j, buf=buf: E_.tensor_scalar(
                    out=s_ohj[buf][:], in0=s_iotac[:], scalar1=slt[:, j:j + 1], scalar2=None,
                    op0=ALU.is_equal),
                    after=[iov, agg_done[ph]] + g_ohj[buf].w())
                mmt = R.op("pe", lambda E_, ph=ph, j=j, buf=buf, lhs=lhs: E_.matmul(
                    out=p_tab4[0:3, :], lhsT=lhs[:, 3 * j:3 * j + 3], rhs=s_ohj[buf][:],
                    start=(ph == 0 and j == 0), stop=(ph == 1 and j == 127)),
                    after=[oh] + ([lhs_done] if ph == 0 and j == 0 else []))
                g_ohj[buf].r(mmt)
        tabev = R.op("act", lambda E_: E_.activation(out=s_tab4[0:3, :], in_=p_tab4[0:3, :],
                                                     func=AF.Copy), after=[mmt])
        tabst = R.dma("qS", "tab", lambda E_: E_.dma_start(
            out=ptab_h.ap(), in_=s_tab4[0:3, :]), after=[tabev])
        c_tab = R.cc(lambda E_: E_.collective_compute(
            "AllGather", ALU.bypass, replica_groups=[list(range(8))],
            ins=[ptab_h.ap().opt()], outs=[rtab_h.ap().opt()]),
            after=[tabst])

        def do_regs_ss(E_):
            E_.reg_load(r_t, s_dyn[0:1, 12:13])
        R.raw("gp", do_regs_ss, after=[lN])
        ssgh = R.dma("qG", "ssg", lambda E_: E_.dma_start(
            out=s_ssfh[:].unsqueeze(2),
            in_=bass.AP(rtab_h, r_t, [[1, 128], [128, 4], [1, 1]]),
            allow_slow_non_contiguous=True),
            after=[c_tab])

        def do_regs_ss2(E_):
            E_.reg_load(r_t, s_dyn[0:1, 14:15])
        R.raw("gp", do_regs_ss2)
        ssgl = R.dma("qG", "ssg2", lambda E_: E_.dma_start(
            out=s_ssfl[:].unsqueeze(2),
            in_=bass.AP(rtab_h, r_t, [[1, 128], [128, 4], [1, 1]]),
            allow_slow_non_contiguous=True))
        scmb = R.op("dve", lambda E_: E_.scalar_tensor_tensor(
            out=s_ssf[:], in0=s_ssfh[:], scalar=128.0, in1=s_ssfl[:],
            op0=ALU.mult, op1=ALU.add), after=[ssgh, ssgl])
        cssi = R.op("dve", lambda E_: E_.tensor_copy(out=s_ssi[:], in_=s_ssf[:]),
                    after=[scmb])
        P4r.close()
        s_xg = [EE(nc.sbuf_tensor("s_xg%d" % m, [128, D], BF16)) for m in range(4)]
        s_xgT = [EE(nc.sbuf_tensor("s_xgT%d" % kk, [128, CAP], BF16)) for kk in range(16)]
        s_w1f = [EE(nc.sbuf_tensor("s_w1f%d" % s, [128, D], BF16)) for s in range(2)]
        s_h116 = EE(nc.sbuf_tensor("s_h116", [128, 64 * CAP], BF16))
        s_w2d = [EE(nc.sbuf_tensor("s_w2d%d" % s, [128, 32 * 128], BF16)) for s in range(2)]
        s_hf = EE(nc.sbuf_tensor("s_hf", [128, CAP], F32))
        s_hf16 = EE(nc.sbuf_tensor("s_hf16", [128, CAP], BF16))
        s_hT = [EE(nc.sbuf_tensor("s_hT%d" % qq, [128, D], BF16)) for qq in range(4)]
        g_pg = [Guard() for _ in range(2)]
        xgld = []
        for m in range(4):
            xgld.append(R.dma("qG", "xg%d" % m, lambda E_, m=m: E_.indirect_dma_start(
                out=s_xg[m][:], out_offset=None,
                in_=xffull_h.ap(), in_offset=IndirectOffsetOnAxis(s_ssi[:, m:m + 1], 0),
                bounds_check=NTOK - 1, oob_is_err=False),
                after=[cssi, c_xf]))
        for m in range(4):
            for kk in range(16):
                slot = (m * 16 + kk) % 2
                tr = R.op("pe", lambda E_, m=m, kk=kk, slot=slot: E_.transpose(
                    out=p_g16[slot][:], in_=s_xg[m][:, kk * 128:(kk + 1) * 128],
                    identity=s_id16[:]),
                    after=[xgld[m], idc] + g_pg[slot].w())
                tev = R.op("act", lambda E_, m=m, kk=kk, slot=slot: E_.activation(
                    out=s_xgT[kk][:, m * 128:(m + 1) * 128], in_=p_g16[slot][:], func=AF.Copy),
                    after=[tr])
                g_pg[slot].r(tev)
        xgT_done = ("act", R.cnt["act"])

        g_w1f = [Guard() for _ in range(2)]
        g_ph1 = [Guard() for _ in range(2)]
        for f in range(64):
            buf = f % 2
            wld = R.dma("qG", "w1f%d" % buf, lambda E_, f=f, buf=buf: E_.dma_start(
                out=s_w1f[buf][:], in_=bass.AP(w1T_h, f * 128, [[F, 128], [128 * F, 16], [1, 128]])),
                after=g_w1f[buf].w())
            mm = None
            for kk in range(16):
                aft = []
                if kk == 0:
                    aft = [xgT_done, wld] + g_ph1[buf].w()
                mm = R.op("pe", lambda E_, kk=kk, buf=buf: E_.matmul(
                    out=p_h1[buf][:], lhsT=s_w1f[buf][:, kk * 128:(kk + 1) * 128], rhs=s_xgT[kk][:],
                    start=(kk == 0), stop=(kk == 15)), after=aft)
            g_w1f[buf].r(mm)
            gel = R.op("act", lambda E_, f=f, buf=buf: E_.activation(
                out=s_h116[:, f * CAP:(f + 1) * CAP], in_=p_h1[buf][:], func=AF.Gelu,
                bias=s_b1r[:, f:f + 1]), after=[mm, b1l])
            g_ph1[buf].r(gel)
        h1_done = ("act", R.cnt["act"])


        g_w2d = [Guard() for _ in range(2)]
        g_ph2 = [Guard() for _ in range(2)]
        g_hf = Guard(); g_hf16 = Guard()
        g_hT = [Guard() for _ in range(4)]
        hstores = []
        for jd in range(16):
            buf = jd % 2
            mm = None
            for half in range(2):
                wbuf = (jd * 2 + half) % 2
                wld = R.dma("qG", "w2d%d" % wbuf, lambda E_, jd=jd, half=half, wbuf=wbuf: E_.dma_start(
                    out=s_w2d[wbuf][:],
                    in_=bass.AP(w2T_h, jd * 128 + half * 32 * 128 * D,
                                [[D, 128], [128 * D, 32], [1, 128]])),
                    after=g_w2d[wbuf].w())
                for fi in range(32):
                    ff = half * 32 + fi
                    aft = [wld]
                    if ff == 0:
                        aft = [h1_done, wld] + g_ph2[buf].w()
                    mm = R.op("pe", lambda E_, fi=fi, ff=ff, buf=buf, wbuf=wbuf: E_.matmul(
                        out=p_h2[buf][:], lhsT=s_w2d[wbuf][:, fi * 128:(fi + 1) * 128],
                        rhs=s_h116[:, ff * CAP:(ff + 1) * CAP],
                        start=(ff == 0), stop=(ff == 63)),
                        after=(aft if fi == 0 else []))
                g_w2d[wbuf].r(mm)
            hev = R.op("act", lambda E_, buf=buf: E_.activation(
                out=s_hf[:], in_=p_h2[buf][:], func=AF.Copy),
                after=[mm] + g_hf.w())
            g_ph2[buf].r(hev)
            hev = R.op("dve", lambda E_, jd=jd: E_.tensor_scalar(
                out=s_hf[:], in0=s_hf[:], scalar1=s_b2r[:, jd:jd + 1], scalar2=None,
                op0=ALU.add), after=[hev, b2l])
            hc16 = R.op("dve", lambda E_: E_.tensor_copy(out=s_hf16[:], in_=s_hf[:]),
                        after=[hev] + g_hf16.w())
            g_hf.r(hc16)
            for qq in range(4):
                slot = qq % 2
                tr = R.op("pe", lambda E_, qq=qq, slot=slot: E_.transpose(
                    out=p_g16[slot][:], in_=s_hf16[:, qq * 128:(qq + 1) * 128], identity=s_id16[:]),
                    after=[hc16] + g_pg[slot].w())
                g_hf16.r(tr)
                tev = R.op("act", lambda E_, qq=qq, jd=jd, slot=slot: E_.activation(
                    out=s_hT[qq][:, jd * 128:(jd + 1) * 128], in_=p_g16[slot][:], func=AF.Copy),
                    after=[tr] + (g_hT[qq].w() if jd == 0 else []))
                g_pg[slot].r(tev)
        hT_done = ("act", R.cnt["act"])
        for qq in range(4):
            hstores.append(R.dma("qS", "hst", lambda E_, qq=qq: E_.dma_start(
                out=hfino_h.ap()[qq * 128:(qq + 1) * 128, :], in_=s_hT[qq][:]), after=[hT_done]))
        p4_tail = R.all_cur()

    c_h = R.cc(lambda E_: E_.collective_compute(
        "AllGather", ALU.bypass, replica_groups=[list(range(8))],
        ins=[hfino_h.ap().opt()], outs=[hfull_h.ap().opt()]), after=p4_tail)

    NSB = NEXP * CAP // 128
    with ExitStack() as P:
        EE = P.enter_context
        s_st2 = EE(nc.sbuf_tensor("s_st2", [128, 2 * NSB], F32))
        s_st4 = EE(nc.sbuf_tensor("s_st4", [128, 3 * NSB], F32))
        s_cb5 = EE(nc.sbuf_tensor("s_cb5", [128, 1], I32))
        s_cbf = EE(nc.sbuf_tensor("s_cbf", [128, 1], F32))
        s_gidb = EE(nc.sbuf_tensor("s_gidb", [128, TOKPC], F32))
        s_MT = EE(nc.sbuf_tensor("s_MT", [128, NSB * 512], BF16))
        s_hfb = [EE(nc.sbuf_tensor("s_hfb%d" % s, [128, 1024], BF16)) for s in range(2)]
        s_ffn4 = EE(nc.sbuf_tensor("s_ffn4", [128, 4 * D], F32))
        s_xr5 = EE(nc.sbuf_tensor("s_xr5", [128, D], F32))
        s_sq5 = EE(nc.sbuf_tensor("s_sq5", [128, D], F32))
        s_oout = EE(nc.sbuf_tensor("s_oout", [128, D], F32))
        s_mu5 = EE(nc.sbuf_tensor("s_mu5", [128, 1], F32))
        s_mu25 = EE(nc.sbuf_tensor("s_mu25", [128, 1], F32))
        s_m25 = EE(nc.sbuf_tensor("s_m25", [128, 1], F32))
        s_sd5 = EE(nc.sbuf_tensor("s_sd5", [128, 1], F32))
        s_rstd5 = EE(nc.sbuf_tensor("s_rstd5", [128, 1], F32))
        p_acc = [EE(nc.psum_tensor("p_acc%d" % t, [128, 1024], F32)) for t in range(4)]

        stl = R.dma("qS", "stl", lambda E_: E_.dma_start(
            out=s_st4[:].rearrange("p (c r q) -> p c r q", r=3, q=4),
            in_=bass.AP(rtab_h, 0, [[1, 128], [1536, 8], [512, 3], [128, 4]]),
            allow_slow_non_contiguous=True), after=[c_tab])
        _v4 = s_st4[:].rearrange("p (c r q) -> p c r q", r=3, q=4)
        _hi = _v4[:, :, 0:1, :].squeeze(2)
        _lo = _v4[:, :, 1:2, :].squeeze(2)
        _wv = _v4[:, :, 2:3, :].squeeze(2)
        _s2 = s_st2[:].rearrange("p (c q c2) -> p c q c2", q=4, c2=2)
        _ido = _s2[:, :, :, 0:1].squeeze(3)
        _wo = _s2[:, :, :, 1:2].squeeze(3)
        cmb1 = R.op("dve", lambda E_: E_.scalar_tensor_tensor(
            out=_ido, in0=_hi, scalar=128.0, in1=_lo, op0=ALU.mult, op1=ALU.add),
            after=[stl])
        cmb2 = R.op("dve", lambda E_: E_.tensor_copy(out=_wo, in_=_wv))
        st2_done = cmb2
        cbl = R.dma("qG", "cbl", lambda E_: E_.dma_start(
            out=s_cb5[:], in_=dynoff[0:1, 8:9].to_broadcast([128, 1])))
        cbc = R.op("dve", lambda E_: E_.tensor_copy(out=s_cbf[:], in_=s_cb5[:]),
                   after=[cbl])
        gio = R.op("gp", lambda E_: E_.iota(s_gidb[:], [[1, TOKPC]], 0, 0,
                                            allow_small_or_imprecise_dtypes=True))
        gid = R.op("dve", lambda E_: E_.tensor_scalar(
            out=s_gidb[:], in0=s_gidb[:], scalar1=s_cbf[:, 0:1], scalar2=None,
            op0=ALU.add), after=[cbc, gio])

        g_MT = [Guard() for _ in range(NSB)]
        g_hfb = [Guard() for _ in range(2)]
        g_pacc = [Guard() for _ in range(4)]
        g_ffn = [Guard() for _ in range(4)]
        g_xr5 = Guard(); g_sq5 = Guard(); g_oout = Guard()
        ihf = 0
        laststore = None
        for quad in range(4):
            mtev = []
            for sb in range(NSB):
                ev = R.op("dve", lambda E_, sb=sb, quad=quad: E_.tensor_scalar(
                    out=s_MT[:, sb * 512:(sb + 1) * 512],
                    in0=s_gidb[:, quad * 512:(quad + 1) * 512],
                    scalar1=s_st2[:, 2 * sb:2 * sb + 1],
                    scalar2=s_st2[:, 2 * sb + 1:2 * sb + 2],
                    op0=ALU.is_equal, op1=ALU.mult),
                    after=[gid, st2_done] + g_MT[sb].w())
                mtev.append(ev)
            evq = [[None] * 4, [None] * 4]
            for dmH in range(2):
                lastmm = [None] * 4
                for sb in range(NSB):
                    buf = ihf % 2
                    ihf += 1
                    ld = R.dma("qS", "hf%d" % buf, lambda E_, sb=sb, dmH=dmH, buf=buf: E_.dma_start(
                        out=s_hfb[buf][:],
                        in_=hfull_h.ap()[sb * 128:(sb + 1) * 128,
                                         dmH * 1024:(dmH + 1) * 1024]),
                        after=[c_h] + g_hfb[buf].w())
                    mm = None
                    for tt in range(4):
                        for nh in range(2):
                            aft = []
                            if tt == 0 and nh == 0:
                                aft = [mtev[sb], ld]
                                if sb == 0:
                                    for t2 in range(4):
                                        aft += g_pacc[t2].w()
                            mm = R.op("pe", lambda E_, sb=sb, tt=tt, nh=nh, buf=buf: E_.matmul(
                                out=p_acc[tt][:, nh * 512:(nh + 1) * 512],
                                lhsT=s_MT[:, sb * 512 + tt * 128:sb * 512 + (tt + 1) * 128],
                                rhs=s_hfb[buf][:, nh * 512:(nh + 1) * 512],
                                start=(sb == 0), stop=(sb == NSB - 1)), after=aft)
                        lastmm[tt] = mm
                    g_hfb[buf].r(mm)
                    if dmH == 1:
                        g_MT[sb].r(mm)
                for tt in range(4):
                    ev = R.op("act", lambda E_, tt=tt, dmH=dmH: E_.activation(
                        out=s_ffn4[:, tt * D + dmH * 1024:tt * D + (dmH + 1) * 1024],
                        in_=p_acc[tt][:], func=AF.Copy),
                        after=[lastmm[tt]] + (g_ffn[tt].w() if dmH == 0 else []))
                    g_pacc[tt].r(ev)
                    evq[dmH][tt] = ev
            for tt in range(4):
                m_ = quad * 4 + tt
                sl = s_ffn4[:, tt * D:(tt + 1) * D]
                xl = R.dma("qS", "xr5", lambda E_, m_=m_: E_.dma_start(
                    out=s_xr5[:], in_=xf32o_h.ap()[m_ * 128:(m_ + 1) * 128, :]),
                    after=g_xr5.w())
                sv = R.op("dve", lambda E_, sl=sl: E_.tensor_tensor(
                    out=sl, in0=sl, in1=s_xr5[:], op=ALU.add),
                    after=[xl, evq[1][tt]])
                g_xr5.r(sv)
                mu = R.op("dve", lambda E_, sl=sl: E_.tensor_reduce(
                    out=s_mu5[:], in_=sl, axis=AX.X, op=ALU.add), after=[sv])
                mub = R.op("dve", lambda E_: E_.tensor_scalar(
                    out=s_mu5[:], in0=s_mu5[:], scalar1=1.0 / D, scalar2=None, op0=ALU.mult))
                sq = R.op("act", lambda E_, sl=sl: E_.activation(
                    out=s_sq5[:], in_=sl, func=AF.Square), after=[sv] + g_sq5.w())
                m2a = R.op("dve", lambda E_: E_.tensor_reduce(
                    out=s_m25[:], in_=s_sq5[:], axis=AX.X, op=ALU.add), after=[sq])
                g_sq5.r(m2a)
                m2b = R.op("dve", lambda E_: E_.tensor_scalar(
                    out=s_m25[:], in0=s_m25[:], scalar1=1.0 / D, scalar2=None, op0=ALU.mult))
                mu2v = R.op("dve", lambda E_: E_.tensor_scalar(
                    out=s_mu25[:], in0=s_mu5[:], scalar1=s_mu5[:, 0:1], scalar2=None,
                    op0=ALU.mult), after=[mub])
                vv = R.op("dve", lambda E_: E_.tensor_tensor(
                    out=s_m25[:], in0=s_m25[:], in1=s_mu25[:], op=ALU.subtract),
                    after=[m2b, mu2v])
                sdv = R.op("act", lambda E_: E_.activation(
                    out=s_sd5[:], in_=s_m25[:], func=AF.Sqrt, bias=s_eps[:, 0:1]), after=[vv])
                rst = R.op("dve", lambda E_: E_.reciprocal(out=s_rstd5[:], in_=s_sd5[:]),
                           after=[sdv])
                nmu = R.op("dve", lambda E_: E_.tensor_scalar(
                    out=s_mu5[:], in0=s_mu5[:], scalar1=-1.0, scalar2=None, op0=ALU.mult),
                    after=[mu2v])
                ov = R.op("dve", lambda E_, sl=sl: E_.tensor_scalar(
                    out=s_oout[:], in0=sl, scalar1=s_mu5[:, 0:1], scalar2=s_rstd5[:, 0:1],
                    op0=ALU.add, op1=ALU.mult), after=[nmu, rst] + g_oout.w())
                g_ffn[tt].r(ov)
                ost = R.dma("qS", "out", lambda E_, m_=m_: E_.dma_start(
                    out=outown[m_ * 128:(m_ + 1) * 128, :], in_=s_oout[:]), after=[ov])
                g_oout.r(ost)
                laststore = ost

    for _fam in list(R.famstream):
        _eng = "qS" if R.famstream[_fam] == "sync" else "gp"
        R.raw(_eng, lambda E_, s=_fam, v=R.cnt[_fam]: E_.wait_ge(R.semh[s], v))

    with nc.Block() as block:
        block.sync(lambda e: [f(e) for f in R.streams["sync"]])
        block.scalar(lambda e: [f(e) for f in R.streams["scalar"]])
        block.vector(lambda e: [f(e) for f in R.streams["vector"]])
        block.tensor(lambda e: [f(e) for f in R.streams["tensor"]])
        block.gpsimd(lambda e: [f(e) for f in R.streams["gpsimd"]])
    ctx.close()
    return nc


def _dynoff(c, h):
    d = np.zeros((1, 16), np.int32)
    for k in range(NCH):
        d[0, k] = k * T + h * 256
    for r_i in range(4):
        d[0, 8 + r_i] = r_i * NTOK + c * TOKPC
    d[0, 12] = c * CAP * 3
    d[0, 13] = c * CAP
    d[0, 14] = c * CAP * 3 + CAP
    return d


def make_inmaps(inputs):
    x = np.asarray(inputs["L0_attn_x"], np.float32)
    ipw = np.asarray(inputs["L0_attn_in_proj_w"], np.float32)
    cwf = np.asarray(inputs["L0_attn_conv_w"], np.float32)
    cbf = np.asarray(inputs["L0_attn_conv_b"], np.float32)
    xpw = np.asarray(inputs["L0_attn_x_proj_w"], np.float32)
    dtw = np.asarray(inputs["L0_attn_dt_proj_w"], np.float32)
    dtbf = np.asarray(inputs["L0_attn_dt_proj_b"], np.float32)
    alogf = np.asarray(inputs["L0_attn_A_log"], np.float32)
    dv = np.asarray(inputs["L0_attn_D"], np.float32)
    opw = np.asarray(inputs["L0_attn_out_proj_w"], np.float32)
    rw = np.asarray(inputs["L0_ffn_router_w"], np.float32)
    w1 = np.asarray(inputs["L0_ffn_w1"], np.float32)
    b1 = np.asarray(inputs["L0_ffn_b1"], np.float32)
    w2 = np.asarray(inputs["L0_ffn_w2"], np.float32)
    b2 = np.asarray(inputs["L0_ffn_b2"], np.float32)

    C = np.ascontiguousarray
    in_maps = []
    own_rows_all = []
    for c in range(8):
        b, h = c // 2, c % 2
        hs = slice(h * HC, (h + 1) * HC)
        rows = np.concatenate([np.arange(k * 512 + h * 256, k * 512 + h * 256 + 256)
                               for k in range(NCH)])
        own_rows_all.append(b * L + rows)
        pack = lambda a: C(a.reshape(NB, 128, -1).transpose(1, 0, 2).reshape(128, -1))
        m = {
            "xb": C(x[b]),
            "xown": C(x[b][rows]),
            "wuT": C(ipw[hs, :].T),
            "wzT": C(ipw[INNER + h * HC: INNER + (h + 1) * HC, :].T),
            "cw": pack(cwf[hs, 0, :]),
            "cb": pack(cbf[hs].reshape(-1, 1)),
            "xpT": C(xpw[:, hs].T),
            "dtT": C(dtw[hs, :].T),
            "dtb": pack(dtbf[hs].reshape(-1, 1)),
            "alog": pack(alogf[hs, :]),
            "dvec": pack(dv[hs].reshape(-1, 1)),
            "opT": C(opw.T),
            "rwT": C(rw.T),
            "w1T": C(w1.T),
            "b1r": C(b1.reshape(F // 128, 128).T),
            "w2T": C(w2.T),
            "b2r": C(b2.reshape(D // 128, 128).T),
            "ident": np.eye(128, dtype=np.float32),
            "sids": C((c * CAP + np.arange(CAP, dtype=np.int32)).reshape(4, 128).T),
            "hoff": np.array([[h * 256]], np.int32),
            "dynoff": _dynoff(c, h),
            "coff": np.array([[c * TOKPC]], np.int32),
            "epst": np.full((128, 1), 1e-5, np.float32),
        }
        in_maps.append(m)
    return in_maps, own_rows_all


def kernel(**inputs):
    in_maps, own_rows_all = make_inmaps(inputs)
    nc = build_program()
    res = run_bass_kernel_spmd(nc, in_maps, core_ids=list(range(8)))
    out = np.empty((B, L, D), np.float32)
    flat = out.reshape(NTOK, D)
    for c in range(8):
        flat[own_rows_all[c]] = res.results[c]["outown"]
    return out


# version 10
# speedup vs baseline: 1.2154x; 1.2154x over previous
import numpy as np
import ml_dtypes
import concourse.bass as bass
import concourse.mybir as mybir
from concourse.bass_utils import run_bass_kernel_spmd
from concourse.bass_core_rust import IndirectOffsetOnAxis

F32 = mybir.dt.float32
BF16 = mybir.dt.bfloat16
I32 = mybir.dt.int32
U32 = mybir.dt.uint32
AF = mybir.ActivationFunctionType
ALU = mybir.AluOpType
AX = mybir.AxisListType

B, L, D = 4, 4096, 2048
INNER, HC, NB = 4096, 2048, 16
NST, DTR = 16, 128
T = 512
NCH = L // T
E, CAP, F = 8, 512, 8192
NEXP = 8
NTOK = B * L
TOKPC = NTOK // 8
TBL = E * CAP + 128
OOBV = 100000.0

ENG = {"pe": "tensor", "act": "scalar", "dve": "vector", "gp": "gpsimd", "qS": "sync", "qG": "gpsimd"}


class Rec:
    def __init__(self, nc):
        self.nc = nc
        self.streams = {k: [] for k in ["sync", "scalar", "vector", "tensor", "gpsimd"]}
        self.cnt = {"pe": 0, "act": 0, "dve": 0, "gp": 0, "qS": 0, "qG": 0}
        self.semh = {}
        self.ccn = 0
        self._ctx = None
        self.famstream = {}
        self.hwm = {k: {} for k in self.streams}

    def _filt(self, stream, aft):
        best = {}
        for (s, v) in aft:
            if v > best.get(s, 0):
                best[s] = v
        h = self.hwm[stream]
        out = []
        for s, v in best.items():
            if v > h.get(s, 0):
                h[s] = v
                out.append((s, v))
        return out

    def init_sems(self, ctx):
        self._ctx = ctx
        for name in ["pe", "act", "dve", "gp", "qS", "qG"]:
            self.semh[name] = ctx.enter_context(self.nc.semaphore("sem_" + name))

    def op(self, eng, fn, after=()):
        self.cnt[eng] += 1
        tgt = self.cnt[eng]
        aft = [x for x in after if x is not None]
        if eng in ("act", "dve", "gp"):
            aft.append((eng, tgt - 1))
        aft = self._filt(ENG[eng], aft)
        sem = self.semh

        def emit(E, aft=tuple(aft), fn=fn, eng=eng):
            for (s, v) in aft:
                if v > 0:
                    E.wait_ge(sem[s], v)
            fn(E).then_inc(sem[eng], 1)
        self.streams[ENG[eng]].append(emit)
        return (eng, tgt)

    def raw(self, eng, fn, after=()):
        aft = self._filt(ENG[eng], [x for x in after if x is not None])
        sem = self.semh

        def emit(E, aft=tuple(aft), fn=fn):
            for (s, v) in aft:
                if v > 0:
                    E.wait_ge(sem[s], v)
            fn(E)
        self.streams[ENG[eng]].append(emit)

    def dma(self, q, fam, fn, after=()):
        if fam not in self.cnt:
            self.cnt[fam] = 0
            self.semh[fam] = self._ctx.enter_context(self.nc.semaphore("sem_" + fam))
            self.famstream[fam] = ENG[q]
        assert self.famstream[fam] == ENG[q], fam
        self.cnt[fam] += 16
        tgt = self.cnt[fam]
        aft = self._filt(ENG[q], [x for x in after if x is not None])
        sem = self.semh

        def emit(E, aft=tuple(aft), fn=fn, fam=fam):
            for (s, v) in aft:
                if v > 0:
                    E.wait_ge(sem[s], v)
            fn(E).then_inc(sem[fam], 16)
        self.streams[ENG[q]].append(emit)
        return (fam, tgt)

    def cc(self, fn, after=()):
        name = "cc%d" % self.ccn
        self.ccn += 1
        self.semh[name] = self._ctx.enter_context(self.nc.semaphore("sem_" + name))
        aft = self._filt("gpsimd", [x for x in after if x is not None])
        sem = self.semh

        def emit(E, aft=tuple(aft), fn=fn, name=name):
            for (s, v) in aft:
                if v > 0:
                    E.wait_ge(sem[s], v)
            fn(E).then_inc(sem[name])
        self.streams["gpsimd"].append(emit)
        return (name, 1)

    def all_cur(self):
        return [(e, v) for (e, v) in self.cnt.items() if v > 0]


class Guard:
    def __init__(self):
        self.evs = []

    def r(self, ev):
        if ev is not None:
            self.evs.append(ev)
        return ev

    def w(self):
        evs, self.evs = self.evs, []
        return evs


def build_program():
    nc = bass.Bass("TRN2", target_bir_lowering=False, dynamic_dma_scratch_size=65536)
    dt = nc.dram_tensor

    xb_h = dt("xb", [L, D], F32, kind="ExternalInput")
    wuT_h = dt("wuT", [D, HC], F32, kind="ExternalInput")
    wzT_h = dt("wzT", [D, HC], F32, kind="ExternalInput")
    cw = dt("cw", [128, NB * 4], F32, kind="ExternalInput").ap()
    cb = dt("cb", [128, NB], F32, kind="ExternalInput").ap()
    xpT_h = dt("xpT", [HC, 160], F32, kind="ExternalInput")
    dtT_h = dt("dtT", [DTR, HC], F32, kind="ExternalInput")
    dtb = dt("dtb", [128, NB], F32, kind="ExternalInput").ap()
    alog = dt("alog", [128, NB * NST], F32, kind="ExternalInput").ap()
    dvec = dt("dvec", [128, NB], F32, kind="ExternalInput").ap()
    opT_h = dt("opT", [INNER, D], F32, kind="ExternalInput")
    rwT_h = dt("rwT", [D, E], F32, kind="ExternalInput")
    w1T_h = dt("w1T", [D, F], F32, kind="ExternalInput")
    b1r = dt("b1r", [128, F // 128], F32, kind="ExternalInput").ap()
    w2T_h = dt("w2T", [F, D], F32, kind="ExternalInput")
    b2r = dt("b2r", [128, D // 128], F32, kind="ExternalInput").ap()
    ident = dt("ident", [128, 128], F32, kind="ExternalInput").ap()
    xown_h = dt("xown", [TOKPC, D], F32, kind="ExternalInput")
    sids = dt("sids", [128, 4], I32, kind="ExternalInput").ap()
    hoff = dt("hoff", [1, 1], I32, kind="ExternalInput").ap()
    coff = dt("coff", [1, 1], I32, kind="ExternalInput").ap()
    epst = dt("epst", [128, 1], F32, kind="ExternalInput").ap()
    dynoff = dt("dynoff", [1, 16], I32, kind="ExternalInput").ap()

    outown = dt("outown", [TOKPC, D], F32, kind="ExternalOutput").ap()

    u16d_h = dt("u16d", [HC, L], BF16)
    z16d_h = dt("z16d", [HC, L], BF16)
    xdblp_h = dt("xdblp", [160, L], F32)
    xdblr_h = dt("xdblr", [160, L], F32)
    bflat_h = dt("bflat", [NST, T], F32)
    cflat_h = dt("cflat", [NST, T], F32)
    ys16d_h = dt("ys16d", [HC, L], BF16)
    ysf_h = dt("ysf", [INNER, L], BF16)
    xf32o_h = dt("xf32o", [TOKPC, D], F32)
    xf16o_h = dt("xf16o", [TOKPC, D], BF16)
    xffull_h = dt("xffull", [NTOK, D], BF16)
    top2o_h = dt("top2o", [TOKPC, 4], F32)
    top2full_h = dt("top2full", [NTOK, 4], F32)
    gtscr_h = dt("gtscr", [1, E], F32)
    ptab_h = dt("ptab", [3, CAP], F32)
    rtab_h = dt("rtab", [E * 3, CAP], F32)
    hfino_h = dt("hfino", [CAP, D], BF16)
    hfull_h = dt("hfull", [E * CAP, D], BF16)

    from contextlib import ExitStack
    ctx = ExitStack()
    R = Rec(nc)
    R.init_sems(ctx)

    s_id16 = ctx.enter_context(nc.sbuf_tensor("s_id16", [128, 128], BF16))
    s_cw = ctx.enter_context(nc.sbuf_tensor("s_cw", [128, NB * 4], F32))
    s_cb = ctx.enter_context(nc.sbuf_tensor("s_cb", [128, NB], F32))
    s_dtb = ctx.enter_context(nc.sbuf_tensor("s_dtb", [128, NB], F32))
    s_negA = ctx.enter_context(nc.sbuf_tensor("s_negA", [128, NB * NST], F32))
    s_dvec = ctx.enter_context(nc.sbuf_tensor("s_dvec", [128, NB], F32))
    s_eps = ctx.enter_context(nc.sbuf_tensor("s_eps", [128, 1], F32))
    s_dyn = ctx.enter_context(nc.sbuf_tensor("s_dyn", [1, 16], I32))
    s_ones = ctx.enter_context(nc.sbuf_tensor("s_ones", [128, 1], F32))

    R.dma("qS", "pre", lambda E: E.dma_start(out=s_cw[:], in_=cw))
    R.dma("qS", "pre", lambda E: E.dma_start(out=s_cb[:], in_=cb))
    R.dma("qS", "pre", lambda E: E.dma_start(out=s_dtb[:], in_=dtb))
    R.dma("qS", "pre", lambda E: E.dma_start(out=s_negA[:], in_=alog))
    R.dma("qS", "pre", lambda E: E.dma_start(out=s_dvec[:], in_=dvec))
    R.dma("qS", "pre", lambda E: E.dma_start(out=s_eps[:], in_=epst))
    lN = R.dma("qS", "pre", lambda E: E.dma_start(out=s_dyn[:], in_=dynoff))
    idld = R.dma("qG", "idg", lambda E: E.dma_start(out=s_id16[:], in_=ident))
    idc = idld
    e1 = R.op("act", lambda E: E.activation(out=s_negA[:], in_=s_negA[:], func=AF.Exp),
               after=[lN])
    negA_done = R.op("dve", lambda E: E.tensor_scalar(out=s_negA[:], in0=s_negA[:], scalar1=-1.0,
                                                      scalar2=None, op0=ALU.mult), after=[e1])
    ones1 = R.op("dve", lambda E: E.memset(s_ones[:], 1.0))
    r_t = nc.gpsimd.alloc_register("r_t")

    with ExitStack() as P:
        EE = P.enter_context
        s_wuT = [EE(nc.sbuf_tensor("s_wuT%d" % k, [128, HC], BF16)) for k in range(16)]
        s_wzT = [EE(nc.sbuf_tensor("s_wzT%d" % k, [128, HC], BF16)) for k in range(16)]
        s_xpT = [EE(nc.sbuf_tensor("s_xpT%d" % i, [128, 160], BF16)) for i in range(NB)]
        s_xc = [EE(nc.sbuf_tensor("s_xc%d" % q, [128, D], BF16)) for q in range(1)]
        s_xT = [EE(nc.sbuf_tensor("s_xTT%d" % k, [128, T], BF16)) for k in range(16)]
        s_upre = EE(nc.sbuf_tensor("s_upre", [128, T + 3], BF16))
        s_halo = EE(nc.sbuf_tensor("s_halo", [128, 3 * NB], BF16))
        s_cvt = EE(nc.sbuf_tensor("s_cvt", [128, T], F32))
        s_u16t = EE(nc.sbuf_tensor("s_u16t", [128, T], BF16))
        s_z16t = s_u16t
        p_tr = [EE(nc.psum_tensor("p_tr%d" % s, [128, 128], BF16)) for s in range(2)]
        p_mm = [EE(nc.psum_tensor("p_mm%d" % s, [128, T], F32)) for s in range(2)]
        p_dt = EE(nc.psum_tensor("p_dt", [128, T], F32))
        p_bc = EE(nc.psum_tensor("p_bc", [32, T], F32))

        for k in range(16):
            R.dma("qG", "wp1", lambda E, k=k: E.dma_start(out=s_wuT[k][:], in_=wuT_h.ap()[k * 128:(k + 1) * 128, :]))
            R.dma("qG", "wp1", lambda E, k=k: E.dma_start(out=s_wzT[k][:], in_=wzT_h.ap()[k * 128:(k + 1) * 128, :]))
        for i in range(NB):
            wl = R.dma("qG", "wp1", lambda E, i=i: E.dma_start(out=s_xpT[i][:], in_=xpT_h.ap()[i * 128:(i + 1) * 128, :]))
        w_done = wl
        halo0 = R.op("dve", lambda E: E.memset(s_halo[:], 0.0))

        g_xc = [Guard() for _ in range(1)]
        g_xT = [Guard() for _ in range(16)]
        g_tr = [Guard() for _ in range(2)]
        g_mm = [Guard() for _ in range(2)]
        g_u16t = Guard(); g_z16t = g_u16t
        g_upre = Guard(); g_halo = Guard()
        g_dtp = Guard(); g_cvt = Guard()

        for k in range(NCH):
            wEs = [g_xT[i].w() for i in range(16)]
            for q in range(4):
                xld = R.dma("qG", "xld", lambda E, q=q, k=k: E.dma_start(
                    out=s_xc[0][:], in_=xb_h.ap()[k * T + q * 128: k * T + (q + 1) * 128, :]),
                    after=g_xc[0].w())
                for i in range(16):
                    slot = i % 2
                    tr = R.op("pe", lambda E, i=i, q=q, slot=slot: E.transpose(
                        out=p_tr[slot][:], in_=s_xc[0][:, i * 128:(i + 1) * 128],
                        identity=s_id16[:]),
                        after=[xld, idc] + g_tr[slot].w()
                        + (wEs[i] if q == 0 else []))
                    g_xc[0].r(tr)
                    ev = R.op("act", lambda E, i=i, q=q, slot=slot: E.activation(
                        out=s_xT[i][:, q * 128:(q + 1) * 128], in_=p_tr[slot][:], func=AF.Copy),
                        after=[tr])
                    g_tr[slot].r(ev)
            xT_all = ("act", R.cnt["act"])

            for j in range(32):
                isu = j < 16
                i = j if isu else j - 16
                wt = s_wuT if isu else s_wzT
                bank = j % 2
                mm = None
                for kk in range(16):
                    aft = []
                    if kk == 0:
                        aft = [xT_all, w_done] + g_mm[bank].w()
                        if j == 0 and k > 0:
                            aft += g_dtp.w()
                    mm = R.op("pe", lambda E, kk=kk, i=i, wt=wt, bank=bank: E.matmul(
                        out=p_mm[bank][:], lhsT=wt[kk][:, i * 128:(i + 1) * 128], rhs=s_xT[kk][:],
                        start=(kk == 0), stop=(kk == 15)), after=aft)
                for kk in range(16):
                    g_xT[kk].r(mm)
                if isu:
                    ev = R.op("act", lambda E, i=i, bank=bank: E.activation(
                        out=s_upre[:, 3:T + 3], in_=p_mm[bank][:], func=AF.Copy),
                        after=[mm] + g_upre.w())
                    g_mm[bank].r(ev)
                    hcp = R.op("dve", lambda E, i=i: E.tensor_copy(
                        out=s_upre[:, 0:3], in_=s_halo[:, 3 * i:3 * i + 3]),
                        after=[halo0] + g_halo.w())
                    c1 = R.op("dve", lambda E, i=i: E.tensor_scalar(
                        out=s_cvt[:], in0=s_upre[:, 0:T], scalar1=s_cw[:, 4 * i:4 * i + 1],
                        scalar2=None, op0=ALU.mult), after=[ev, hcp] + g_cvt.w())
                    for kk in range(1, 4):
                        c1 = R.op("dve", lambda E, i=i, kk=kk: E.scalar_tensor_tensor(
                            out=s_cvt[:], in0=s_upre[:, kk:kk + T],
                            scalar=s_cw[:, 4 * i + kk:4 * i + kk + 1],
                            in1=s_cvt[:], op0=ALU.mult, op1=ALU.add), after=[c1])
                    hal = R.op("dve", lambda E, i=i: E.tensor_copy(
                        out=s_halo[:, 3 * i:3 * i + 3], in_=s_upre[:, T:T + 3]), after=[c1])
                    g_upre.r(hal)
                    g_halo.r(hal)
                    sil = R.op("act", lambda E, i=i: E.activation(
                        out=s_u16t[:], in_=s_cvt[:], func=AF.Silu, bias=s_cb[:, i:i + 1]),
                        after=[c1] + g_u16t.w())
                    g_cvt.r(sil)
                    st_u = R.dma("qS", "stuz", lambda E, i=i, k=k: E.dma_start(
                        out=u16d_h.ap()[i * 128:(i + 1) * 128, k * T:(k + 1) * T], in_=s_u16t[:]),
                        after=[sil])
                    m1 = R.op("pe", lambda E, i=i: E.matmul(
                        out=p_dt[:], lhsT=s_xpT[i][:, 0:128], rhs=s_u16t[:],
                        start=(i == 0), stop=(i == 15)), after=[sil])
                    m2 = R.op("pe", lambda E, i=i: E.matmul(
                        out=p_bc[:], lhsT=s_xpT[i][:, 128:160], rhs=s_u16t[:],
                        start=(i == 0), stop=(i == 15)), after=[sil])
                    g_u16t.r(st_u); g_u16t.r(m1); g_u16t.r(m2)
                else:
                    ev = R.op("act", lambda E, bank=bank: E.activation(
                        out=s_z16t[:], in_=p_mm[bank][:], func=AF.Copy),
                        after=[mm] + g_z16t.w())
                    g_mm[bank].r(ev)
                    st_z = R.dma("qS", "stuz", lambda E, i=i, k=k: E.dma_start(
                        out=z16d_h.ap()[i * 128:(i + 1) * 128, k * T:(k + 1) * T], in_=s_z16t[:]),
                        after=[ev])
                    g_z16t.r(st_z)

            dtev = R.op("dve", lambda E: E.tensor_copy(out=s_cvt[:], in_=p_dt[:]),
                        after=[m1] + g_cvt.w())
            g_dtp.r(dtev)
            st1 = R.dma("qS", "cvt", lambda E, k=k: E.dma_start(
                out=xdblp_h.ap()[0:128, k * T:(k + 1) * T], in_=s_cvt[:]), after=[dtev])
            g_cvt.r(st1)
            bcev = R.op("dve", lambda E: E.tensor_copy(out=s_cvt[0:32, :], in_=p_bc[:]),
                        after=[m2, st1] + g_cvt.w())
            g_dtp.r(bcev)
            st2 = R.dma("qS", "cvt", lambda E, k=k: E.dma_start(
                out=xdblp_h.ap()[128:160, k * T:(k + 1) * T], in_=s_cvt[0:32, :]), after=[bcev])
            g_cvt.r(st2)

        p1_tail = R.all_cur()

    c_xdbl = R.cc(lambda E: E.collective_compute(
        "AllReduce", ALU.add, replica_groups=[[0, 1], [2, 3], [4, 5], [6, 7]],
        ins=[xdblp_h.ap().opt()], outs=[xdblr_h.ap().opt()]), after=p1_tail)

    with ExitStack() as P:
        EE = P.enter_context
        s_dtT = EE(nc.sbuf_tensor("s_dtT", [128, HC], BF16))
        s_carry = EE(nc.sbuf_tensor("s_carry", [128, NB * NST], F32))
        s_dt16 = EE(nc.sbuf_tensor("s_dt16", [128, T], BF16))
        s_Brep = EE(nc.sbuf_tensor("s_Brep", [128, NST * T], BF16))
        s_Crep = EE(nc.sbuf_tensor("s_Crep", [128, NST * T], BF16))
        s_bcfl = EE(nc.sbuf_tensor("s_bcfl", [33, 2048], BF16))
        s_on1 = EE(nc.sbuf_tensor("s_on1", [33, 128], BF16))
        s_dA = EE(nc.sbuf_tensor("s_dA", [128, NST * T], F32))
        s_bt = EE(nc.sbuf_tensor("s_bt", [128, NST * T], BF16))
        s_h = EE(nc.sbuf_tensor("s_h", [128, NST * T], BF16))
        s_del = EE(nc.sbuf_tensor("s_del", [128, T], F32))
        s_t1 = EE(nc.sbuf_tensor("s_t1", [128, T], F32))
        s_u16 = EE(nc.sbuf_tensor("s_u16", [128, T], BF16))
        s_du16 = EE(nc.sbuf_tensor("s_du16", [128, T], BF16))
        s_z16 = EE(nc.sbuf_tensor("s_z16", [128, T], BF16))
        s_y = EE(nc.sbuf_tensor("s_y", [128, T], F32))
        s_sz = EE(nc.sbuf_tensor("s_sz", [128, T], BF16))
        s_ys16 = EE(nc.sbuf_tensor("s_ys16", [128, T], BF16))
        p_d = EE(nc.psum_tensor("p_d", [128, T], F32))
        p_rep = [EE(nc.psum_tensor("p_rep%d" % s, [128, 1024], F32)) for s in range(2)]

        dtT_ld = R.dma("qG", "dtt", lambda E: E.dma_start(out=s_dtT[:], in_=dtT_h.ap()), after=[c_xdbl])
        on1v = R.op("dve", lambda E: E.memset(s_on1[:], 1.0))
        carry0 = R.op("dve", lambda E: E.memset(s_carry[:], 0.0))

        g_dt16 = Guard()
        g_Brep = Guard(); g_Crep = Guard(); g_bflat = Guard(); g_cflat = Guard()
        g_pd = Guard(); g_t1 = Guard(); g_del = Guard()
        g_u16 = Guard(); g_z16 = Guard(); g_dA = Guard()
        g_bt = Guard(); g_h = Guard(); g_y = Guard(); g_sz = Guard(); g_ys = Guard()
        g_du16 = Guard()

        for k in range(NCH):
            dtl = R.dma("qG", "dtl", lambda E, k=k: E.dma_start(
                out=s_dt16[:], in_=xdblr_h.ap()[0:128, k * T:(k + 1) * T]),
                after=[c_xdbl] + g_dt16.w())
            cast_dt = dtl
            bfd = R.dma("qS", "bfd", lambda E, k=k: E.dma_start(
                out=bflat_h.ap(), in_=xdblr_h.ap()[128:144, k * T:(k + 1) * T]),
                after=[c_xdbl] + g_bflat.w())
            cfd = R.dma("qS", "cfd", lambda E, k=k: E.dma_start(
                out=cflat_h.ap(), in_=xdblr_h.ap()[144:160, k * T:(k + 1) * T]),
                after=[c_xdbl] + g_cflat.w())
            g_prep = getattr(R, "_g_prep", None)
            if g_prep is None:
                g_prep = R._g_prep = [Guard(), Guard()]
                R._g_bcfl = [Guard(), Guard()]
            g_bcfl = R._g_bcfl
            brl = crl = None
            for rep_i, (row0, s_rep, flat_ap, dep) in enumerate(
                    [(0, s_Brep, bflat_h, bfd), (32, s_Crep, cflat_h, cfd)]):
                last = None
                for seg in range(NST * T // 1024):
                    par = seg % 2
                    fl_ld = R.dma("qG", "fl%d" % par, lambda E, row0=row0, seg=seg, par=par, flat_ap=flat_ap: E.dma_start(
                        out=s_bcfl[row0:row0 + 1, par * 1024:(par + 1) * 1024],
                        in_=flat_ap.ap().rearrange("n t -> (n t)").unsqueeze(0)
                        [:, seg * 1024:(seg + 1) * 1024]),
                        after=[dep] + g_bcfl[par].w())
                    mmr0 = R.op("pe", lambda E, row0=row0, par=par: E.matmul(
                        out=p_rep[par][:, 0:512], lhsT=s_on1[row0:row0 + 1, :],
                        rhs=s_bcfl[row0:row0 + 1, par * 1024:par * 1024 + 512],
                        start=True, stop=True),
                        after=[fl_ld, on1v] + g_prep[par].w())
                    mmr = R.op("pe", lambda E, row0=row0, par=par: E.matmul(
                        out=p_rep[par][:, 512:1024], lhsT=s_on1[row0:row0 + 1, :],
                        rhs=s_bcfl[row0:row0 + 1, par * 1024 + 512:(par + 1) * 1024],
                        start=True, stop=True))
                    g_bcfl[par].r(mmr)
                    evr = R.op("act", lambda E, s_rep=s_rep, seg=seg, par=par: E.activation(
                        out=s_rep[:, seg * 1024:(seg + 1) * 1024], in_=p_rep[par][:],
                        func=AF.Copy), after=[mmr])
                    g_prep[par].r(evr)
                    last = evr
                if rep_i == 0:
                    brl = last
                    g_bflat.r(fl_ld)
                else:
                    crl = last
                    g_cflat.r(fl_ld)

            for i in range(NB):
                mmd = R.op("pe", lambda E, i=i: E.matmul(
                    out=p_d[:], lhsT=s_dtT[:, i * 128:(i + 1) * 128], rhs=s_dt16[:],
                    start=True, stop=True),
                    after=[cast_dt, dtT_ld] + g_pd.w())
                g_dt16.r(mmd)
                a1 = R.op("act", lambda E, i=i: E.activation(
                    out=s_t1[:], in_=p_d[:], func=AF.Abs, bias=s_dtb[:, i:i + 1]),
                    after=[mmd] + g_t1.w())
                a2 = R.op("act", lambda E: E.activation(out=s_t1[:], in_=s_t1[:], func=AF.Exp, scale=-1.0))
                a3 = R.op("act", lambda E: E.activation(out=s_t1[:], in_=s_t1[:], func=AF.Ln, bias=1.0))
                a4 = R.op("dve", lambda E, i=i: E.tensor_scalar(
                    out=s_del[:], in0=p_d[:], scalar1=s_dtb[:, i:i + 1], scalar2=0.0,
                    op0=ALU.add, op1=ALU.max),
                    after=[mmd] + g_del.w())
                g_pd.r(a1); g_pd.r(a4)
                dl = R.op("dve", lambda E: E.tensor_tensor(out=s_del[:], in0=s_del[:], in1=s_t1[:],
                                                           op=ALU.add), after=[a3, a4])
                g_t1.r(dl)
                ul = R.dma("qS", "ul", lambda E, i=i, k=k: E.dma_start(
                    out=s_u16[:], in_=u16d_h.ap()[i * 128:(i + 1) * 128, k * T:(k + 1) * T]),
                    after=g_u16.w())
                du = R.op("dve", lambda E: E.tensor_tensor(out=s_du16[:], in0=s_del[:], in1=s_u16[:],
                                                           op=ALU.mult),
                          after=[dl, ul] + g_du16.w())
                exps = []
                dAw = g_dA.w()
                for n in range(NST):
                    exps.append(R.op("act", lambda E, i=i, n=n: E.activation(
                        out=s_dA[:, n * T:(n + 1) * T], in_=s_del[:], func=AF.Exp,
                        scale=s_negA[:, i * NST + n: i * NST + n + 1]),
                        after=[dl, negA_done] + (dAw if n == 0 else [])))
                    g_del.r(exps[n])
                du_bc = s_du16[:].unsqueeze(1).broadcast_to([128, NST, T])
                bt = R.op("dve", lambda E, du_bc=du_bc: E.tensor_tensor(
                    out=s_bt.rearrange("p (n t) -> p n t", n=NST), in0=du_bc,
                    in1=s_Brep.rearrange("p (n t) -> p n t", n=NST), op=ALU.mult),
                    after=[du, brl] + g_bt.w())
                g_du16.r(bt); g_Brep.r(bt)
                sc = bt
                hw = g_h.w()
                for n in range(NST):
                    sc = R.op("dve", lambda E, i=i, n=n: E.tensor_tensor_scan(
                        out=s_h[:, n * T:(n + 1) * T], data0=s_dA[:, n * T:(n + 1) * T],
                        data1=s_bt[:, n * T:(n + 1) * T],
                        initial=s_carry[:, i * NST + n: i * NST + n + 1],
                        op0=ALU.mult, op1=ALU.add),
                        after=[exps[n], carry0, bt] + (hw if n == 0 else []))
                    g_dA.r(sc)
                cu = R.op("dve", lambda E, i=i: E.tensor_copy(
                    out=s_carry[:, i * NST:(i + 1) * NST],
                    in_=s_h[:].rearrange("p (n t) -> p n t", n=NST)[:, :, T - 1:T].squeeze(2)),
                    after=[sc])
                hc = R.op("dve", lambda E: E.tensor_tensor(out=s_bt[:], in0=s_h[:], in1=s_Crep[:],
                                                           op=ALU.mult), after=[cu, crl])
                g_h.r(hc); g_Crep.r(hc)
                tr1 = R.op("dve", lambda E: E.tensor_tensor(
                    out=s_bt[:, 0:8 * T], in0=s_bt[:, 0:8 * T], in1=s_bt[:, 8 * T:16 * T],
                    op=ALU.add), after=[hc])
                tr2 = R.op("dve", lambda E: E.tensor_tensor(
                    out=s_bt[:, 0:4 * T], in0=s_bt[:, 0:4 * T], in1=s_bt[:, 4 * T:8 * T],
                    op=ALU.add))
                tr3 = R.op("dve", lambda E: E.tensor_tensor(
                    out=s_bt[:, 0:2 * T], in0=s_bt[:, 0:2 * T], in1=s_bt[:, 2 * T:4 * T],
                    op=ALU.add))
                yr = R.op("dve", lambda E: E.tensor_tensor(
                    out=s_y[:], in0=s_bt[:, 0:T], in1=s_bt[:, T:2 * T],
                    op=ALU.add), after=g_y.w())
                g_bt.r(yr)
                zl = R.dma("qS", "zl", lambda E, i=i, k=k: E.dma_start(
                    out=s_z16[:], in_=z16d_h.ap()[i * 128:(i + 1) * 128, k * T:(k + 1) * T]),
                    after=g_z16.w())
                szv = R.op("act", lambda E: E.activation(out=s_sz[:], in_=s_z16[:], func=AF.Silu),
                           after=[zl] + g_sz.w())
                g_z16.r(szv)
                y2 = R.op("dve", lambda E, i=i: E.scalar_tensor_tensor(
                    out=s_y[:], in0=s_u16[:], scalar=s_dvec[:, i:i + 1], in1=s_y[:],
                    op0=ALU.mult, op1=ALU.add), after=[yr])
                g_u16.r(y2)
                ysv = R.op("dve", lambda E: E.tensor_tensor(out=s_ys16[:], in0=s_y[:], in1=s_sz[:],
                                                            op=ALU.mult), after=[y2, szv] + g_ys.w())
                g_y.r(ysv); g_sz.r(ysv)
                yst = R.dma("qS", "yst", lambda E, i=i, k=k: E.dma_start(
                    out=ys16d_h.ap()[i * 128:(i + 1) * 128, k * T:(k + 1) * T], in_=s_ys16[:]),
                    after=[ysv])
                g_ys.r(yst)
        p2_tail = R.all_cur()

    c_ys = R.cc(lambda E: E.collective_compute(
        "AllGather", ALU.bypass, replica_groups=[[0, 1], [2, 3], [4, 5], [6, 7]],
        ins=[ys16d_h.ap().opt()], outs=[ysf_h.ap().opt()]), after=p2_tail)

    with ExitStack() as P:
        EE = P.enter_context
        s_opk = [EE(nc.sbuf_tensor("s_opk%d" % s, [128, D], BF16)) for s in range(2)]
        s_id32_3 = EE(nc.sbuf_tensor("s_id32p3", [128, 128], F32))
        s_ysall = EE(nc.sbuf_tensor("s_ysall", [128, 32 * 256], BF16))
        s_sT = [EE(nc.sbuf_tensor("s_sTq%d" % q, [128, D], F32)) for q in range(2)]
        s_xrow = EE(nc.sbuf_tensor("s_xrow", [128, D], F32))
        s_mu = EE(nc.sbuf_tensor("s_mu", [128, 1], F32))
        s_mu2 = EE(nc.sbuf_tensor("s_mu2", [128, 1], F32))
        s_m2 = EE(nc.sbuf_tensor("s_m2", [128, 1], F32))
        s_sd = EE(nc.sbuf_tensor("s_sd", [128, 1], F32))
        s_rstd = EE(nc.sbuf_tensor("s_rstd", [128, 1], F32))
        s_xf = EE(nc.sbuf_tensor("s_xf", [128, D], F32))
        s_sq = EE(nc.sbuf_tensor("s_sq", [128, D], F32))
        s_xf16 = EE(nc.sbuf_tensor("s_xf16", [128, D], BF16))
        s_xft = [EE(nc.sbuf_tensor("s_xft%d" % kk, [128, 128], F32)) for kk in range(16)]
        s_rwT = EE(nc.sbuf_tensor("s_rwT", [128, 16 * NEXP], F32))
        s_lg = EE(nc.sbuf_tensor("s_lg", [8, 128], F32))
        s_lgt = EE(nc.sbuf_tensor("s_lgt", [128, 8], F32))
        s_mx = EE(nc.sbuf_tensor("s_mx", [128, 8], F32))
        s_mi = EE(nc.sbuf_tensor("s_mi", [128, 8], U32))
        s_nm = EE(nc.sbuf_tensor("s_nm", [128, 1], F32))
        s_ex = EE(nc.sbuf_tensor("s_ex", [128, 8], F32))
        s_ssum = EE(nc.sbuf_tensor("s_ssum", [128, 1], F32))
        s_rs = EE(nc.sbuf_tensor("s_rs", [128, 1], F32))
        s_t2p = EE(nc.sbuf_tensor("s_t2p", [128, 4], F32))
        p_sw = [EE(nc.psum_tensor("p_sw%d" % s, [128, 2048], F32)) for s in range(2)]

        id32l = R.dma("qS", "id3", lambda E: E.dma_start(out=s_id32_3[:], in_=ident), after=[c_ys])
        rwld = R.dma("qS", "rw", lambda E: E.dma_start(
            out=s_rwT[:].rearrange("p (kk e) -> p kk e", e=NEXP),
            in_=bass.AP(rwT_h, 0, [[NEXP, 128], [128 * NEXP, 16], [1, NEXP]])), after=[c_ys])

        g_ysall = Guard()
        g_opk = [Guard() for _ in range(2)]
        g_psw = [Guard(), Guard()]
        g_sT = [Guard() for _ in range(2)]
        g_xrow = Guard(); g_xf = Guard(); g_sq = Guard(); g_xf16 = Guard()
        g_ptr = [Guard(), Guard()]; g_plg = Guard(); g_ptl = Guard()
        g_xft = [Guard() for _ in range(16)]
        g_lg = Guard(); g_lgt = Guard()
        g_mx = Guard(); g_mi = Guard(); g_nm = Guard(); g_ex = Guard()
        g_ssum = Guard(); g_rs = Guard(); g_t2p = Guard()
        g_mu = Guard(); g_mu2 = Guard(); g_m2 = Guard(); g_sd = Guard(); g_rstd = Guard()

        for k in range(NCH):
            jofs = k
            g_w = g_ysall.w()

            def do_regs(E, jofs=jofs):
                E.reg_load(r_t, s_dyn[0:1, jofs:jofs + 1])
            R.raw("gp", do_regs, after=[lN])
            ysld = R.dma("qG", "ys", lambda E: E.dma_start(
                out=s_ysall[:].rearrange("p (kk t) -> p kk t", t=256),
                in_=bass.AP(ysf_h, r_t, [[L, 128], [128 * L, 32], [1, 256]])),
                after=[c_ys] + g_w)
            ys_all = ysld

            mm = None
            lastq = [None, None]
            for kk in range(32):
                buf = kk % 2
                opl = R.dma("qG", "op%d" % buf, lambda E, kk=kk, buf=buf: E.dma_start(
                    out=s_opk[buf][:], in_=opT_h.ap()[kk * 128:(kk + 1) * 128, :]),
                    after=g_opk[buf].w() + ([c_ys] if k == 0 and kk < 2 else []))
                for q in range(2):
                    for dq in range(4):
                        aft = []
                        if dq == 0:
                            aft = [ys_all, opl]
                            if kk == 0:
                                aft = aft + g_psw[q].w()
                        mm = R.op("pe", lambda E, kk=kk, q=q, dq=dq, buf=buf: E.matmul(
                            out=p_sw[q][:, dq * 512:(dq + 1) * 512],
                            lhsT=s_ysall[:, kk * 256 + q * 128:kk * 256 + (q + 1) * 128],
                            rhs=s_opk[buf][:, dq * 512:(dq + 1) * 512],
                            start=(kk == 0), stop=(kk == 31)), after=aft)
                    lastq[q] = mm
                g_opk[buf].r(mm)
            g_ysall.r(mm)
            for q in range(2):
                ev = R.op("act", lambda E, q=q: E.activation(
                    out=s_sT[q][:], in_=p_sw[q][:], func=AF.Copy),
                    after=[lastq[q]] + g_sT[q].w())
                g_psw[q].r(ev)
            sT_done = ("act", R.cnt["act"])

            for q in range(2):
                row0 = k * 256 + q * 128
                xl = R.dma("qS", "xr3", lambda E, row0=row0: E.dma_start(
                    out=s_xrow[:], in_=xown_h.ap()[row0:row0 + 128, :]), after=g_xrow.w())
                add = R.op("dve", lambda E, q=q: E.tensor_tensor(
                    out=s_sT[q][:], in0=s_sT[q][:], in1=s_xrow[:], op=ALU.add),
                    after=[sT_done, xl])
                g_xrow.r(add)
                mu = R.op("dve", lambda E, q=q: E.tensor_reduce(
                    out=s_mu[:], in_=s_sT[q][:], axis=AX.X, op=ALU.add), after=[add] + g_mu.w())
                mub = R.op("dve", lambda E: E.tensor_scalar(
                    out=s_mu[:], in0=s_mu[:], scalar1=1.0 / D, scalar2=None, op0=ALU.mult))
                sq = R.op("act", lambda E, q=q: E.activation(
                    out=s_sq[:], in_=s_sT[q][:], func=AF.Square), after=[add] + g_sq.w())
                m2a = R.op("dve", lambda E: E.tensor_reduce(
                    out=s_m2[:], in_=s_sq[:], axis=AX.X, op=ALU.add), after=[sq] + g_m2.w())
                g_sq.r(m2a)
                m2b = R.op("dve", lambda E: E.tensor_scalar(
                    out=s_m2[:], in0=s_m2[:], scalar1=1.0 / D, scalar2=None, op0=ALU.mult))
                mu2v = R.op("dve", lambda E: E.tensor_scalar(
                    out=s_mu2[:], in0=s_mu[:], scalar1=s_mu[:, 0:1], scalar2=None, op0=ALU.mult),
                    after=[mub] + g_mu2.w())
                vv = R.op("dve", lambda E: E.tensor_tensor(
                    out=s_m2[:], in0=s_m2[:], in1=s_mu2[:], op=ALU.subtract), after=[m2b, mu2v])
                g_mu2.r(vv)
                sdv = R.op("act", lambda E: E.activation(
                    out=s_sd[:], in_=s_m2[:], func=AF.Sqrt, bias=s_eps[:, 0:1]),
                    after=[vv] + g_sd.w())
                g_m2.r(sdv)
                rst = R.op("dve", lambda E: E.reciprocal(out=s_rstd[:], in_=s_sd[:]),
                           after=[sdv] + g_rstd.w())
                g_sd.r(rst)
                nmu = R.op("dve", lambda E: E.tensor_scalar(
                    out=s_mu[:], in0=s_mu[:], scalar1=-1.0, scalar2=None, op0=ALU.mult),
                    after=[mu2v])
                g_mu.r(nmu)
                xfv = R.op("dve", lambda E, q=q: E.tensor_scalar(
                    out=s_xf[:], in0=s_sT[q][:], scalar1=s_mu[:, 0:1], scalar2=s_rstd[:, 0:1],
                    op0=ALU.add, op1=ALU.mult), after=[nmu, rst] + g_xf.w())
                g_sT[q].r(xfv); g_mu.r(xfv); g_rstd.r(xfv)
                xf16v = R.op("dve", lambda E: E.tensor_copy(out=s_xf16[:], in_=s_xf[:]),
                             after=[xfv] + g_xf16.w())
                st32 = R.dma("qS", "xf32s", lambda E, row0=row0: E.dma_start(
                    out=xf32o_h.ap()[row0:row0 + 128, :], in_=s_xf[:]), after=[xfv])
                st16 = R.dma("qS", "xf16s", lambda E, row0=row0: E.dma_start(
                    out=xf16o_h.ap()[row0:row0 + 128, :], in_=s_xf16[:]), after=[xf16v])
                g_xf16.r(st16)
                g_xf.r(st32)
                for kk in range(16):
                    slot = kk % 2
                    tr = R.op("pe", lambda E, kk=kk, slot=slot: E.transpose(
                        out=p_sw[0][:, slot * 512:slot * 512 + 128],
                        in_=s_xf[:, kk * 128:(kk + 1) * 128],
                        identity=s_id32_3[:]),
                        after=[xfv, id32l] + g_ptr[slot].w())
                    g_xf.r(tr)
                    tev = R.op("act", lambda E, kk=kk, slot=slot: E.activation(
                        out=s_xft[kk][:], in_=p_sw[0][:, slot * 512:slot * 512 + 128],
                        func=AF.Copy),
                        after=[tr] + g_xft[kk].w())
                    g_ptr[slot].r(tev)
                    g_psw[0].r(tev)
                xft_done = ("act", R.cnt["act"])
                mm2 = None
                for kk in range(16):
                    aft = ([xft_done, rwld] + g_plg.w()) if kk == 0 else []
                    mm2 = R.op("pe", lambda E, kk=kk: E.matmul(
                        out=p_sw[1][0:8, 0:128], lhsT=s_rwT[:, kk * NEXP:(kk + 1) * NEXP],
                        rhs=s_xft[kk][:],
                        start=(kk == 0), stop=(kk == 15)), after=aft)
                for kk in range(16):
                    g_xft[kk].r(mm2)
                lgev = R.op("act", lambda E: E.activation(out=s_lg[:], in_=p_sw[1][0:8, 0:128],
                                                          func=AF.Copy),
                            after=[mm2] + g_lg.w())
                g_plg.r(lgev)
                g_psw[1].r(lgev)
                trl = R.op("pe", lambda E: E.transpose(
                    out=p_sw[1][:, 512:520], in_=s_lg[0:8, :], identity=s_id32_3[0:8, 0:8]),
                    after=[lgev] + g_ptl.w())
                g_lg.r(trl)
                ltv = R.op("act", lambda E: E.activation(out=s_lgt[:], in_=p_sw[1][:, 512:520],
                                                         func=AF.Copy), after=[trl] + g_lgt.w())
                g_ptl.r(ltv)
                g_psw[1].r(ltv)
                mx = R.op("dve", lambda E: E.max(out=s_mx[:], in_=s_lgt[:]),
                          after=[ltv] + g_mx.w())
                g_lgt.r(mx)
                mi = R.op("dve", lambda E: E.max_index(out=s_mi[:], in_max=s_mx[:], in_values=s_lgt[:]),
                          after=[mx] + g_mi.w())
                g_lgt.r(mi)
                nm = R.op("dve", lambda E: E.tensor_scalar(
                    out=s_nm[:], in0=s_mx[:, 0:1], scalar1=-1.0, scalar2=None, op0=ALU.mult),
                    after=[mx] + g_nm.w())
                exv = R.op("act", lambda E: E.activation(out=s_ex[:], in_=s_mx[:], func=AF.Exp,
                                                         bias=s_nm[:, 0:1]),
                           after=[nm] + g_ex.w())
                g_mx.r(exv); g_nm.r(exv)
                ssv = R.op("dve", lambda E: E.tensor_reduce(
                    out=s_ssum[:], in_=s_ex[:], axis=AX.X, op=ALU.add), after=[exv] + g_ssum.w())
                rsv = R.op("dve", lambda E: E.reciprocal(out=s_rs[:], in_=s_ssum[:]),
                           after=[ssv] + g_rs.w())
                g_ssum.r(rsv)
                c0 = R.op("dve", lambda E: E.tensor_copy(out=s_t2p[:, 0:1], in_=s_mi[:, 0:1]),
                          after=[mi] + g_t2p.w())
                c1v = R.op("dve", lambda E: E.tensor_copy(out=s_t2p[:, 1:2], in_=s_mi[:, 1:2]))
                g_mi.r(c1v)
                c2 = R.op("dve", lambda E: E.tensor_tensor(out=s_t2p[:, 2:3], in0=s_ex[:, 0:1],
                                                           in1=s_rs[:, 0:1], op=ALU.mult),
                          after=[rsv])
                c3 = R.op("dve", lambda E: E.tensor_tensor(out=s_t2p[:, 3:4], in0=s_ex[:, 1:2],
                                                           in1=s_rs[:, 0:1], op=ALU.mult))
                g_ex.r(c3); g_rs.r(c3)
                stt2 = R.dma("qS", "t2s", lambda E, row0=row0: E.dma_start(
                    out=top2o_h.ap()[row0:row0 + 128, :], in_=s_t2p[:]), after=[c3, c0, c1v, c2])
                g_t2p.r(stt2)
        p3_tail = R.all_cur()

    c_xf = R.cc(lambda E: E.collective_compute(
        "AllGather", ALU.bypass, replica_groups=[list(range(8))],
        ins=[xf16o_h.ap().opt()], outs=[xffull_h.ap().opt()]), after=p3_tail)
    c_t2 = R.cc(lambda E: E.collective_compute(
        "AllGather", ALU.bypass, replica_groups=[list(range(8))],
        ins=[top2o_h.ap().opt()], outs=[top2full_h.ap().opt()]), after=[c_xf])

    with ExitStack() as P:
        EE = P.enter_context
        s_id32_4 = EE(nc.sbuf_tensor("s_id32p4", [128, 128], F32))
        s_b1r = EE(nc.sbuf_tensor("s_b1r", [128, F // 128], F32))
        s_b2r = EE(nc.sbuf_tensor("s_b2r", [128, D // 128], F32))
        s_ssf = EE(nc.sbuf_tensor("s_ssf", [128, 4], F32))
        s_ssi = EE(nc.sbuf_tensor("s_ssi", [128, 4], I32))
        P4r = ExitStack()
        ER = P4r.enter_context
        s_r4 = ER(nc.sbuf_tensor("s_r4", [128, 512], F32))
        s_E1 = ER(nc.sbuf_tensor("s_E1", [128, 128], F32))
        s_E2 = ER(nc.sbuf_tensor("s_E2", [128, 128], F32))
        s_P1 = ER(nc.sbuf_tensor("s_P1", [128, 128], F32))
        s_P2 = ER(nc.sbuf_tensor("s_P2", [128, 128], F32))
        s_oh = ER(nc.sbuf_tensor("s_oh", [128, 8 * 128], F32))
        s_pos = ER(nc.sbuf_tensor("s_pos", [128, 8 * 128], F32))
        s_keep = ER(nc.sbuf_tensor("s_keep", [128, 8 * 128], F32))
        s_cum = ER(nc.sbuf_tensor("s_cum", [128, 128], F32))
        s_tot = ER(nc.sbuf_tensor("s_tot", [128, 8], F32))
        s_TB = ER(nc.sbuf_tensor("s_TB", [8, 128], F32))
        s_TBc = ER(nc.sbuf_tensor("s_TBc", [8, 128], F32))
        s_prefP = ER(nc.sbuf_tensor("s_prefP", [128, 8], F32))
        s_gt8 = ER(nc.sbuf_tensor("s_gt8", [8, 1], F32))
        s_gtb = ER(nc.sbuf_tensor("s_gtb", [128, 8], F32))
        s_ts1 = ER(nc.sbuf_tensor("s_ts1", [128, 128], F32))
        s_ts2v = ER(nc.sbuf_tensor("s_ts2v", [128, 128], F32))
        s_kp = ER(nc.sbuf_tensor("s_kp", [128, 128], F32))
        s_w1t = ER(nc.sbuf_tensor("s_w1t", [128, 128], F32))
        s_w2t = ER(nc.sbuf_tensor("s_w2t", [128, 128], F32))
        s_tmp = ER(nc.sbuf_tensor("s_tmpP4", [128, 128], F32))
        s_iotac = ER(nc.sbuf_tensor("s_iotac", [128, CAP], F32))
        s_cc4 = ER(nc.sbuf_tensor("s_cc4", [128, 1], I32))
        s_cc4f = ER(nc.sbuf_tensor("s_cc4f", [128, 1], F32))
        s_pcol = ER(nc.sbuf_tensor("s_pcol", [128, 1], F32))
        s_jrow = ER(nc.sbuf_tensor("s_jrow", [128, 128], F32))
        s_lhsA = ER(nc.sbuf_tensor("s_lhsA", [128, 384], BF16))
        s_lhsB = ER(nc.sbuf_tensor("s_lhsB", [128, 384], BF16))
        s_ohj = [ER(nc.sbuf_tensor("s_ohj%d" % s, [128, CAP], BF16)) for s in range(2)]
        s_tab4 = ER(nc.sbuf_tensor("s_tab4", [4, CAP], F32))
        s_ssfh = ER(nc.sbuf_tensor("s_ssfh", [128, 4], F32))
        s_ssfl = ER(nc.sbuf_tensor("s_ssfl", [128, 4], F32))
        p_h1 = [EE(nc.psum_tensor("p_h1%d" % s, [128, CAP], F32)) for s in range(2)]
        p_h2 = [EE(nc.psum_tensor("p_h2%d" % s, [128, CAP], F32)) for s in range(2)]
        p_g16 = [EE(nc.psum_tensor("p_g16%d" % s, [128, 128], BF16)) for s in range(2)]
        p_r = EE(nc.psum_tensor("p_r", [128, 128], F32))
        p_tab4 = EE(nc.psum_tensor("p_tab4", [4, CAP], F32))

        id32l4 = R.dma("qS", "id4", lambda E: E.dma_start(out=s_id32_4[:], in_=ident), after=[c_t2])
        b1l = R.dma("qS", "b1", lambda E: E.dma_start(out=s_b1r[:], in_=b1r), after=[c_t2])
        b2l = R.dma("qS", "b2", lambda E: E.dma_start(out=s_b2r[:], in_=b2r))
        r4l = R.dma("qS", "r4", lambda E: E.dma_start(
            out=s_r4[:], in_=top2full_h.ap().rearrange("(p j) c -> p (j c)", p=128)),
            after=[c_t2])
        exE1 = R.op("dve", lambda E: E.tensor_copy(
            out=s_E1[:], in_=s_r4[:].rearrange("p (j c) -> p j c", c=4)[:, :, 0:1].squeeze(2)),
            after=[r4l])
        exE2 = R.op("dve", lambda E: E.tensor_copy(
            out=s_E2[:], in_=s_r4[:].rearrange("p (j c) -> p j c", c=4)[:, :, 1:2].squeeze(2)))
        exP1 = R.op("dve", lambda E: E.tensor_copy(
            out=s_P1[:], in_=s_r4[:].rearrange("p (j c) -> p j c", c=4)[:, :, 2:3].squeeze(2)))
        exP2 = R.op("dve", lambda E: E.tensor_copy(
            out=s_P2[:], in_=s_r4[:].rearrange("p (j c) -> p j c", c=4)[:, :, 3:4].squeeze(2)))

        ones_bc = s_ones[:].unsqueeze(2).broadcast_to([128, 1, 128]).squeeze(1)
        ones8_bc = s_ones[0:8, :].unsqueeze(2).broadcast_to([8, 1, 128]).squeeze(1)
        gtb_done = None
        agg_done = []
        g_tot = Guard(); g_TB = Guard(); g_TBc = Guard(); g_prefP = Guard(); g_pr = Guard()
        g_cumb = Guard()
        for ph in range(2):
            Ed = s_E1 if ph == 0 else s_E2
            Edev = exE1 if ph == 0 else exE2
            for e in range(E):
                oh = R.op("dve", lambda E_, e=e, Ed=Ed: E_.tensor_single_scalar(
                    out=s_oh[:, e * 128:(e + 1) * 128], in_=Ed[:], scalar=float(e), op=ALU.is_equal),
                    after=[Edev])
                cm = R.op("dve", lambda E_, e=e: E_.tensor_tensor_scan(
                    out=s_cum[:], data0=ones_bc, data1=s_oh[:, e * 128:(e + 1) * 128],
                    initial=0.0, op0=ALU.mult, op1=ALU.add), after=[oh, ones1] + g_cumb.w())
                ps = R.op("dve", lambda E_, e=e: E_.tensor_tensor(
                    out=s_pos[:, e * 128:(e + 1) * 128], in0=s_cum[:],
                    in1=s_oh[:, e * 128:(e + 1) * 128], op=ALU.subtract), after=[cm])
                tc = R.op("dve", lambda E_, e=e: E_.tensor_copy(
                    out=s_tot[:, e:e + 1], in_=s_cum[:, 127:128]),
                    after=[cm] + (g_tot.w() if e == 0 else []))
                g_cumb.r(tc)
            tot_done = ("dve", R.cnt["dve"])
            trt = R.op("pe", lambda E_: E_.transpose(out=p_r[0:8, :], in_=s_tot[:, 0:8],
                                                     identity=s_id32_4[:]),
                       after=[tot_done, id32l4] + g_pr.w())
            g_tot.r(trt)
            tbv = R.op("act", lambda E_: E_.activation(out=s_TB[:], in_=p_r[0:8, :], func=AF.Copy),
                       after=[trt] + g_TB.w())
            g_pr.r(tbv)
            cm8 = R.op("dve", lambda E_: E_.tensor_tensor_scan(
                out=s_TBc[:], data0=ones8_bc, data1=s_TB[:], initial=0.0,
                op0=ALU.mult, op1=ALU.add), after=[tbv, ones1] + g_TBc.w())
            if ph == 0:
                gtc = R.op("dve", lambda E_: E_.tensor_copy(out=s_gt8[:], in_=s_TBc[:, 127:128]),
                           after=[cm8])
                gtd = R.dma("qS", "gt", lambda E_: E_.dma_start(
                    out=gtscr_h.ap().rearrange("1 e -> e 1"), in_=s_gt8[:]), after=[gtc])
                gtb_done = R.dma("qG", "gtb", lambda E_: E_.dma_start(
                    out=s_gtb[:], in_=gtscr_h.ap().to_broadcast([128, NEXP])), after=[gtd])
            prf = R.op("dve", lambda E_: E_.tensor_tensor(out=s_TB[:], in0=s_TBc[:], in1=s_TB[:],
                                                          op=ALU.subtract), after=[cm8])
            trb = R.op("pe", lambda E_: E_.transpose(out=p_r[:, 0:8], in_=s_TB[:],
                                                     identity=s_id32_4[0:8, 0:8]),
                       after=[prf] + g_pr.w())
            g_TB.r(trb)
            prp = R.op("act", lambda E_: E_.activation(out=s_prefP[:], in_=p_r[:, 0:8], func=AF.Copy),
                       after=[trb] + g_prefP.w())
            g_pr.r(prp)
            g_TBc.r(prf)
            for e in range(E):
                if ph == 0:
                    pg = R.op("dve", lambda E_, e=e: E_.tensor_scalar(
                        out=s_pos[:, e * 128:(e + 1) * 128], in0=s_pos[:, e * 128:(e + 1) * 128],
                        scalar1=s_prefP[:, e:e + 1], scalar2=None, op0=ALU.add), after=[prp])
                else:
                    pg = R.op("dve", lambda E_, e=e: E_.tensor_scalar(
                        out=s_pos[:, e * 128:(e + 1) * 128], in0=s_pos[:, e * 128:(e + 1) * 128],
                        scalar1=s_prefP[:, e:e + 1], scalar2=s_gtb[:, e:e + 1],
                        op0=ALU.add, op1=ALU.add), after=[prp, gtb_done])
                g_prefP.r(pg)
                kp = R.op("dve", lambda E_, e=e: E_.tensor_single_scalar(
                    out=s_keep[:, e * 128:(e + 1) * 128], in_=s_pos[:, e * 128:(e + 1) * 128],
                    scalar=float(CAP), op=ALU.is_lt), after=[pg])
                t1_ = R.op("dve", lambda E_, e=e: E_.tensor_scalar(
                    out=s_pos[:, e * 128:(e + 1) * 128], in0=s_pos[:, e * 128:(e + 1) * 128],
                    scalar1=float(e * CAP) - OOBV, scalar2=None, op0=ALU.add), after=[kp])
                t2_ = R.op("dve", lambda E_, e=e: E_.tensor_tensor(
                    out=s_pos[:, e * 128:(e + 1) * 128], in0=s_pos[:, e * 128:(e + 1) * 128],
                    in1=s_keep[:, e * 128:(e + 1) * 128], op=ALU.mult), after=[t1_])
                R.op("dve", lambda E_, e=e: E_.tensor_scalar(
                    out=s_pos[:, e * 128:(e + 1) * 128], in0=s_pos[:, e * 128:(e + 1) * 128],
                    scalar1=OOBV, scalar2=None, op0=ALU.add), after=[t2_])
            slotm_done = ("dve", R.cnt["dve"])
            ts_dst = s_ts1 if ph == 0 else s_ts2v
            z1 = R.op("dve", lambda E_, ts_dst=ts_dst: E_.memset(ts_dst[:], 0.0), after=[slotm_done])
            z2 = R.op("dve", lambda E_: E_.memset(s_kp[:], 0.0))
            for e in range(E):
                tm = R.op("dve", lambda E_, e=e: E_.tensor_tensor(
                    out=s_tmp[:], in0=s_oh[:, e * 128:(e + 1) * 128],
                    in1=s_pos[:, e * 128:(e + 1) * 128], op=ALU.mult))
                R.op("dve", lambda E_, ts_dst=ts_dst: E_.tensor_tensor(
                    out=ts_dst[:], in0=ts_dst[:], in1=s_tmp[:], op=ALU.add))
                tm2 = R.op("dve", lambda E_, e=e: E_.tensor_tensor(
                    out=s_tmp[:], in0=s_oh[:, e * 128:(e + 1) * 128],
                    in1=s_keep[:, e * 128:(e + 1) * 128], op=ALU.mult))
                R.op("dve", lambda E_: E_.tensor_tensor(
                    out=s_kp[:], in0=s_kp[:], in1=s_tmp[:], op=ALU.add))
            wdst = s_w1t if ph == 0 else s_w2t
            Pv = s_P1 if ph == 0 else s_P2
            Pev = exP1 if ph == 0 else exP2
            wv = R.op("dve", lambda E_, wdst=wdst, Pv=Pv: E_.tensor_tensor(
                out=wdst[:], in0=Pv[:], in1=s_kp[:], op=ALU.mult), after=[Pev])
            agg_done.append(("dve", R.cnt["dve"]))

        io0 = R.op("gp", lambda E_: E_.iota(s_iotac[:], [[1, CAP]], 0, 0,
                                            allow_small_or_imprecise_dtypes=True))
        cc4l = R.dma("qG", "ccb", lambda E_: E_.dma_start(
            out=s_cc4[:], in_=dynoff[0:1, 13:14].to_broadcast([128, 1])))
        cc4c = R.op("dve", lambda E_: E_.tensor_copy(out=s_cc4f[:], in_=s_cc4[:]),
                    after=[cc4l])
        iov = R.op("dve", lambda E_: E_.tensor_scalar(
            out=s_iotac[:], in0=s_iotac[:], scalar1=s_cc4f[:, 0:1], scalar2=None,
            op0=ALU.add), after=[io0, cc4c])
        iop = R.op("gp", lambda E_: E_.iota(s_pcol[:], [[1, 1]], 0, 1,
                                            allow_small_or_imprecise_dtypes=True))
        ioj = R.op("gp", lambda E_: E_.iota(s_jrow[:], [[1, 128]], 0, 0,
                                            allow_small_or_imprecise_dtypes=True))
        for (dst, col, srcv, aftev) in (
                (s_lhsA, 0, None, [iop]), (s_lhsA, 1, s_jrow, [ioj]),
                (s_lhsA, 2, s_w1t, [agg_done[0]]),
                (s_lhsB, 0, None, [iop]), (s_lhsB, 1, s_jrow, [ioj]),
                (s_lhsB, 2, s_w2t, [agg_done[1]])):
            R.op("dve", lambda E_, dst=dst, col=col, srcv=srcv: E_.tensor_copy(
                out=dst[:].rearrange("p (j c) -> p j c", c=3)[:, :, col:col + 1].squeeze(2),
                in_=(s_pcol[:].broadcast_to([128, 128]) if srcv is None else srcv[:])),
                after=aftev)
        lhs_done = ("dve", R.cnt["dve"])
        g_ohj = [Guard(), Guard()]
        mmt = None
        for ph in range(2):
            slt = s_ts1 if ph == 0 else s_ts2v
            lhs = s_lhsA if ph == 0 else s_lhsB
            for j in range(128):
                buf = j % 2
                oh = R.op("dve", lambda E_, slt=slt, j=j, buf=buf: E_.tensor_scalar(
                    out=s_ohj[buf][:], in0=s_iotac[:], scalar1=slt[:, j:j + 1], scalar2=None,
                    op0=ALU.is_equal),
                    after=[iov, agg_done[ph]] + g_ohj[buf].w())
                mmt = R.op("pe", lambda E_, ph=ph, j=j, buf=buf, lhs=lhs: E_.matmul(
                    out=p_tab4[0:3, :], lhsT=lhs[:, 3 * j:3 * j + 3], rhs=s_ohj[buf][:],
                    start=(ph == 0 and j == 0), stop=(ph == 1 and j == 127)),
                    after=[oh] + ([lhs_done] if ph == 0 and j == 0 else []))
                g_ohj[buf].r(mmt)
        tabev = R.op("act", lambda E_: E_.activation(out=s_tab4[0:3, :], in_=p_tab4[0:3, :],
                                                     func=AF.Copy), after=[mmt])
        tabst = R.dma("qS", "tab", lambda E_: E_.dma_start(
            out=ptab_h.ap(), in_=s_tab4[0:3, :]), after=[tabev])
        c_tab = R.cc(lambda E_: E_.collective_compute(
            "AllGather", ALU.bypass, replica_groups=[list(range(8))],
            ins=[ptab_h.ap().opt()], outs=[rtab_h.ap().opt()]),
            after=[tabst])

        def do_regs_ss(E_):
            E_.reg_load(r_t, s_dyn[0:1, 12:13])
        R.raw("gp", do_regs_ss, after=[lN])
        ssgh = R.dma("qG", "ssg", lambda E_: E_.dma_start(
            out=s_ssfh[:].unsqueeze(2),
            in_=bass.AP(rtab_h, r_t, [[1, 128], [128, 4], [1, 1]]),
            allow_slow_non_contiguous=True),
            after=[c_tab])

        def do_regs_ss2(E_):
            E_.reg_load(r_t, s_dyn[0:1, 14:15])
        R.raw("gp", do_regs_ss2)
        ssgl = R.dma("qG", "ssg2", lambda E_: E_.dma_start(
            out=s_ssfl[:].unsqueeze(2),
            in_=bass.AP(rtab_h, r_t, [[1, 128], [128, 4], [1, 1]]),
            allow_slow_non_contiguous=True))
        scmb = R.op("dve", lambda E_: E_.scalar_tensor_tensor(
            out=s_ssf[:], in0=s_ssfh[:], scalar=128.0, in1=s_ssfl[:],
            op0=ALU.mult, op1=ALU.add), after=[ssgh, ssgl])
        cssi = R.op("dve", lambda E_: E_.tensor_copy(out=s_ssi[:], in_=s_ssf[:]),
                    after=[scmb])
        P4r.close()
        s_xg = [EE(nc.sbuf_tensor("s_xg%d" % m, [128, D], BF16)) for m in range(4)]
        s_xgT = [EE(nc.sbuf_tensor("s_xgT%d" % kk, [128, CAP], BF16)) for kk in range(16)]
        s_w1f = [EE(nc.sbuf_tensor("s_w1f%d" % s, [128, D], BF16)) for s in range(2)]
        s_h116 = EE(nc.sbuf_tensor("s_h116", [128, 64 * CAP], BF16))
        s_w2d = [EE(nc.sbuf_tensor("s_w2d%d" % s, [128, 32 * 128], BF16)) for s in range(2)]
        s_hf = EE(nc.sbuf_tensor("s_hf", [128, CAP], F32))
        s_hf16 = EE(nc.sbuf_tensor("s_hf16", [128, CAP], BF16))
        s_hT = [EE(nc.sbuf_tensor("s_hT%d" % qq, [128, D], BF16)) for qq in range(4)]
        g_pg = [Guard() for _ in range(2)]
        xgld = []
        for m in range(4):
            xgld.append(R.dma("qG", "xg%d" % m, lambda E_, m=m: E_.indirect_dma_start(
                out=s_xg[m][:], out_offset=None,
                in_=xffull_h.ap(), in_offset=IndirectOffsetOnAxis(s_ssi[:, m:m + 1], 0),
                bounds_check=NTOK - 1, oob_is_err=False),
                after=[cssi, c_xf]))
        for m in range(4):
            for kk in range(16):
                slot = (m * 16 + kk) % 2
                tr = R.op("pe", lambda E_, m=m, kk=kk, slot=slot: E_.transpose(
                    out=p_g16[slot][:], in_=s_xg[m][:, kk * 128:(kk + 1) * 128],
                    identity=s_id16[:]),
                    after=[xgld[m], idc] + g_pg[slot].w())
                tev = R.op("act", lambda E_, m=m, kk=kk, slot=slot: E_.activation(
                    out=s_xgT[kk][:, m * 128:(m + 1) * 128], in_=p_g16[slot][:], func=AF.Copy),
                    after=[tr])
                g_pg[slot].r(tev)
        xgT_done = ("act", R.cnt["act"])

        g_w1f = [Guard() for _ in range(2)]
        g_ph1 = [Guard() for _ in range(2)]
        for f in range(64):
            buf = f % 2
            wld = R.dma("qG", "w1f%d" % buf, lambda E_, f=f, buf=buf: E_.dma_start(
                out=s_w1f[buf][:], in_=bass.AP(w1T_h, f * 128, [[F, 128], [128 * F, 16], [1, 128]])),
                after=g_w1f[buf].w())
            mm = None
            for kk in range(16):
                aft = []
                if kk == 0:
                    aft = [xgT_done, wld] + g_ph1[buf].w()
                mm = R.op("pe", lambda E_, kk=kk, buf=buf: E_.matmul(
                    out=p_h1[buf][:], lhsT=s_w1f[buf][:, kk * 128:(kk + 1) * 128], rhs=s_xgT[kk][:],
                    start=(kk == 0), stop=(kk == 15)), after=aft)
            g_w1f[buf].r(mm)
            gel = R.op("act", lambda E_, f=f, buf=buf: E_.activation(
                out=s_h116[:, f * CAP:(f + 1) * CAP], in_=p_h1[buf][:], func=AF.Gelu,
                bias=s_b1r[:, f:f + 1]), after=[mm, b1l])
            g_ph1[buf].r(gel)
        h1_done = ("act", R.cnt["act"])


        g_w2d = [Guard() for _ in range(2)]
        g_ph2 = [Guard() for _ in range(2)]
        g_hf = Guard(); g_hf16 = Guard()
        g_hT = [Guard() for _ in range(4)]
        hstores = []
        for jd in range(16):
            buf = jd % 2
            mm = None
            for half in range(2):
                wbuf = (jd * 2 + half) % 2
                wld = R.dma("qG", "w2d%d" % wbuf, lambda E_, jd=jd, half=half, wbuf=wbuf: E_.dma_start(
                    out=s_w2d[wbuf][:],
                    in_=bass.AP(w2T_h, jd * 128 + half * 32 * 128 * D,
                                [[D, 128], [128 * D, 32], [1, 128]])),
                    after=g_w2d[wbuf].w())
                for fi in range(32):
                    ff = half * 32 + fi
                    aft = [wld]
                    if ff == 0:
                        aft = [h1_done, wld] + g_ph2[buf].w()
                    mm = R.op("pe", lambda E_, fi=fi, ff=ff, buf=buf, wbuf=wbuf: E_.matmul(
                        out=p_h2[buf][:], lhsT=s_w2d[wbuf][:, fi * 128:(fi + 1) * 128],
                        rhs=s_h116[:, ff * CAP:(ff + 1) * CAP],
                        start=(ff == 0), stop=(ff == 63)),
                        after=(aft if fi == 0 else []))
                g_w2d[wbuf].r(mm)
            hev = R.op("act", lambda E_, buf=buf: E_.activation(
                out=s_hf[:], in_=p_h2[buf][:], func=AF.Copy),
                after=[mm] + g_hf.w())
            g_ph2[buf].r(hev)
            hev = R.op("dve", lambda E_, jd=jd: E_.tensor_scalar(
                out=s_hf[:], in0=s_hf[:], scalar1=s_b2r[:, jd:jd + 1], scalar2=None,
                op0=ALU.add), after=[hev, b2l])
            hc16 = R.op("dve", lambda E_: E_.tensor_copy(out=s_hf16[:], in_=s_hf[:]),
                        after=[hev] + g_hf16.w())
            g_hf.r(hc16)
            for qq in range(4):
                slot = qq % 2
                tr = R.op("pe", lambda E_, qq=qq, slot=slot: E_.transpose(
                    out=p_g16[slot][:], in_=s_hf16[:, qq * 128:(qq + 1) * 128], identity=s_id16[:]),
                    after=[hc16] + g_pg[slot].w())
                g_hf16.r(tr)
                tev = R.op("act", lambda E_, qq=qq, jd=jd, slot=slot: E_.activation(
                    out=s_hT[qq][:, jd * 128:(jd + 1) * 128], in_=p_g16[slot][:], func=AF.Copy),
                    after=[tr] + (g_hT[qq].w() if jd == 0 else []))
                g_pg[slot].r(tev)
        hT_done = ("act", R.cnt["act"])
        for qq in range(4):
            hstores.append(R.dma("qS", "hst", lambda E_, qq=qq: E_.dma_start(
                out=hfino_h.ap()[qq * 128:(qq + 1) * 128, :], in_=s_hT[qq][:]), after=[hT_done]))
        p4_tail = R.all_cur()

    c_h = R.cc(lambda E_: E_.collective_compute(
        "AllGather", ALU.bypass, replica_groups=[list(range(8))],
        ins=[hfino_h.ap().opt()], outs=[hfull_h.ap().opt()]), after=p4_tail)

    NSB = NEXP * CAP // 128
    with ExitStack() as P:
        EE = P.enter_context
        s_st2 = EE(nc.sbuf_tensor("s_st2", [128, 2 * NSB], F32))
        s_st4 = EE(nc.sbuf_tensor("s_st4", [128, 3 * NSB], F32))
        s_cb5 = EE(nc.sbuf_tensor("s_cb5", [128, 1], I32))
        s_cbf = EE(nc.sbuf_tensor("s_cbf", [128, 1], F32))
        s_gidb = EE(nc.sbuf_tensor("s_gidb", [128, TOKPC], F32))
        s_MT = EE(nc.sbuf_tensor("s_MT", [128, NSB * 512], BF16))
        s_hfb = [EE(nc.sbuf_tensor("s_hfb%d" % s, [128, 1024], BF16)) for s in range(2)]
        s_ffn4 = EE(nc.sbuf_tensor("s_ffn4", [128, 4 * D], F32))
        s_xr5 = EE(nc.sbuf_tensor("s_xr5", [128, D], F32))
        s_sq5 = EE(nc.sbuf_tensor("s_sq5", [128, D], F32))
        s_oout = EE(nc.sbuf_tensor("s_oout", [128, D], F32))
        s_mu5 = EE(nc.sbuf_tensor("s_mu5", [128, 1], F32))
        s_mu25 = EE(nc.sbuf_tensor("s_mu25", [128, 1], F32))
        s_m25 = EE(nc.sbuf_tensor("s_m25", [128, 1], F32))
        s_sd5 = EE(nc.sbuf_tensor("s_sd5", [128, 1], F32))
        s_rstd5 = EE(nc.sbuf_tensor("s_rstd5", [128, 1], F32))
        p_acc = [EE(nc.psum_tensor("p_acc%d" % t, [128, 1024], F32)) for t in range(4)]

        stl = R.dma("qS", "stl", lambda E_: E_.dma_start(
            out=s_st4[:].rearrange("p (c r q) -> p c r q", r=3, q=4),
            in_=bass.AP(rtab_h, 0, [[1, 128], [1536, 8], [512, 3], [128, 4]]),
            allow_slow_non_contiguous=True), after=[c_tab])
        _v4 = s_st4[:].rearrange("p (c r q) -> p c r q", r=3, q=4)
        _hi = _v4[:, :, 0:1, :].squeeze(2)
        _lo = _v4[:, :, 1:2, :].squeeze(2)
        _wv = _v4[:, :, 2:3, :].squeeze(2)
        _s2 = s_st2[:].rearrange("p (c q c2) -> p c q c2", q=4, c2=2)
        _ido = _s2[:, :, :, 0:1].squeeze(3)
        _wo = _s2[:, :, :, 1:2].squeeze(3)
        cmb1 = R.op("dve", lambda E_: E_.scalar_tensor_tensor(
            out=_ido, in0=_hi, scalar=128.0, in1=_lo, op0=ALU.mult, op1=ALU.add),
            after=[stl])
        cmb2 = R.op("dve", lambda E_: E_.tensor_copy(out=_wo, in_=_wv))
        st2_done = cmb2
        cbl = R.dma("qG", "cbl", lambda E_: E_.dma_start(
            out=s_cb5[:], in_=dynoff[0:1, 8:9].to_broadcast([128, 1])))
        cbc = R.op("dve", lambda E_: E_.tensor_copy(out=s_cbf[:], in_=s_cb5[:]),
                   after=[cbl])
        gio = R.op("gp", lambda E_: E_.iota(s_gidb[:], [[1, TOKPC]], 0, 0,
                                            allow_small_or_imprecise_dtypes=True))
        gid = R.op("dve", lambda E_: E_.tensor_scalar(
            out=s_gidb[:], in0=s_gidb[:], scalar1=s_cbf[:, 0:1], scalar2=None,
            op0=ALU.add), after=[cbc, gio])

        g_MT = [Guard() for _ in range(NSB)]
        g_hfb = [Guard() for _ in range(2)]
        g_pacc = [Guard() for _ in range(4)]
        g_ffn = [Guard() for _ in range(4)]
        g_xr5 = Guard(); g_sq5 = Guard(); g_oout = Guard()
        ihf = 0
        laststore = None
        for quad in range(4):
            mtev = []
            for sb in range(NSB):
                ev = R.op("dve", lambda E_, sb=sb, quad=quad: E_.tensor_scalar(
                    out=s_MT[:, sb * 512:(sb + 1) * 512],
                    in0=s_gidb[:, quad * 512:(quad + 1) * 512],
                    scalar1=s_st2[:, 2 * sb:2 * sb + 1],
                    scalar2=s_st2[:, 2 * sb + 1:2 * sb + 2],
                    op0=ALU.is_equal, op1=ALU.mult),
                    after=[gid, st2_done] + g_MT[sb].w())
                mtev.append(ev)
            evq = [[None] * 4, [None] * 4]
            for dmH in range(2):
                lastmm = [None] * 4
                for sb in range(NSB):
                    buf = ihf % 2
                    ihf += 1
                    ld = R.dma("qS", "hf%d" % buf, lambda E_, sb=sb, dmH=dmH, buf=buf: E_.dma_start(
                        out=s_hfb[buf][:],
                        in_=hfull_h.ap()[sb * 128:(sb + 1) * 128,
                                         dmH * 1024:(dmH + 1) * 1024]),
                        after=[c_h] + g_hfb[buf].w())
                    mm = None
                    for tt in range(4):
                        for nh in range(2):
                            aft = []
                            if tt == 0 and nh == 0:
                                aft = [mtev[sb], ld]
                                if sb == 0:
                                    for t2 in range(4):
                                        aft += g_pacc[t2].w()
                            mm = R.op("pe", lambda E_, sb=sb, tt=tt, nh=nh, buf=buf: E_.matmul(
                                out=p_acc[tt][:, nh * 512:(nh + 1) * 512],
                                lhsT=s_MT[:, sb * 512 + tt * 128:sb * 512 + (tt + 1) * 128],
                                rhs=s_hfb[buf][:, nh * 512:(nh + 1) * 512],
                                start=(sb == 0), stop=(sb == NSB - 1)), after=aft)
                        lastmm[tt] = mm
                    g_hfb[buf].r(mm)
                    if dmH == 1:
                        g_MT[sb].r(mm)
                for tt in range(4):
                    ev = R.op("act", lambda E_, tt=tt, dmH=dmH: E_.activation(
                        out=s_ffn4[:, tt * D + dmH * 1024:tt * D + (dmH + 1) * 1024],
                        in_=p_acc[tt][:], func=AF.Copy),
                        after=[lastmm[tt]] + (g_ffn[tt].w() if dmH == 0 else []))
                    g_pacc[tt].r(ev)
                    evq[dmH][tt] = ev
            for tt in range(4):
                m_ = quad * 4 + tt
                sl = s_ffn4[:, tt * D:(tt + 1) * D]
                xl = R.dma("qS", "xr5", lambda E_, m_=m_: E_.dma_start(
                    out=s_xr5[:], in_=xf32o_h.ap()[m_ * 128:(m_ + 1) * 128, :]),
                    after=g_xr5.w())
                sv = R.op("dve", lambda E_, sl=sl: E_.tensor_tensor(
                    out=sl, in0=sl, in1=s_xr5[:], op=ALU.add),
                    after=[xl, evq[1][tt]])
                g_xr5.r(sv)
                mu = R.op("dve", lambda E_, sl=sl: E_.tensor_reduce(
                    out=s_mu5[:], in_=sl, axis=AX.X, op=ALU.add), after=[sv])
                mub = R.op("dve", lambda E_: E_.tensor_scalar(
                    out=s_mu5[:], in0=s_mu5[:], scalar1=1.0 / D, scalar2=None, op0=ALU.mult))
                sq = R.op("act", lambda E_, sl=sl: E_.activation(
                    out=s_sq5[:], in_=sl, func=AF.Square), after=[sv] + g_sq5.w())
                m2a = R.op("dve", lambda E_: E_.tensor_reduce(
                    out=s_m25[:], in_=s_sq5[:], axis=AX.X, op=ALU.add), after=[sq])
                g_sq5.r(m2a)
                m2b = R.op("dve", lambda E_: E_.tensor_scalar(
                    out=s_m25[:], in0=s_m25[:], scalar1=1.0 / D, scalar2=None, op0=ALU.mult))
                mu2v = R.op("dve", lambda E_: E_.tensor_scalar(
                    out=s_mu25[:], in0=s_mu5[:], scalar1=s_mu5[:, 0:1], scalar2=None,
                    op0=ALU.mult), after=[mub])
                vv = R.op("dve", lambda E_: E_.tensor_tensor(
                    out=s_m25[:], in0=s_m25[:], in1=s_mu25[:], op=ALU.subtract),
                    after=[m2b, mu2v])
                sdv = R.op("act", lambda E_: E_.activation(
                    out=s_sd5[:], in_=s_m25[:], func=AF.Sqrt, bias=s_eps[:, 0:1]), after=[vv])
                rst = R.op("dve", lambda E_: E_.reciprocal(out=s_rstd5[:], in_=s_sd5[:]),
                           after=[sdv])
                nmu = R.op("dve", lambda E_: E_.tensor_scalar(
                    out=s_mu5[:], in0=s_mu5[:], scalar1=-1.0, scalar2=None, op0=ALU.mult),
                    after=[mu2v])
                ov = R.op("dve", lambda E_, sl=sl: E_.tensor_scalar(
                    out=s_oout[:], in0=sl, scalar1=s_mu5[:, 0:1], scalar2=s_rstd5[:, 0:1],
                    op0=ALU.add, op1=ALU.mult), after=[nmu, rst] + g_oout.w())
                g_ffn[tt].r(ov)
                ost = R.dma("qS", "out", lambda E_, m_=m_: E_.dma_start(
                    out=outown[m_ * 128:(m_ + 1) * 128, :], in_=s_oout[:]), after=[ov])
                g_oout.r(ost)
                laststore = ost

    for _fam in list(R.famstream):
        _eng = "qS" if R.famstream[_fam] == "sync" else "gp"
        R.raw(_eng, lambda E_, s=_fam, v=R.cnt[_fam]: E_.wait_ge(R.semh[s], v))

    with nc.Block() as block:
        block.sync(lambda e: [f(e) for f in R.streams["sync"]])
        block.scalar(lambda e: [f(e) for f in R.streams["scalar"]])
        block.vector(lambda e: [f(e) for f in R.streams["vector"]])
        block.tensor(lambda e: [f(e) for f in R.streams["tensor"]])
        block.gpsimd(lambda e: [f(e) for f in R.streams["gpsimd"]])
    ctx.close()
    return nc


def _dynoff(c, h):
    d = np.zeros((1, 16), np.int32)
    for k in range(NCH):
        d[0, k] = k * T + h * 256
    for r_i in range(4):
        d[0, 8 + r_i] = r_i * NTOK + c * TOKPC
    d[0, 12] = c * CAP * 3
    d[0, 13] = c * CAP
    d[0, 14] = c * CAP * 3 + CAP
    return d


def make_inmaps(inputs):
    x = np.asarray(inputs["L0_attn_x"], np.float32)
    ipw = np.asarray(inputs["L0_attn_in_proj_w"], np.float32)
    cwf = np.asarray(inputs["L0_attn_conv_w"], np.float32)
    cbf = np.asarray(inputs["L0_attn_conv_b"], np.float32)
    xpw = np.asarray(inputs["L0_attn_x_proj_w"], np.float32)
    dtw = np.asarray(inputs["L0_attn_dt_proj_w"], np.float32)
    dtbf = np.asarray(inputs["L0_attn_dt_proj_b"], np.float32)
    alogf = np.asarray(inputs["L0_attn_A_log"], np.float32)
    dv = np.asarray(inputs["L0_attn_D"], np.float32)
    opw = np.asarray(inputs["L0_attn_out_proj_w"], np.float32)
    rw = np.asarray(inputs["L0_ffn_router_w"], np.float32)
    w1 = np.asarray(inputs["L0_ffn_w1"], np.float32)
    b1 = np.asarray(inputs["L0_ffn_b1"], np.float32)
    w2 = np.asarray(inputs["L0_ffn_w2"], np.float32)
    b2 = np.asarray(inputs["L0_ffn_b2"], np.float32)

    C = np.ascontiguousarray
    in_maps = []
    own_rows_all = []
    for c in range(8):
        b, h = c // 2, c % 2
        hs = slice(h * HC, (h + 1) * HC)
        rows = np.concatenate([np.arange(k * 512 + h * 256, k * 512 + h * 256 + 256)
                               for k in range(NCH)])
        own_rows_all.append(b * L + rows)
        pack = lambda a: C(a.reshape(NB, 128, -1).transpose(1, 0, 2).reshape(128, -1))
        m = {
            "xb": C(x[b]),
            "xown": C(x[b][rows]),
            "wuT": C(ipw[hs, :].T),
            "wzT": C(ipw[INNER + h * HC: INNER + (h + 1) * HC, :].T),
            "cw": pack(cwf[hs, 0, :]),
            "cb": pack(cbf[hs].reshape(-1, 1)),
            "xpT": C(xpw[:, hs].T),
            "dtT": C(dtw[hs, :].T),
            "dtb": pack(dtbf[hs].reshape(-1, 1)),
            "alog": pack(alogf[hs, :]),
            "dvec": pack(dv[hs].reshape(-1, 1)),
            "opT": C(opw.T),
            "rwT": C(rw.T),
            "w1T": C(w1.T),
            "b1r": C(b1.reshape(F // 128, 128).T),
            "w2T": C(w2.T),
            "b2r": C(b2.reshape(D // 128, 128).T),
            "ident": np.eye(128, dtype=np.float32),
            "sids": C((c * CAP + np.arange(CAP, dtype=np.int32)).reshape(4, 128).T),
            "hoff": np.array([[h * 256]], np.int32),
            "dynoff": _dynoff(c, h),
            "coff": np.array([[c * TOKPC]], np.int32),
            "epst": np.full((128, 1), 1e-5, np.float32),
        }
        in_maps.append(m)
    return in_maps, own_rows_all


def kernel(**inputs):
    in_maps, own_rows_all = make_inmaps(inputs)
    nc = build_program()
    res = run_bass_kernel_spmd(nc, in_maps, core_ids=list(range(8)))
    out = np.empty((B, L, D), np.float32)
    flat = out.reshape(NTOK, D)
    for c in range(8):
        flat[own_rows_all[c]] = res.results[c]["outown"]
    return out


# version 11
# speedup vs baseline: 1.2263x; 1.0090x over previous
import numpy as np
import ml_dtypes
import concourse.bass as bass
import concourse.mybir as mybir
from concourse.bass_utils import run_bass_kernel_spmd
from concourse.bass_core_rust import IndirectOffsetOnAxis

F32 = mybir.dt.float32
BF16 = mybir.dt.bfloat16
I32 = mybir.dt.int32
U32 = mybir.dt.uint32
AF = mybir.ActivationFunctionType
ALU = mybir.AluOpType
AX = mybir.AxisListType

B, L, D = 4, 4096, 2048
INNER, HC, NB = 4096, 2048, 16
NST, DTR = 16, 128
T = 512
NCH = L // T
E, CAP, F = 8, 512, 8192
NEXP = 8
NTOK = B * L
TOKPC = NTOK // 8
TBL = E * CAP + 128
OOBV = 100000.0

ENG = {"pe": "tensor", "act": "scalar", "dve": "vector", "gp": "gpsimd", "qS": "sync",
       "qG": "gpsimd", "qA": "scalar"}


class Rec:
    def __init__(self, nc):
        self.nc = nc
        self.streams = {k: [] for k in ["sync", "scalar", "vector", "tensor", "gpsimd"]}
        self.cnt = {"pe": 0, "act": 0, "dve": 0, "gp": 0, "qS": 0, "qG": 0}
        self.semh = {}
        self.ccn = 0
        self._ctx = None
        self.famstream = {}
        self.hwm = {k: {} for k in self.streams}

    def _filt(self, stream, aft):
        best = {}
        for (s, v) in aft:
            if v > best.get(s, 0):
                best[s] = v
        h = self.hwm[stream]
        out = []
        for s, v in best.items():
            if v > h.get(s, 0):
                h[s] = v
                out.append((s, v))
        return out

    def init_sems(self, ctx):
        self._ctx = ctx
        for name in ["pe", "act", "dve", "gp", "qS", "qG"]:
            self.semh[name] = ctx.enter_context(self.nc.semaphore("sem_" + name))

    def op(self, eng, fn, after=()):
        self.cnt[eng] += 1
        tgt = self.cnt[eng]
        aft = [x for x in after if x is not None]
        if eng in ("act", "dve", "gp"):
            aft.append((eng, tgt - 1))
        aft = self._filt(ENG[eng], aft)
        sem = self.semh

        def emit(E, aft=tuple(aft), fn=fn, eng=eng):
            for (s, v) in aft:
                if v > 0:
                    E.wait_ge(sem[s], v)
            fn(E).then_inc(sem[eng], 1)
        self.streams[ENG[eng]].append(emit)
        return (eng, tgt)

    def raw(self, eng, fn, after=()):
        aft = self._filt(ENG[eng], [x for x in after if x is not None])
        sem = self.semh

        def emit(E, aft=tuple(aft), fn=fn):
            for (s, v) in aft:
                if v > 0:
                    E.wait_ge(sem[s], v)
            fn(E)
        self.streams[ENG[eng]].append(emit)

    def dma(self, q, fam, fn, after=()):
        if fam not in self.cnt:
            self.cnt[fam] = 0
            self.semh[fam] = self._ctx.enter_context(self.nc.semaphore("sem_" + fam))
            self.famstream[fam] = ENG[q]
        assert self.famstream[fam] == ENG[q], fam
        self.cnt[fam] += 16
        tgt = self.cnt[fam]
        aft = self._filt(ENG[q], [x for x in after if x is not None])
        sem = self.semh

        def emit(E, aft=tuple(aft), fn=fn, fam=fam):
            for (s, v) in aft:
                if v > 0:
                    E.wait_ge(sem[s], v)
            fn(E).then_inc(sem[fam], 16)
        self.streams[ENG[q]].append(emit)
        return (fam, tgt)

    def cc(self, fn, after=()):
        name = "cc%d" % self.ccn
        self.ccn += 1
        self.semh[name] = self._ctx.enter_context(self.nc.semaphore("sem_" + name))
        aft = self._filt("gpsimd", [x for x in after if x is not None])
        sem = self.semh

        def emit(E, aft=tuple(aft), fn=fn, name=name):
            for (s, v) in aft:
                if v > 0:
                    E.wait_ge(sem[s], v)
            fn(E).then_inc(sem[name])
        self.streams["gpsimd"].append(emit)
        return (name, 1)

    def all_cur(self):
        return [(e, v) for (e, v) in self.cnt.items() if v > 0]


class Guard:
    def __init__(self):
        self.evs = []

    def r(self, ev):
        if ev is not None:
            self.evs.append(ev)
        return ev

    def w(self):
        evs, self.evs = self.evs, []
        return evs


def build_program():
    nc = bass.Bass("TRN2", target_bir_lowering=False, dynamic_dma_scratch_size=65536)
    dt = nc.dram_tensor

    xb_h = dt("xb", [L, D], F32, kind="ExternalInput")
    wuT_h = dt("wuT", [D, HC], F32, kind="ExternalInput")
    wzT_h = dt("wzT", [D, HC], F32, kind="ExternalInput")
    cw = dt("cw", [128, NB * 4], F32, kind="ExternalInput").ap()
    cb = dt("cb", [128, NB], F32, kind="ExternalInput").ap()
    xpT_h = dt("xpT", [HC, 160], F32, kind="ExternalInput")
    dtT_h = dt("dtT", [DTR, HC], F32, kind="ExternalInput")
    dtb = dt("dtb", [128, NB], F32, kind="ExternalInput").ap()
    alog = dt("alog", [128, NB * NST], F32, kind="ExternalInput").ap()
    dvec = dt("dvec", [128, NB], F32, kind="ExternalInput").ap()
    opT_h = dt("opT", [INNER, D], F32, kind="ExternalInput")
    rwT_h = dt("rwT", [D, E], F32, kind="ExternalInput")
    w1T_h = dt("w1T", [D, F], F32, kind="ExternalInput")
    b1r = dt("b1r", [128, F // 128], F32, kind="ExternalInput").ap()
    w2T_h = dt("w2T", [F, D], F32, kind="ExternalInput")
    b2r = dt("b2r", [128, D // 128], F32, kind="ExternalInput").ap()
    ident = dt("ident", [128, 128], F32, kind="ExternalInput").ap()
    xown_h = dt("xown", [TOKPC, D], F32, kind="ExternalInput")
    sids = dt("sids", [128, 4], I32, kind="ExternalInput").ap()
    hoff = dt("hoff", [1, 1], I32, kind="ExternalInput").ap()
    coff = dt("coff", [1, 1], I32, kind="ExternalInput").ap()
    epst = dt("epst", [128, 1], F32, kind="ExternalInput").ap()
    dynoff = dt("dynoff", [1, 16], I32, kind="ExternalInput").ap()

    outown = dt("outown", [TOKPC, D], F32, kind="ExternalOutput").ap()

    u16d_h = dt("u16d", [HC, L], BF16)
    z16d_h = dt("z16d", [HC, L], BF16)
    xdblp_h = dt("xdblp", [160, L], F32)
    xdblr_h = dt("xdblr", [160, L], F32)
    bflat_h = dt("bflat", [NST, T], F32)
    cflat_h = dt("cflat", [NST, T], F32)
    ys16d_h = dt("ys16d", [HC, L], BF16)
    ysf_h = dt("ysf", [INNER, L], BF16)
    opT16_h = dt("opT16", [INNER, D], BF16)
    xf32o_h = dt("xf32o", [TOKPC, D], F32)
    xf16o_h = dt("xf16o", [TOKPC, D], BF16)
    xffull_h = dt("xffull", [NTOK, D], BF16)
    top2o_h = dt("top2o", [TOKPC, 4], F32)
    top2full_h = dt("top2full", [NTOK, 4], F32)
    gtscr_h = dt("gtscr", [1, E], F32)
    ptab_h = dt("ptab", [3, CAP], F32)
    rtab_h = dt("rtab", [E * 3, CAP], F32)
    hfino_h = dt("hfino", [CAP, D], BF16)
    hfull_h = dt("hfull", [E * CAP, D], BF16)

    from contextlib import ExitStack
    ctx = ExitStack()
    R = Rec(nc)
    R.init_sems(ctx)

    s_id16 = ctx.enter_context(nc.sbuf_tensor("s_id16", [128, 128], BF16))
    s_cw = ctx.enter_context(nc.sbuf_tensor("s_cw", [128, NB * 4], F32))
    s_cb = ctx.enter_context(nc.sbuf_tensor("s_cb", [128, NB], F32))
    s_dtb = ctx.enter_context(nc.sbuf_tensor("s_dtb", [128, NB], F32))
    s_negA = ctx.enter_context(nc.sbuf_tensor("s_negA", [128, NB * NST], F32))
    s_dvec = ctx.enter_context(nc.sbuf_tensor("s_dvec", [128, NB], F32))
    s_eps = ctx.enter_context(nc.sbuf_tensor("s_eps", [128, 1], F32))
    s_dyn = ctx.enter_context(nc.sbuf_tensor("s_dyn", [1, 16], I32))
    s_ones = ctx.enter_context(nc.sbuf_tensor("s_ones", [128, 1], F32))

    R.dma("qS", "pre", lambda E: E.dma_start(out=s_cw[:], in_=cw))
    R.dma("qS", "pre", lambda E: E.dma_start(out=s_cb[:], in_=cb))
    R.dma("qS", "pre", lambda E: E.dma_start(out=s_dtb[:], in_=dtb))
    R.dma("qS", "pre", lambda E: E.dma_start(out=s_negA[:], in_=alog))
    R.dma("qS", "pre", lambda E: E.dma_start(out=s_dvec[:], in_=dvec))
    R.dma("qS", "pre", lambda E: E.dma_start(out=s_eps[:], in_=epst))
    lN = R.dma("qS", "pre", lambda E: E.dma_start(out=s_dyn[:], in_=dynoff))
    idld = R.dma("qG", "idg", lambda E: E.dma_start(out=s_id16[:], in_=ident))
    idc = idld
    e1 = R.op("act", lambda E: E.activation(out=s_negA[:], in_=s_negA[:], func=AF.Exp),
               after=[lN])
    negA_done = R.op("dve", lambda E: E.tensor_scalar(out=s_negA[:], in0=s_negA[:], scalar1=-1.0,
                                                      scalar2=None, op0=ALU.mult), after=[e1])
    ones1 = R.op("dve", lambda E: E.memset(s_ones[:], 1.0))
    r_t = nc.gpsimd.alloc_register("r_t")
    for c8 in range(8):
        opc = R.dma("qG", "opc", lambda E, c8=c8: E.dma_start(
            out=opT16_h.ap()[c8 * 512:(c8 + 1) * 512, :],
            in_=opT_h.ap()[c8 * 512:(c8 + 1) * 512, :]))
    opc_done = opc

    with ExitStack() as P:
        EE = P.enter_context
        s_wuT = [EE(nc.sbuf_tensor("s_wuT%d" % k, [128, HC], BF16)) for k in range(16)]
        s_wzT = [EE(nc.sbuf_tensor("s_wzT%d" % k, [128, HC], BF16)) for k in range(16)]
        s_xpT = [EE(nc.sbuf_tensor("s_xpT%d" % i, [128, 160], BF16)) for i in range(NB)]
        s_xc = [EE(nc.sbuf_tensor("s_xc%d" % q, [128, D], BF16)) for q in range(1)]
        s_xT = [EE(nc.sbuf_tensor("s_xTT%d" % k, [128, T], BF16)) for k in range(16)]
        s_upre = EE(nc.sbuf_tensor("s_upre", [128, T + 3], BF16))
        s_halo = EE(nc.sbuf_tensor("s_halo", [128, 3 * NB], BF16))
        s_cvt = EE(nc.sbuf_tensor("s_cvt", [128, T], F32))
        s_u16t = EE(nc.sbuf_tensor("s_u16t", [128, T], BF16))
        s_z16t = s_u16t
        p_tr = [EE(nc.psum_tensor("p_tr%d" % s, [128, 128], BF16)) for s in range(2)]
        p_mm = [EE(nc.psum_tensor("p_mm%d" % s, [128, T], F32)) for s in range(2)]
        p_dt = EE(nc.psum_tensor("p_dt", [128, T], F32))
        p_bc = EE(nc.psum_tensor("p_bc", [32, T], F32))

        for k in range(16):
            R.dma("qG", "wp1", lambda E, k=k: E.dma_start(out=s_wuT[k][:], in_=wuT_h.ap()[k * 128:(k + 1) * 128, :]))
            R.dma("qG", "wp1", lambda E, k=k: E.dma_start(out=s_wzT[k][:], in_=wzT_h.ap()[k * 128:(k + 1) * 128, :]))
        for i in range(NB):
            wl = R.dma("qG", "wp1", lambda E, i=i: E.dma_start(out=s_xpT[i][:], in_=xpT_h.ap()[i * 128:(i + 1) * 128, :]))
        w_done = wl
        halo0 = R.op("dve", lambda E: E.memset(s_halo[:], 0.0))

        g_xc = [Guard() for _ in range(1)]
        g_xT = [Guard() for _ in range(16)]
        g_tr = [Guard() for _ in range(2)]
        g_mm = [Guard() for _ in range(2)]
        g_u16t = Guard(); g_z16t = g_u16t
        g_upre = Guard(); g_halo = Guard()
        g_dtp = Guard(); g_cvt = Guard()

        for k in range(NCH):
            wEs = [g_xT[i].w() for i in range(16)]
            for q in range(4):
                xld = R.dma("qG", "xld", lambda E, q=q, k=k: E.dma_start(
                    out=s_xc[0][:], in_=xb_h.ap()[k * T + q * 128: k * T + (q + 1) * 128, :]),
                    after=g_xc[0].w())
                for i in range(16):
                    slot = i % 2
                    tr = R.op("pe", lambda E, i=i, q=q, slot=slot: E.transpose(
                        out=p_tr[slot][:], in_=s_xc[0][:, i * 128:(i + 1) * 128],
                        identity=s_id16[:]),
                        after=[xld, idc] + g_tr[slot].w()
                        + (wEs[i] if q == 0 else []))
                    g_xc[0].r(tr)
                    ev = R.op("act", lambda E, i=i, q=q, slot=slot: E.activation(
                        out=s_xT[i][:, q * 128:(q + 1) * 128], in_=p_tr[slot][:], func=AF.Copy),
                        after=[tr])
                    g_tr[slot].r(ev)
            xT_all = ("act", R.cnt["act"])

            for j in range(32):
                isu = j < 16
                i = j if isu else j - 16
                wt = s_wuT if isu else s_wzT
                bank = j % 2
                mm = None
                for kk in range(16):
                    aft = []
                    if kk == 0:
                        aft = [xT_all, w_done] + g_mm[bank].w()
                        if j == 0 and k > 0:
                            aft += g_dtp.w()
                    mm = R.op("pe", lambda E, kk=kk, i=i, wt=wt, bank=bank: E.matmul(
                        out=p_mm[bank][:], lhsT=wt[kk][:, i * 128:(i + 1) * 128], rhs=s_xT[kk][:],
                        start=(kk == 0), stop=(kk == 15)), after=aft)
                for kk in range(16):
                    g_xT[kk].r(mm)
                if isu:
                    ev = R.op("act", lambda E, i=i, bank=bank: E.activation(
                        out=s_upre[:, 3:T + 3], in_=p_mm[bank][:], func=AF.Copy),
                        after=[mm] + g_upre.w())
                    g_mm[bank].r(ev)
                    hcp = R.op("dve", lambda E, i=i: E.tensor_copy(
                        out=s_upre[:, 0:3], in_=s_halo[:, 3 * i:3 * i + 3]),
                        after=[halo0] + g_halo.w())
                    c1 = R.op("dve", lambda E, i=i: E.tensor_scalar(
                        out=s_cvt[:], in0=s_upre[:, 0:T], scalar1=s_cw[:, 4 * i:4 * i + 1],
                        scalar2=None, op0=ALU.mult), after=[ev, hcp] + g_cvt.w())
                    for kk in range(1, 4):
                        c1 = R.op("dve", lambda E, i=i, kk=kk: E.scalar_tensor_tensor(
                            out=s_cvt[:], in0=s_upre[:, kk:kk + T],
                            scalar=s_cw[:, 4 * i + kk:4 * i + kk + 1],
                            in1=s_cvt[:], op0=ALU.mult, op1=ALU.add), after=[c1])
                    hal = R.op("dve", lambda E, i=i: E.tensor_copy(
                        out=s_halo[:, 3 * i:3 * i + 3], in_=s_upre[:, T:T + 3]), after=[c1])
                    g_upre.r(hal)
                    g_halo.r(hal)
                    sil = R.op("act", lambda E, i=i: E.activation(
                        out=s_u16t[:], in_=s_cvt[:], func=AF.Silu, bias=s_cb[:, i:i + 1]),
                        after=[c1] + g_u16t.w())
                    g_cvt.r(sil)
                    st_u = R.dma("qS", "stuz", lambda E, i=i, k=k: E.dma_start(
                        out=u16d_h.ap()[i * 128:(i + 1) * 128, k * T:(k + 1) * T], in_=s_u16t[:]),
                        after=[sil])
                    m1 = R.op("pe", lambda E, i=i: E.matmul(
                        out=p_dt[:], lhsT=s_xpT[i][:, 0:128], rhs=s_u16t[:],
                        start=(i == 0), stop=(i == 15)), after=[sil])
                    m2 = R.op("pe", lambda E, i=i: E.matmul(
                        out=p_bc[:], lhsT=s_xpT[i][:, 128:160], rhs=s_u16t[:],
                        start=(i == 0), stop=(i == 15)), after=[sil])
                    g_u16t.r(st_u); g_u16t.r(m1); g_u16t.r(m2)
                else:
                    ev = R.op("act", lambda E, bank=bank: E.activation(
                        out=s_z16t[:], in_=p_mm[bank][:], func=AF.Copy),
                        after=[mm] + g_z16t.w())
                    g_mm[bank].r(ev)
                    st_z = R.dma("qS", "stuz", lambda E, i=i, k=k: E.dma_start(
                        out=z16d_h.ap()[i * 128:(i + 1) * 128, k * T:(k + 1) * T], in_=s_z16t[:]),
                        after=[ev])
                    g_z16t.r(st_z)

            dtev = R.op("dve", lambda E: E.tensor_copy(out=s_cvt[:], in_=p_dt[:]),
                        after=[m1] + g_cvt.w())
            g_dtp.r(dtev)
            st1 = R.dma("qS", "cvt", lambda E, k=k: E.dma_start(
                out=xdblp_h.ap()[0:128, k * T:(k + 1) * T], in_=s_cvt[:]), after=[dtev])
            g_cvt.r(st1)
            bcev = R.op("dve", lambda E: E.tensor_copy(out=s_cvt[0:32, :], in_=p_bc[:]),
                        after=[m2, st1] + g_cvt.w())
            g_dtp.r(bcev)
            st2 = R.dma("qS", "cvt", lambda E, k=k: E.dma_start(
                out=xdblp_h.ap()[128:160, k * T:(k + 1) * T], in_=s_cvt[0:32, :]), after=[bcev])
            g_cvt.r(st2)

        p1_tail = R.all_cur()

    c_xdbl = R.cc(lambda E: E.collective_compute(
        "AllReduce", ALU.add, replica_groups=[[0, 1], [2, 3], [4, 5], [6, 7]],
        ins=[xdblp_h.ap().opt()], outs=[xdblr_h.ap().opt()]), after=p1_tail)

    with ExitStack() as P:
        EE = P.enter_context
        s_dtT = EE(nc.sbuf_tensor("s_dtT", [128, HC], BF16))
        s_carry = EE(nc.sbuf_tensor("s_carry", [128, NB * NST], F32))
        s_dt16 = EE(nc.sbuf_tensor("s_dt16", [128, T], BF16))
        s_Brep = EE(nc.sbuf_tensor("s_Brep", [128, NST * T], BF16))
        s_Crep = EE(nc.sbuf_tensor("s_Crep", [128, NST * T], BF16))
        s_bcfl = EE(nc.sbuf_tensor("s_bcfl", [33, 2048], BF16))
        s_on1 = EE(nc.sbuf_tensor("s_on1", [33, 128], BF16))
        s_dA = EE(nc.sbuf_tensor("s_dA", [128, NST * T], F32))
        s_bt = EE(nc.sbuf_tensor("s_bt", [128, NST * T], BF16))
        s_h = EE(nc.sbuf_tensor("s_h", [128, NST * T], BF16))
        s_del = EE(nc.sbuf_tensor("s_del", [128, T], F32))
        s_t1 = EE(nc.sbuf_tensor("s_t1", [128, T], F32))
        s_u16 = EE(nc.sbuf_tensor("s_u16", [128, T], BF16))
        s_du16 = EE(nc.sbuf_tensor("s_du16", [128, T], BF16))
        s_z16 = EE(nc.sbuf_tensor("s_z16", [128, T], BF16))
        s_y = EE(nc.sbuf_tensor("s_y", [128, T], F32))
        s_sz = EE(nc.sbuf_tensor("s_sz", [128, T], BF16))
        s_ys16 = EE(nc.sbuf_tensor("s_ys16", [128, T], BF16))
        p_d = EE(nc.psum_tensor("p_d", [128, T], F32))
        p_rep = [EE(nc.psum_tensor("p_rep%d" % s, [128, 1024], F32)) for s in range(2)]

        dtT_ld = R.dma("qG", "dtt", lambda E: E.dma_start(out=s_dtT[:], in_=dtT_h.ap()), after=[c_xdbl])
        on1v = R.op("dve", lambda E: E.memset(s_on1[:], 1.0))
        carry0 = R.op("dve", lambda E: E.memset(s_carry[:], 0.0))

        g_dt16 = Guard()
        g_Brep = Guard(); g_Crep = Guard(); g_bflat = Guard(); g_cflat = Guard()
        g_pd = Guard(); g_t1 = Guard(); g_del = Guard()
        g_u16 = Guard(); g_z16 = Guard(); g_dA = Guard()
        g_bt = Guard(); g_h = Guard(); g_y = Guard(); g_sz = Guard(); g_ys = Guard()
        g_du16 = Guard()

        for k in range(NCH):
            dtl = R.dma("qG", "dtl", lambda E, k=k: E.dma_start(
                out=s_dt16[:], in_=xdblr_h.ap()[0:128, k * T:(k + 1) * T]),
                after=[c_xdbl] + g_dt16.w())
            cast_dt = dtl
            bfd = R.dma("qS", "bfd", lambda E, k=k: E.dma_start(
                out=bflat_h.ap(), in_=xdblr_h.ap()[128:144, k * T:(k + 1) * T]),
                after=[c_xdbl] + g_bflat.w())
            cfd = R.dma("qS", "cfd", lambda E, k=k: E.dma_start(
                out=cflat_h.ap(), in_=xdblr_h.ap()[144:160, k * T:(k + 1) * T]),
                after=[c_xdbl] + g_cflat.w())
            g_prep = getattr(R, "_g_prep", None)
            if g_prep is None:
                g_prep = R._g_prep = [Guard(), Guard()]
                R._g_bcfl = [Guard(), Guard()]
            g_bcfl = R._g_bcfl
            brl = crl = None
            for rep_i, (row0, s_rep, flat_ap, dep) in enumerate(
                    [(0, s_Brep, bflat_h, bfd), (32, s_Crep, cflat_h, cfd)]):
                last = None
                for seg in range(NST * T // 1024):
                    par = seg % 2
                    fl_ld = R.dma("qG", "fl%d" % par, lambda E, row0=row0, seg=seg, par=par, flat_ap=flat_ap: E.dma_start(
                        out=s_bcfl[row0:row0 + 1, par * 1024:(par + 1) * 1024],
                        in_=flat_ap.ap().rearrange("n t -> (n t)").unsqueeze(0)
                        [:, seg * 1024:(seg + 1) * 1024]),
                        after=[dep] + g_bcfl[par].w())
                    mmr0 = R.op("pe", lambda E, row0=row0, par=par: E.matmul(
                        out=p_rep[par][:, 0:512], lhsT=s_on1[row0:row0 + 1, :],
                        rhs=s_bcfl[row0:row0 + 1, par * 1024:par * 1024 + 512],
                        start=True, stop=True),
                        after=[fl_ld, on1v] + g_prep[par].w())
                    mmr = R.op("pe", lambda E, row0=row0, par=par: E.matmul(
                        out=p_rep[par][:, 512:1024], lhsT=s_on1[row0:row0 + 1, :],
                        rhs=s_bcfl[row0:row0 + 1, par * 1024 + 512:(par + 1) * 1024],
                        start=True, stop=True))
                    g_bcfl[par].r(mmr)
                    evr = R.op("act", lambda E, s_rep=s_rep, seg=seg, par=par: E.activation(
                        out=s_rep[:, seg * 1024:(seg + 1) * 1024], in_=p_rep[par][:],
                        func=AF.Copy), after=[mmr])
                    g_prep[par].r(evr)
                    last = evr
                if rep_i == 0:
                    brl = last
                    g_bflat.r(fl_ld)
                else:
                    crl = last
                    g_cflat.r(fl_ld)

            for i in range(NB):
                mmd = R.op("pe", lambda E, i=i: E.matmul(
                    out=p_d[:], lhsT=s_dtT[:, i * 128:(i + 1) * 128], rhs=s_dt16[:],
                    start=True, stop=True),
                    after=[cast_dt, dtT_ld] + g_pd.w())
                g_dt16.r(mmd)
                a1 = R.op("act", lambda E, i=i: E.activation(
                    out=s_t1[:], in_=p_d[:], func=AF.Abs, bias=s_dtb[:, i:i + 1]),
                    after=[mmd] + g_t1.w())
                a2 = R.op("act", lambda E: E.activation(out=s_t1[:], in_=s_t1[:], func=AF.Exp, scale=-1.0))
                a3 = R.op("act", lambda E: E.activation(out=s_t1[:], in_=s_t1[:], func=AF.Ln, bias=1.0))
                a4 = R.op("dve", lambda E, i=i: E.tensor_scalar(
                    out=s_del[:], in0=p_d[:], scalar1=s_dtb[:, i:i + 1], scalar2=0.0,
                    op0=ALU.add, op1=ALU.max),
                    after=[mmd] + g_del.w())
                g_pd.r(a1); g_pd.r(a4)
                dl = R.op("dve", lambda E: E.tensor_tensor(out=s_del[:], in0=s_del[:], in1=s_t1[:],
                                                           op=ALU.add), after=[a3, a4])
                g_t1.r(dl)
                ul = R.dma("qS", "ul", lambda E, i=i, k=k: E.dma_start(
                    out=s_u16[:], in_=u16d_h.ap()[i * 128:(i + 1) * 128, k * T:(k + 1) * T]),
                    after=g_u16.w())
                du = R.op("dve", lambda E: E.tensor_tensor(out=s_du16[:], in0=s_del[:], in1=s_u16[:],
                                                           op=ALU.mult),
                          after=[dl, ul] + g_du16.w())
                exps = []
                dAw = g_dA.w()
                for n in range(NST):
                    exps.append(R.op("act", lambda E, i=i, n=n: E.activation(
                        out=s_dA[:, n * T:(n + 1) * T], in_=s_del[:], func=AF.Exp,
                        scale=s_negA[:, i * NST + n: i * NST + n + 1]),
                        after=[dl, negA_done] + (dAw if n == 0 else [])))
                    g_del.r(exps[n])
                du_bc = s_du16[:].unsqueeze(1).broadcast_to([128, NST, T])
                bt = R.op("dve", lambda E, du_bc=du_bc: E.tensor_tensor(
                    out=s_bt.rearrange("p (n t) -> p n t", n=NST), in0=du_bc,
                    in1=s_Brep.rearrange("p (n t) -> p n t", n=NST), op=ALU.mult),
                    after=[du, brl] + g_bt.w())
                g_du16.r(bt); g_Brep.r(bt)
                sc = bt
                hw = g_h.w()
                for n in range(NST):
                    sc = R.op("dve", lambda E, i=i, n=n: E.tensor_tensor_scan(
                        out=s_h[:, n * T:(n + 1) * T], data0=s_dA[:, n * T:(n + 1) * T],
                        data1=s_bt[:, n * T:(n + 1) * T],
                        initial=s_carry[:, i * NST + n: i * NST + n + 1],
                        op0=ALU.mult, op1=ALU.add),
                        after=[exps[n], carry0, bt] + (hw if n == 0 else []))
                    g_dA.r(sc)
                cu = R.op("dve", lambda E, i=i: E.tensor_copy(
                    out=s_carry[:, i * NST:(i + 1) * NST],
                    in_=s_h[:].rearrange("p (n t) -> p n t", n=NST)[:, :, T - 1:T].squeeze(2)),
                    after=[sc])
                hc = R.op("dve", lambda E: E.tensor_tensor(out=s_bt[:], in0=s_h[:], in1=s_Crep[:],
                                                           op=ALU.mult), after=[cu, crl])
                g_h.r(hc); g_Crep.r(hc)
                tr1 = R.op("dve", lambda E: E.tensor_tensor(
                    out=s_bt[:, 0:8 * T], in0=s_bt[:, 0:8 * T], in1=s_bt[:, 8 * T:16 * T],
                    op=ALU.add), after=[hc])
                tr2 = R.op("dve", lambda E: E.tensor_tensor(
                    out=s_bt[:, 0:4 * T], in0=s_bt[:, 0:4 * T], in1=s_bt[:, 4 * T:8 * T],
                    op=ALU.add))
                tr3 = R.op("dve", lambda E: E.tensor_tensor(
                    out=s_bt[:, 0:2 * T], in0=s_bt[:, 0:2 * T], in1=s_bt[:, 2 * T:4 * T],
                    op=ALU.add))
                yr = R.op("dve", lambda E: E.tensor_tensor(
                    out=s_y[:], in0=s_bt[:, 0:T], in1=s_bt[:, T:2 * T],
                    op=ALU.add), after=g_y.w())
                g_bt.r(yr)
                zl = R.dma("qS", "zl", lambda E, i=i, k=k: E.dma_start(
                    out=s_z16[:], in_=z16d_h.ap()[i * 128:(i + 1) * 128, k * T:(k + 1) * T]),
                    after=g_z16.w())
                szv = R.op("act", lambda E: E.activation(out=s_sz[:], in_=s_z16[:], func=AF.Silu),
                           after=[zl] + g_sz.w())
                g_z16.r(szv)
                y2 = R.op("dve", lambda E, i=i: E.scalar_tensor_tensor(
                    out=s_y[:], in0=s_u16[:], scalar=s_dvec[:, i:i + 1], in1=s_y[:],
                    op0=ALU.mult, op1=ALU.add), after=[yr])
                g_u16.r(y2)
                ysv = R.op("dve", lambda E: E.tensor_tensor(out=s_ys16[:], in0=s_y[:], in1=s_sz[:],
                                                            op=ALU.mult), after=[y2, szv] + g_ys.w())
                g_y.r(ysv); g_sz.r(ysv)
                yst = R.dma("qS", "yst", lambda E, i=i, k=k: E.dma_start(
                    out=ys16d_h.ap()[i * 128:(i + 1) * 128, k * T:(k + 1) * T], in_=s_ys16[:]),
                    after=[ysv])
                g_ys.r(yst)
        p2_tail = R.all_cur()

    c_ys = R.cc(lambda E: E.collective_compute(
        "AllGather", ALU.bypass, replica_groups=[[0, 1], [2, 3], [4, 5], [6, 7]],
        ins=[ys16d_h.ap().opt()], outs=[ysf_h.ap().opt()]), after=p2_tail)

    with ExitStack() as P:
        EE = P.enter_context
        s_opk = [EE(nc.sbuf_tensor("s_opk%d" % s, [128, D], BF16)) for s in range(2)]
        s_id32_3 = EE(nc.sbuf_tensor("s_id32p3", [128, 128], F32))
        s_ysall = EE(nc.sbuf_tensor("s_ysall", [128, 32 * 256], BF16))
        s_sT = [EE(nc.sbuf_tensor("s_sTq%d" % q, [128, D], F32)) for q in range(2)]
        s_xrow = EE(nc.sbuf_tensor("s_xrow", [128, D], F32))
        s_mu = EE(nc.sbuf_tensor("s_mu", [128, 1], F32))
        s_mu2 = EE(nc.sbuf_tensor("s_mu2", [128, 1], F32))
        s_m2 = EE(nc.sbuf_tensor("s_m2", [128, 1], F32))
        s_sd = EE(nc.sbuf_tensor("s_sd", [128, 1], F32))
        s_rstd = EE(nc.sbuf_tensor("s_rstd", [128, 1], F32))
        s_xf = EE(nc.sbuf_tensor("s_xf", [128, D], F32))
        s_sq = EE(nc.sbuf_tensor("s_sq", [128, D], F32))
        s_xf16 = EE(nc.sbuf_tensor("s_xf16", [128, D], BF16))
        s_xft = [EE(nc.sbuf_tensor("s_xft%d" % kk, [128, 128], F32)) for kk in range(16)]
        s_rwT = EE(nc.sbuf_tensor("s_rwT", [128, 16 * NEXP], F32))
        s_lg = EE(nc.sbuf_tensor("s_lg", [8, 128], F32))
        s_lgt = EE(nc.sbuf_tensor("s_lgt", [128, 8], F32))
        s_mx = EE(nc.sbuf_tensor("s_mx", [128, 8], F32))
        s_mi = EE(nc.sbuf_tensor("s_mi", [128, 8], U32))
        s_nm = EE(nc.sbuf_tensor("s_nm", [128, 1], F32))
        s_ex = EE(nc.sbuf_tensor("s_ex", [128, 8], F32))
        s_ssum = EE(nc.sbuf_tensor("s_ssum", [128, 1], F32))
        s_rs = EE(nc.sbuf_tensor("s_rs", [128, 1], F32))
        s_t2p = EE(nc.sbuf_tensor("s_t2p", [128, 4], F32))
        p_sw = [EE(nc.psum_tensor("p_sw%d" % s, [128, 2048], F32)) for s in range(2)]

        id32l = R.dma("qS", "id3", lambda E: E.dma_start(out=s_id32_3[:], in_=ident), after=[c_ys])
        rwld = R.dma("qS", "rw", lambda E: E.dma_start(
            out=s_rwT[:].rearrange("p (kk e) -> p kk e", e=NEXP),
            in_=bass.AP(rwT_h, 0, [[NEXP, 128], [128 * NEXP, 16], [1, NEXP]])), after=[c_ys])

        g_ysall = Guard()
        g_opk = [Guard() for _ in range(2)]
        g_psw = [Guard(), Guard()]
        g_sT = [Guard() for _ in range(2)]
        g_xrow = Guard(); g_xf = Guard(); g_sq = Guard(); g_xf16 = Guard()
        g_ptr = [Guard(), Guard()]; g_plg = Guard(); g_ptl = Guard()
        g_xft = [Guard() for _ in range(16)]
        g_lg = Guard(); g_lgt = Guard()
        g_mx = Guard(); g_mi = Guard(); g_nm = Guard(); g_ex = Guard()
        g_ssum = Guard(); g_rs = Guard(); g_t2p = Guard()
        g_mu = Guard(); g_mu2 = Guard(); g_m2 = Guard(); g_sd = Guard(); g_rstd = Guard()

        for k in range(NCH):
            jofs = k
            g_w = g_ysall.w()

            def do_regs(E, jofs=jofs):
                E.reg_load(r_t, s_dyn[0:1, jofs:jofs + 1])
            R.raw("gp", do_regs, after=[lN])
            ysld = R.dma("qG", "ys", lambda E: E.dma_start(
                out=s_ysall[:].rearrange("p (kk t) -> p kk t", t=256),
                in_=bass.AP(ysf_h, r_t, [[L, 128], [128 * L, 32], [1, 256]])),
                after=[c_ys] + g_w)
            ys_all = ysld

            mm = None
            lastq = [None, None]
            for kk in range(32):
                buf = kk % 2
                opl = R.dma("qG" if buf == 0 else "qA", "op%d" % buf,
                            lambda E, kk=kk, buf=buf: E.dma_start(
                    out=s_opk[buf][:], in_=opT16_h.ap()[kk * 128:(kk + 1) * 128, :]),
                    after=g_opk[buf].w() + ([c_ys, opc_done] if k == 0 and kk < 2 else []))
                for q in range(2):
                    for dq in range(4):
                        aft = []
                        if dq == 0:
                            aft = [ys_all, opl]
                            if kk == 0:
                                aft = aft + g_psw[q].w()
                        mm = R.op("pe", lambda E, kk=kk, q=q, dq=dq, buf=buf: E.matmul(
                            out=p_sw[q][:, dq * 512:(dq + 1) * 512],
                            lhsT=s_ysall[:, kk * 256 + q * 128:kk * 256 + (q + 1) * 128],
                            rhs=s_opk[buf][:, dq * 512:(dq + 1) * 512],
                            start=(kk == 0), stop=(kk == 31)), after=aft)
                    lastq[q] = mm
                g_opk[buf].r(mm)
            g_ysall.r(mm)
            for q in range(2):
                ev = R.op("act", lambda E, q=q: E.activation(
                    out=s_sT[q][:], in_=p_sw[q][:], func=AF.Copy),
                    after=[lastq[q]] + g_sT[q].w())
                g_psw[q].r(ev)
            sT_done = ("act", R.cnt["act"])

            for q in range(2):
                row0 = k * 256 + q * 128
                xl = R.dma("qS", "xr3", lambda E, row0=row0: E.dma_start(
                    out=s_xrow[:], in_=xown_h.ap()[row0:row0 + 128, :]), after=g_xrow.w())
                add = R.op("dve", lambda E, q=q: E.tensor_tensor(
                    out=s_sT[q][:], in0=s_sT[q][:], in1=s_xrow[:], op=ALU.add),
                    after=[sT_done, xl])
                g_xrow.r(add)
                mu = R.op("dve", lambda E, q=q: E.tensor_reduce(
                    out=s_mu[:], in_=s_sT[q][:], axis=AX.X, op=ALU.add), after=[add] + g_mu.w())
                mub = R.op("dve", lambda E: E.tensor_scalar(
                    out=s_mu[:], in0=s_mu[:], scalar1=1.0 / D, scalar2=None, op0=ALU.mult))
                sq = R.op("act", lambda E, q=q: E.activation(
                    out=s_sq[:], in_=s_sT[q][:], func=AF.Square), after=[add] + g_sq.w())
                m2a = R.op("dve", lambda E: E.tensor_reduce(
                    out=s_m2[:], in_=s_sq[:], axis=AX.X, op=ALU.add), after=[sq] + g_m2.w())
                g_sq.r(m2a)
                m2b = R.op("dve", lambda E: E.tensor_scalar(
                    out=s_m2[:], in0=s_m2[:], scalar1=1.0 / D, scalar2=None, op0=ALU.mult))
                mu2v = R.op("dve", lambda E: E.tensor_scalar(
                    out=s_mu2[:], in0=s_mu[:], scalar1=s_mu[:, 0:1], scalar2=None, op0=ALU.mult),
                    after=[mub] + g_mu2.w())
                vv = R.op("dve", lambda E: E.tensor_tensor(
                    out=s_m2[:], in0=s_m2[:], in1=s_mu2[:], op=ALU.subtract), after=[m2b, mu2v])
                g_mu2.r(vv)
                sdv = R.op("act", lambda E: E.activation(
                    out=s_sd[:], in_=s_m2[:], func=AF.Sqrt, bias=s_eps[:, 0:1]),
                    after=[vv] + g_sd.w())
                g_m2.r(sdv)
                rst = R.op("dve", lambda E: E.reciprocal(out=s_rstd[:], in_=s_sd[:]),
                           after=[sdv] + g_rstd.w())
                g_sd.r(rst)
                nmu = R.op("dve", lambda E: E.tensor_scalar(
                    out=s_mu[:], in0=s_mu[:], scalar1=-1.0, scalar2=None, op0=ALU.mult),
                    after=[mu2v])
                g_mu.r(nmu)
                xfv = R.op("dve", lambda E, q=q: E.tensor_scalar(
                    out=s_xf[:], in0=s_sT[q][:], scalar1=s_mu[:, 0:1], scalar2=s_rstd[:, 0:1],
                    op0=ALU.add, op1=ALU.mult), after=[nmu, rst] + g_xf.w())
                g_sT[q].r(xfv); g_mu.r(xfv); g_rstd.r(xfv)
                xf16v = R.op("dve", lambda E: E.tensor_copy(out=s_xf16[:], in_=s_xf[:]),
                             after=[xfv] + g_xf16.w())
                st32 = R.dma("qS", "xf32s", lambda E, row0=row0: E.dma_start(
                    out=xf32o_h.ap()[row0:row0 + 128, :], in_=s_xf[:]), after=[xfv])
                st16 = R.dma("qS", "xf16s", lambda E, row0=row0: E.dma_start(
                    out=xf16o_h.ap()[row0:row0 + 128, :], in_=s_xf16[:]), after=[xf16v])
                g_xf16.r(st16)
                g_xf.r(st32)
                for kk in range(16):
                    slot = kk % 2
                    tr = R.op("pe", lambda E, kk=kk, slot=slot: E.transpose(
                        out=p_sw[0][:, slot * 512:slot * 512 + 128],
                        in_=s_xf[:, kk * 128:(kk + 1) * 128],
                        identity=s_id32_3[:]),
                        after=[xfv, id32l] + g_ptr[slot].w())
                    g_xf.r(tr)
                    tev = R.op("act", lambda E, kk=kk, slot=slot: E.activation(
                        out=s_xft[kk][:], in_=p_sw[0][:, slot * 512:slot * 512 + 128],
                        func=AF.Copy),
                        after=[tr] + g_xft[kk].w())
                    g_ptr[slot].r(tev)
                    g_psw[0].r(tev)
                xft_done = ("act", R.cnt["act"])
                mm2 = None
                for kk in range(16):
                    aft = ([xft_done, rwld] + g_plg.w()) if kk == 0 else []
                    mm2 = R.op("pe", lambda E, kk=kk: E.matmul(
                        out=p_sw[1][0:8, 0:128], lhsT=s_rwT[:, kk * NEXP:(kk + 1) * NEXP],
                        rhs=s_xft[kk][:],
                        start=(kk == 0), stop=(kk == 15)), after=aft)
                for kk in range(16):
                    g_xft[kk].r(mm2)
                lgev = R.op("act", lambda E: E.activation(out=s_lg[:], in_=p_sw[1][0:8, 0:128],
                                                          func=AF.Copy),
                            after=[mm2] + g_lg.w())
                g_plg.r(lgev)
                g_psw[1].r(lgev)
                trl = R.op("pe", lambda E: E.transpose(
                    out=p_sw[1][:, 512:520], in_=s_lg[0:8, :], identity=s_id32_3[0:8, 0:8]),
                    after=[lgev] + g_ptl.w())
                g_lg.r(trl)
                ltv = R.op("act", lambda E: E.activation(out=s_lgt[:], in_=p_sw[1][:, 512:520],
                                                         func=AF.Copy), after=[trl] + g_lgt.w())
                g_ptl.r(ltv)
                g_psw[1].r(ltv)
                mx = R.op("dve", lambda E: E.max(out=s_mx[:], in_=s_lgt[:]),
                          after=[ltv] + g_mx.w())
                g_lgt.r(mx)
                mi = R.op("dve", lambda E: E.max_index(out=s_mi[:], in_max=s_mx[:], in_values=s_lgt[:]),
                          after=[mx] + g_mi.w())
                g_lgt.r(mi)
                nm = R.op("dve", lambda E: E.tensor_scalar(
                    out=s_nm[:], in0=s_mx[:, 0:1], scalar1=-1.0, scalar2=None, op0=ALU.mult),
                    after=[mx] + g_nm.w())
                exv = R.op("act", lambda E: E.activation(out=s_ex[:], in_=s_mx[:], func=AF.Exp,
                                                         bias=s_nm[:, 0:1]),
                           after=[nm] + g_ex.w())
                g_mx.r(exv); g_nm.r(exv)
                ssv = R.op("dve", lambda E: E.tensor_reduce(
                    out=s_ssum[:], in_=s_ex[:], axis=AX.X, op=ALU.add), after=[exv] + g_ssum.w())
                rsv = R.op("dve", lambda E: E.reciprocal(out=s_rs[:], in_=s_ssum[:]),
                           after=[ssv] + g_rs.w())
                g_ssum.r(rsv)
                c0 = R.op("dve", lambda E: E.tensor_copy(out=s_t2p[:, 0:1], in_=s_mi[:, 0:1]),
                          after=[mi] + g_t2p.w())
                c1v = R.op("dve", lambda E: E.tensor_copy(out=s_t2p[:, 1:2], in_=s_mi[:, 1:2]))
                g_mi.r(c1v)
                c2 = R.op("dve", lambda E: E.tensor_tensor(out=s_t2p[:, 2:3], in0=s_ex[:, 0:1],
                                                           in1=s_rs[:, 0:1], op=ALU.mult),
                          after=[rsv])
                c3 = R.op("dve", lambda E: E.tensor_tensor(out=s_t2p[:, 3:4], in0=s_ex[:, 1:2],
                                                           in1=s_rs[:, 0:1], op=ALU.mult))
                g_ex.r(c3); g_rs.r(c3)
                stt2 = R.dma("qS", "t2s", lambda E, row0=row0: E.dma_start(
                    out=top2o_h.ap()[row0:row0 + 128, :], in_=s_t2p[:]), after=[c3, c0, c1v, c2])
                g_t2p.r(stt2)
        p3_tail = R.all_cur()

    c_xf = R.cc(lambda E: E.collective_compute(
        "AllGather", ALU.bypass, replica_groups=[list(range(8))],
        ins=[xf16o_h.ap().opt()], outs=[xffull_h.ap().opt()]), after=p3_tail)
    c_t2 = R.cc(lambda E: E.collective_compute(
        "AllGather", ALU.bypass, replica_groups=[list(range(8))],
        ins=[top2o_h.ap().opt()], outs=[top2full_h.ap().opt()]), after=[c_xf])

    with ExitStack() as P:
        EE = P.enter_context
        s_id32_4 = EE(nc.sbuf_tensor("s_id32p4", [128, 128], F32))
        s_b1r = EE(nc.sbuf_tensor("s_b1r", [128, F // 128], F32))
        s_b2r = EE(nc.sbuf_tensor("s_b2r", [128, D // 128], F32))
        s_ssf = EE(nc.sbuf_tensor("s_ssf", [128, 4], F32))
        s_ssi = EE(nc.sbuf_tensor("s_ssi", [128, 4], I32))
        P4r = ExitStack()
        ER = P4r.enter_context
        s_r4 = ER(nc.sbuf_tensor("s_r4", [128, 512], F32))
        s_E1 = ER(nc.sbuf_tensor("s_E1", [128, 128], F32))
        s_E2 = ER(nc.sbuf_tensor("s_E2", [128, 128], F32))
        s_P1 = ER(nc.sbuf_tensor("s_P1", [128, 128], F32))
        s_P2 = ER(nc.sbuf_tensor("s_P2", [128, 128], F32))
        s_oh = ER(nc.sbuf_tensor("s_oh", [128, 8 * 128], F32))
        s_pos = ER(nc.sbuf_tensor("s_pos", [128, 8 * 128], F32))
        s_keep = ER(nc.sbuf_tensor("s_keep", [128, 8 * 128], F32))
        s_cum = ER(nc.sbuf_tensor("s_cum", [128, 128], F32))
        s_tot = ER(nc.sbuf_tensor("s_tot", [128, 8], F32))
        s_TB = ER(nc.sbuf_tensor("s_TB", [8, 128], F32))
        s_TBc = ER(nc.sbuf_tensor("s_TBc", [8, 128], F32))
        s_prefP = ER(nc.sbuf_tensor("s_prefP", [128, 8], F32))
        s_gt8 = ER(nc.sbuf_tensor("s_gt8", [8, 1], F32))
        s_gtb = ER(nc.sbuf_tensor("s_gtb", [128, 8], F32))
        s_ts1 = ER(nc.sbuf_tensor("s_ts1", [128, 128], F32))
        s_ts2v = ER(nc.sbuf_tensor("s_ts2v", [128, 128], F32))
        s_kp = ER(nc.sbuf_tensor("s_kp", [128, 128], F32))
        s_w1t = ER(nc.sbuf_tensor("s_w1t", [128, 128], F32))
        s_w2t = ER(nc.sbuf_tensor("s_w2t", [128, 128], F32))
        s_tmp = ER(nc.sbuf_tensor("s_tmpP4", [128, 128], F32))
        s_iotac = ER(nc.sbuf_tensor("s_iotac", [128, CAP], F32))
        s_cc4 = ER(nc.sbuf_tensor("s_cc4", [128, 1], I32))
        s_cc4f = ER(nc.sbuf_tensor("s_cc4f", [128, 1], F32))
        s_pcol = ER(nc.sbuf_tensor("s_pcol", [128, 1], F32))
        s_jrow = ER(nc.sbuf_tensor("s_jrow", [128, 128], F32))
        s_lhsA = ER(nc.sbuf_tensor("s_lhsA", [128, 384], BF16))
        s_lhsB = ER(nc.sbuf_tensor("s_lhsB", [128, 384], BF16))
        s_ohj = [ER(nc.sbuf_tensor("s_ohj%d" % s, [128, CAP], BF16)) for s in range(2)]
        s_tab4 = ER(nc.sbuf_tensor("s_tab4", [4, CAP], F32))
        s_ssfh = ER(nc.sbuf_tensor("s_ssfh", [128, 4], F32))
        s_ssfl = ER(nc.sbuf_tensor("s_ssfl", [128, 4], F32))
        p_h1 = [EE(nc.psum_tensor("p_h1%d" % s, [128, CAP], F32)) for s in range(2)]
        p_h2 = [EE(nc.psum_tensor("p_h2%d" % s, [128, CAP], F32)) for s in range(2)]
        p_g16 = [EE(nc.psum_tensor("p_g16%d" % s, [128, 128], BF16)) for s in range(2)]
        p_r = EE(nc.psum_tensor("p_r", [128, 128], F32))
        p_tab4 = EE(nc.psum_tensor("p_tab4", [4, CAP], F32))

        id32l4 = R.dma("qS", "id4", lambda E: E.dma_start(out=s_id32_4[:], in_=ident), after=[c_t2])
        b1l = R.dma("qS", "b1", lambda E: E.dma_start(out=s_b1r[:], in_=b1r), after=[c_t2])
        b2l = R.dma("qS", "b2", lambda E: E.dma_start(out=s_b2r[:], in_=b2r))
        r4l = R.dma("qS", "r4", lambda E: E.dma_start(
            out=s_r4[:], in_=top2full_h.ap().rearrange("(p j) c -> p (j c)", p=128)),
            after=[c_t2])
        exE1 = R.op("dve", lambda E: E.tensor_copy(
            out=s_E1[:], in_=s_r4[:].rearrange("p (j c) -> p j c", c=4)[:, :, 0:1].squeeze(2)),
            after=[r4l])
        exE2 = R.op("dve", lambda E: E.tensor_copy(
            out=s_E2[:], in_=s_r4[:].rearrange("p (j c) -> p j c", c=4)[:, :, 1:2].squeeze(2)))
        exP1 = R.op("dve", lambda E: E.tensor_copy(
            out=s_P1[:], in_=s_r4[:].rearrange("p (j c) -> p j c", c=4)[:, :, 2:3].squeeze(2)))
        exP2 = R.op("dve", lambda E: E.tensor_copy(
            out=s_P2[:], in_=s_r4[:].rearrange("p (j c) -> p j c", c=4)[:, :, 3:4].squeeze(2)))

        ones_bc = s_ones[:].unsqueeze(2).broadcast_to([128, 1, 128]).squeeze(1)
        ones8_bc = s_ones[0:8, :].unsqueeze(2).broadcast_to([8, 1, 128]).squeeze(1)
        gtb_done = None
        agg_done = []
        g_tot = Guard(); g_TB = Guard(); g_TBc = Guard(); g_prefP = Guard(); g_pr = Guard()
        g_cumb = Guard()
        for ph in range(2):
            Ed = s_E1 if ph == 0 else s_E2
            Edev = exE1 if ph == 0 else exE2
            for e in range(E):
                oh = R.op("dve", lambda E_, e=e, Ed=Ed: E_.tensor_single_scalar(
                    out=s_oh[:, e * 128:(e + 1) * 128], in_=Ed[:], scalar=float(e), op=ALU.is_equal),
                    after=[Edev])
                cm = R.op("dve", lambda E_, e=e: E_.tensor_tensor_scan(
                    out=s_cum[:], data0=ones_bc, data1=s_oh[:, e * 128:(e + 1) * 128],
                    initial=0.0, op0=ALU.mult, op1=ALU.add), after=[oh, ones1] + g_cumb.w())
                ps = R.op("dve", lambda E_, e=e: E_.tensor_tensor(
                    out=s_pos[:, e * 128:(e + 1) * 128], in0=s_cum[:],
                    in1=s_oh[:, e * 128:(e + 1) * 128], op=ALU.subtract), after=[cm])
                tc = R.op("dve", lambda E_, e=e: E_.tensor_copy(
                    out=s_tot[:, e:e + 1], in_=s_cum[:, 127:128]),
                    after=[cm] + (g_tot.w() if e == 0 else []))
                g_cumb.r(tc)
            tot_done = ("dve", R.cnt["dve"])
            trt = R.op("pe", lambda E_: E_.transpose(out=p_r[0:8, :], in_=s_tot[:, 0:8],
                                                     identity=s_id32_4[:]),
                       after=[tot_done, id32l4] + g_pr.w())
            g_tot.r(trt)
            tbv = R.op("act", lambda E_: E_.activation(out=s_TB[:], in_=p_r[0:8, :], func=AF.Copy),
                       after=[trt] + g_TB.w())
            g_pr.r(tbv)
            cm8 = R.op("dve", lambda E_: E_.tensor_tensor_scan(
                out=s_TBc[:], data0=ones8_bc, data1=s_TB[:], initial=0.0,
                op0=ALU.mult, op1=ALU.add), after=[tbv, ones1] + g_TBc.w())
            if ph == 0:
                gtc = R.op("dve", lambda E_: E_.tensor_copy(out=s_gt8[:], in_=s_TBc[:, 127:128]),
                           after=[cm8])
                gtd = R.dma("qS", "gt", lambda E_: E_.dma_start(
                    out=gtscr_h.ap().rearrange("1 e -> e 1"), in_=s_gt8[:]), after=[gtc])
                gtb_done = R.dma("qG", "gtb", lambda E_: E_.dma_start(
                    out=s_gtb[:], in_=gtscr_h.ap().to_broadcast([128, NEXP])), after=[gtd])
            prf = R.op("dve", lambda E_: E_.tensor_tensor(out=s_TB[:], in0=s_TBc[:], in1=s_TB[:],
                                                          op=ALU.subtract), after=[cm8])
            trb = R.op("pe", lambda E_: E_.transpose(out=p_r[:, 0:8], in_=s_TB[:],
                                                     identity=s_id32_4[0:8, 0:8]),
                       after=[prf] + g_pr.w())
            g_TB.r(trb)
            prp = R.op("act", lambda E_: E_.activation(out=s_prefP[:], in_=p_r[:, 0:8], func=AF.Copy),
                       after=[trb] + g_prefP.w())
            g_pr.r(prp)
            g_TBc.r(prf)
            for e in range(E):
                if ph == 0:
                    pg = R.op("dve", lambda E_, e=e: E_.tensor_scalar(
                        out=s_pos[:, e * 128:(e + 1) * 128], in0=s_pos[:, e * 128:(e + 1) * 128],
                        scalar1=s_prefP[:, e:e + 1], scalar2=None, op0=ALU.add), after=[prp])
                else:
                    pg = R.op("dve", lambda E_, e=e: E_.tensor_scalar(
                        out=s_pos[:, e * 128:(e + 1) * 128], in0=s_pos[:, e * 128:(e + 1) * 128],
                        scalar1=s_prefP[:, e:e + 1], scalar2=s_gtb[:, e:e + 1],
                        op0=ALU.add, op1=ALU.add), after=[prp, gtb_done])
                g_prefP.r(pg)
                kp = R.op("dve", lambda E_, e=e: E_.tensor_single_scalar(
                    out=s_keep[:, e * 128:(e + 1) * 128], in_=s_pos[:, e * 128:(e + 1) * 128],
                    scalar=float(CAP), op=ALU.is_lt), after=[pg])
                t1_ = R.op("dve", lambda E_, e=e: E_.tensor_scalar(
                    out=s_pos[:, e * 128:(e + 1) * 128], in0=s_pos[:, e * 128:(e + 1) * 128],
                    scalar1=float(e * CAP) - OOBV, scalar2=None, op0=ALU.add), after=[kp])
                t2_ = R.op("dve", lambda E_, e=e: E_.tensor_tensor(
                    out=s_pos[:, e * 128:(e + 1) * 128], in0=s_pos[:, e * 128:(e + 1) * 128],
                    in1=s_keep[:, e * 128:(e + 1) * 128], op=ALU.mult), after=[t1_])
                R.op("dve", lambda E_, e=e: E_.tensor_scalar(
                    out=s_pos[:, e * 128:(e + 1) * 128], in0=s_pos[:, e * 128:(e + 1) * 128],
                    scalar1=OOBV, scalar2=None, op0=ALU.add), after=[t2_])
            slotm_done = ("dve", R.cnt["dve"])
            ts_dst = s_ts1 if ph == 0 else s_ts2v
            z1 = R.op("dve", lambda E_, ts_dst=ts_dst: E_.memset(ts_dst[:], 0.0), after=[slotm_done])
            z2 = R.op("dve", lambda E_: E_.memset(s_kp[:], 0.0))
            for e in range(E):
                tm = R.op("dve", lambda E_, e=e: E_.tensor_tensor(
                    out=s_tmp[:], in0=s_oh[:, e * 128:(e + 1) * 128],
                    in1=s_pos[:, e * 128:(e + 1) * 128], op=ALU.mult))
                R.op("dve", lambda E_, ts_dst=ts_dst: E_.tensor_tensor(
                    out=ts_dst[:], in0=ts_dst[:], in1=s_tmp[:], op=ALU.add))
                tm2 = R.op("dve", lambda E_, e=e: E_.tensor_tensor(
                    out=s_tmp[:], in0=s_oh[:, e * 128:(e + 1) * 128],
                    in1=s_keep[:, e * 128:(e + 1) * 128], op=ALU.mult))
                R.op("dve", lambda E_: E_.tensor_tensor(
                    out=s_kp[:], in0=s_kp[:], in1=s_tmp[:], op=ALU.add))
            wdst = s_w1t if ph == 0 else s_w2t
            Pv = s_P1 if ph == 0 else s_P2
            Pev = exP1 if ph == 0 else exP2
            wv = R.op("dve", lambda E_, wdst=wdst, Pv=Pv: E_.tensor_tensor(
                out=wdst[:], in0=Pv[:], in1=s_kp[:], op=ALU.mult), after=[Pev])
            agg_done.append(("dve", R.cnt["dve"]))

        io0 = R.op("gp", lambda E_: E_.iota(s_iotac[:], [[1, CAP]], 0, 0,
                                            allow_small_or_imprecise_dtypes=True))
        cc4l = R.dma("qG", "ccb", lambda E_: E_.dma_start(
            out=s_cc4[:], in_=dynoff[0:1, 13:14].to_broadcast([128, 1])))
        cc4c = R.op("dve", lambda E_: E_.tensor_copy(out=s_cc4f[:], in_=s_cc4[:]),
                    after=[cc4l])
        iov = R.op("dve", lambda E_: E_.tensor_scalar(
            out=s_iotac[:], in0=s_iotac[:], scalar1=s_cc4f[:, 0:1], scalar2=None,
            op0=ALU.add), after=[io0, cc4c])
        iop = R.op("gp", lambda E_: E_.iota(s_pcol[:], [[1, 1]], 0, 1,
                                            allow_small_or_imprecise_dtypes=True))
        ioj = R.op("gp", lambda E_: E_.iota(s_jrow[:], [[1, 128]], 0, 0,
                                            allow_small_or_imprecise_dtypes=True))
        for (dst, col, srcv, aftev) in (
                (s_lhsA, 0, None, [iop]), (s_lhsA, 1, s_jrow, [ioj]),
                (s_lhsA, 2, s_w1t, [agg_done[0]]),
                (s_lhsB, 0, None, [iop]), (s_lhsB, 1, s_jrow, [ioj]),
                (s_lhsB, 2, s_w2t, [agg_done[1]])):
            R.op("dve", lambda E_, dst=dst, col=col, srcv=srcv: E_.tensor_copy(
                out=dst[:].rearrange("p (j c) -> p j c", c=3)[:, :, col:col + 1].squeeze(2),
                in_=(s_pcol[:].broadcast_to([128, 128]) if srcv is None else srcv[:])),
                after=aftev)
        lhs_done = ("dve", R.cnt["dve"])
        g_ohj = [Guard(), Guard()]
        mmt = None
        for ph in range(2):
            slt = s_ts1 if ph == 0 else s_ts2v
            lhs = s_lhsA if ph == 0 else s_lhsB
            for j in range(128):
                buf = j % 2
                oh = R.op("dve", lambda E_, slt=slt, j=j, buf=buf: E_.tensor_scalar(
                    out=s_ohj[buf][:], in0=s_iotac[:], scalar1=slt[:, j:j + 1], scalar2=None,
                    op0=ALU.is_equal),
                    after=[iov, agg_done[ph]] + g_ohj[buf].w())
                mmt = R.op("pe", lambda E_, ph=ph, j=j, buf=buf, lhs=lhs: E_.matmul(
                    out=p_tab4[0:3, :], lhsT=lhs[:, 3 * j:3 * j + 3], rhs=s_ohj[buf][:],
                    start=(ph == 0 and j == 0), stop=(ph == 1 and j == 127)),
                    after=[oh] + ([lhs_done] if ph == 0 and j == 0 else []))
                g_ohj[buf].r(mmt)
        tabev = R.op("act", lambda E_: E_.activation(out=s_tab4[0:3, :], in_=p_tab4[0:3, :],
                                                     func=AF.Copy), after=[mmt])
        tabst = R.dma("qS", "tab", lambda E_: E_.dma_start(
            out=ptab_h.ap(), in_=s_tab4[0:3, :]), after=[tabev])
        c_tab = R.cc(lambda E_: E_.collective_compute(
            "AllGather", ALU.bypass, replica_groups=[list(range(8))],
            ins=[ptab_h.ap().opt()], outs=[rtab_h.ap().opt()]),
            after=[tabst])

        def do_regs_ss(E_):
            E_.reg_load(r_t, s_dyn[0:1, 12:13])
        R.raw("gp", do_regs_ss, after=[lN])
        ssgh = R.dma("qG", "ssg", lambda E_: E_.dma_start(
            out=s_ssfh[:].unsqueeze(2),
            in_=bass.AP(rtab_h, r_t, [[1, 128], [128, 4], [1, 1]]),
            allow_slow_non_contiguous=True),
            after=[c_tab])

        def do_regs_ss2(E_):
            E_.reg_load(r_t, s_dyn[0:1, 14:15])
        R.raw("gp", do_regs_ss2)
        ssgl = R.dma("qG", "ssg2", lambda E_: E_.dma_start(
            out=s_ssfl[:].unsqueeze(2),
            in_=bass.AP(rtab_h, r_t, [[1, 128], [128, 4], [1, 1]]),
            allow_slow_non_contiguous=True))
        scmb = R.op("dve", lambda E_: E_.scalar_tensor_tensor(
            out=s_ssf[:], in0=s_ssfh[:], scalar=128.0, in1=s_ssfl[:],
            op0=ALU.mult, op1=ALU.add), after=[ssgh, ssgl])
        cssi = R.op("dve", lambda E_: E_.tensor_copy(out=s_ssi[:], in_=s_ssf[:]),
                    after=[scmb])
        P4r.close()
        s_xg = [EE(nc.sbuf_tensor("s_xg%d" % m, [128, D], BF16)) for m in range(4)]
        s_xgT = [EE(nc.sbuf_tensor("s_xgT%d" % kk, [128, CAP], BF16)) for kk in range(16)]
        s_w1f = [EE(nc.sbuf_tensor("s_w1f%d" % s, [128, D], BF16)) for s in range(2)]
        s_h116 = EE(nc.sbuf_tensor("s_h116", [128, 64 * CAP], BF16))
        s_w2d = [EE(nc.sbuf_tensor("s_w2d%d" % s, [128, 32 * 128], BF16)) for s in range(2)]
        s_hf = EE(nc.sbuf_tensor("s_hf", [128, CAP], F32))
        s_hf16 = EE(nc.sbuf_tensor("s_hf16", [128, CAP], BF16))
        s_hT = [EE(nc.sbuf_tensor("s_hT%d" % qq, [128, D], BF16)) for qq in range(4)]
        g_pg = [Guard() for _ in range(2)]
        xgld = []
        for m in range(4):
            xgld.append(R.dma("qG", "xg%d" % m, lambda E_, m=m: E_.indirect_dma_start(
                out=s_xg[m][:], out_offset=None,
                in_=xffull_h.ap(), in_offset=IndirectOffsetOnAxis(s_ssi[:, m:m + 1], 0),
                bounds_check=NTOK - 1, oob_is_err=False),
                after=[cssi, c_xf]))
        for m in range(4):
            for kk in range(16):
                slot = (m * 16 + kk) % 2
                tr = R.op("pe", lambda E_, m=m, kk=kk, slot=slot: E_.transpose(
                    out=p_g16[slot][:], in_=s_xg[m][:, kk * 128:(kk + 1) * 128],
                    identity=s_id16[:]),
                    after=[xgld[m], idc] + g_pg[slot].w())
                tev = R.op("act", lambda E_, m=m, kk=kk, slot=slot: E_.activation(
                    out=s_xgT[kk][:, m * 128:(m + 1) * 128], in_=p_g16[slot][:], func=AF.Copy),
                    after=[tr])
                g_pg[slot].r(tev)
        xgT_done = ("act", R.cnt["act"])

        g_w1f = [Guard() for _ in range(2)]
        g_ph1 = [Guard() for _ in range(2)]
        for f in range(64):
            buf = f % 2
            wld = R.dma("qG", "w1f%d" % buf, lambda E_, f=f, buf=buf: E_.dma_start(
                out=s_w1f[buf][:], in_=bass.AP(w1T_h, f * 128, [[F, 128], [128 * F, 16], [1, 128]])),
                after=g_w1f[buf].w())
            mm = None
            for kk in range(16):
                aft = []
                if kk == 0:
                    aft = [xgT_done, wld] + g_ph1[buf].w()
                mm = R.op("pe", lambda E_, kk=kk, buf=buf: E_.matmul(
                    out=p_h1[buf][:], lhsT=s_w1f[buf][:, kk * 128:(kk + 1) * 128], rhs=s_xgT[kk][:],
                    start=(kk == 0), stop=(kk == 15)), after=aft)
            g_w1f[buf].r(mm)
            gel = R.op("act", lambda E_, f=f, buf=buf: E_.activation(
                out=s_h116[:, f * CAP:(f + 1) * CAP], in_=p_h1[buf][:], func=AF.Gelu,
                bias=s_b1r[:, f:f + 1]), after=[mm, b1l])
            g_ph1[buf].r(gel)
        h1_done = ("act", R.cnt["act"])


        g_w2d = [Guard() for _ in range(2)]
        g_ph2 = [Guard() for _ in range(2)]
        g_hf = Guard(); g_hf16 = Guard()
        g_hT = [Guard() for _ in range(4)]
        hstores = []
        for jd in range(16):
            buf = jd % 2
            mm = None
            for half in range(2):
                wbuf = (jd * 2 + half) % 2
                wld = R.dma("qG", "w2d%d" % wbuf, lambda E_, jd=jd, half=half, wbuf=wbuf: E_.dma_start(
                    out=s_w2d[wbuf][:],
                    in_=bass.AP(w2T_h, jd * 128 + half * 32 * 128 * D,
                                [[D, 128], [128 * D, 32], [1, 128]])),
                    after=g_w2d[wbuf].w())
                for fi in range(32):
                    ff = half * 32 + fi
                    aft = [wld]
                    if ff == 0:
                        aft = [h1_done, wld] + g_ph2[buf].w()
                    mm = R.op("pe", lambda E_, fi=fi, ff=ff, buf=buf, wbuf=wbuf: E_.matmul(
                        out=p_h2[buf][:], lhsT=s_w2d[wbuf][:, fi * 128:(fi + 1) * 128],
                        rhs=s_h116[:, ff * CAP:(ff + 1) * CAP],
                        start=(ff == 0), stop=(ff == 63)),
                        after=(aft if fi == 0 else []))
                g_w2d[wbuf].r(mm)
            hev = R.op("act", lambda E_, buf=buf: E_.activation(
                out=s_hf[:], in_=p_h2[buf][:], func=AF.Copy),
                after=[mm] + g_hf.w())
            g_ph2[buf].r(hev)
            hev = R.op("dve", lambda E_, jd=jd: E_.tensor_scalar(
                out=s_hf[:], in0=s_hf[:], scalar1=s_b2r[:, jd:jd + 1], scalar2=None,
                op0=ALU.add), after=[hev, b2l])
            hc16 = R.op("dve", lambda E_: E_.tensor_copy(out=s_hf16[:], in_=s_hf[:]),
                        after=[hev] + g_hf16.w())
            g_hf.r(hc16)
            for qq in range(4):
                slot = qq % 2
                tr = R.op("pe", lambda E_, qq=qq, slot=slot: E_.transpose(
                    out=p_g16[slot][:], in_=s_hf16[:, qq * 128:(qq + 1) * 128], identity=s_id16[:]),
                    after=[hc16] + g_pg[slot].w())
                g_hf16.r(tr)
                tev = R.op("act", lambda E_, qq=qq, jd=jd, slot=slot: E_.activation(
                    out=s_hT[qq][:, jd * 128:(jd + 1) * 128], in_=p_g16[slot][:], func=AF.Copy),
                    after=[tr] + (g_hT[qq].w() if jd == 0 else []))
                g_pg[slot].r(tev)
        hT_done = ("act", R.cnt["act"])
        for qq in range(4):
            hstores.append(R.dma("qS", "hst", lambda E_, qq=qq: E_.dma_start(
                out=hfino_h.ap()[qq * 128:(qq + 1) * 128, :], in_=s_hT[qq][:]), after=[hT_done]))
        p4_tail = R.all_cur()

    c_h = R.cc(lambda E_: E_.collective_compute(
        "AllGather", ALU.bypass, replica_groups=[list(range(8))],
        ins=[hfino_h.ap().opt()], outs=[hfull_h.ap().opt()]), after=p4_tail)

    NSB = NEXP * CAP // 128
    with ExitStack() as P:
        EE = P.enter_context
        s_st2 = EE(nc.sbuf_tensor("s_st2", [128, 2 * NSB], F32))
        s_st4 = EE(nc.sbuf_tensor("s_st4", [128, 3 * NSB], F32))
        s_cb5 = EE(nc.sbuf_tensor("s_cb5", [128, 1], I32))
        s_cbf = EE(nc.sbuf_tensor("s_cbf", [128, 1], F32))
        s_gidb = EE(nc.sbuf_tensor("s_gidb", [128, TOKPC], F32))
        s_MT = EE(nc.sbuf_tensor("s_MT", [128, NSB * 512], BF16))
        s_hfb = [EE(nc.sbuf_tensor("s_hfb%d" % s, [128, 1024], BF16)) for s in range(2)]
        s_ffn4 = EE(nc.sbuf_tensor("s_ffn4", [128, 4 * D], F32))
        s_xr5 = EE(nc.sbuf_tensor("s_xr5", [128, D], F32))
        s_sq5 = EE(nc.sbuf_tensor("s_sq5", [128, D], F32))
        s_oout = EE(nc.sbuf_tensor("s_oout", [128, D], F32))
        s_mu5 = EE(nc.sbuf_tensor("s_mu5", [128, 1], F32))
        s_mu25 = EE(nc.sbuf_tensor("s_mu25", [128, 1], F32))
        s_m25 = EE(nc.sbuf_tensor("s_m25", [128, 1], F32))
        s_sd5 = EE(nc.sbuf_tensor("s_sd5", [128, 1], F32))
        s_rstd5 = EE(nc.sbuf_tensor("s_rstd5", [128, 1], F32))
        p_acc = [EE(nc.psum_tensor("p_acc%d" % t, [128, 1024], F32)) for t in range(4)]

        stl = R.dma("qS", "stl", lambda E_: E_.dma_start(
            out=s_st4[:].rearrange("p (c r q) -> p c r q", r=3, q=4),
            in_=bass.AP(rtab_h, 0, [[1, 128], [1536, 8], [512, 3], [128, 4]]),
            allow_slow_non_contiguous=True), after=[c_tab])
        _v4 = s_st4[:].rearrange("p (c r q) -> p c r q", r=3, q=4)
        _hi = _v4[:, :, 0:1, :].squeeze(2)
        _lo = _v4[:, :, 1:2, :].squeeze(2)
        _wv = _v4[:, :, 2:3, :].squeeze(2)
        _s2 = s_st2[:].rearrange("p (c q c2) -> p c q c2", q=4, c2=2)
        _ido = _s2[:, :, :, 0:1].squeeze(3)
        _wo = _s2[:, :, :, 1:2].squeeze(3)
        cmb1 = R.op("dve", lambda E_: E_.scalar_tensor_tensor(
            out=_ido, in0=_hi, scalar=128.0, in1=_lo, op0=ALU.mult, op1=ALU.add),
            after=[stl])
        cmb2 = R.op("dve", lambda E_: E_.tensor_copy(out=_wo, in_=_wv))
        st2_done = cmb2
        cbl = R.dma("qG", "cbl", lambda E_: E_.dma_start(
            out=s_cb5[:], in_=dynoff[0:1, 8:9].to_broadcast([128, 1])))
        cbc = R.op("dve", lambda E_: E_.tensor_copy(out=s_cbf[:], in_=s_cb5[:]),
                   after=[cbl])
        gio = R.op("gp", lambda E_: E_.iota(s_gidb[:], [[1, TOKPC]], 0, 0,
                                            allow_small_or_imprecise_dtypes=True))
        gid = R.op("dve", lambda E_: E_.tensor_scalar(
            out=s_gidb[:], in0=s_gidb[:], scalar1=s_cbf[:, 0:1], scalar2=None,
            op0=ALU.add), after=[cbc, gio])

        g_MT = [Guard() for _ in range(NSB)]
        g_hfb = [Guard() for _ in range(2)]
        g_pacc = [Guard() for _ in range(4)]
        g_ffn = [Guard() for _ in range(4)]
        g_xr5 = Guard(); g_sq5 = Guard(); g_oout = Guard()
        ihf = 0
        laststore = None
        for quad in range(4):
            mtev = []
            for sb in range(NSB):
                ev = R.op("dve", lambda E_, sb=sb, quad=quad: E_.tensor_scalar(
                    out=s_MT[:, sb * 512:(sb + 1) * 512],
                    in0=s_gidb[:, quad * 512:(quad + 1) * 512],
                    scalar1=s_st2[:, 2 * sb:2 * sb + 1],
                    scalar2=s_st2[:, 2 * sb + 1:2 * sb + 2],
                    op0=ALU.is_equal, op1=ALU.mult),
                    after=[gid, st2_done] + g_MT[sb].w())
                mtev.append(ev)
            evq = [[None] * 4, [None] * 4]
            for dmH in range(2):
                lastmm = [None] * 4
                for sb in range(NSB):
                    buf = ihf % 2
                    ihf += 1
                    ld = R.dma("qS" if buf == 0 else "qA", "hf%d" % buf,
                               lambda E_, sb=sb, dmH=dmH, buf=buf: E_.dma_start(
                        out=s_hfb[buf][:],
                        in_=hfull_h.ap()[sb * 128:(sb + 1) * 128,
                                         dmH * 1024:(dmH + 1) * 1024]),
                        after=[c_h] + g_hfb[buf].w())
                    mm = None
                    for tt in range(4):
                        for nh in range(2):
                            aft = []
                            if tt == 0 and nh == 0:
                                aft = [mtev[sb], ld]
                                if sb == 0:
                                    for t2 in range(4):
                                        aft += g_pacc[t2].w()
                            mm = R.op("pe", lambda E_, sb=sb, tt=tt, nh=nh, buf=buf: E_.matmul(
                                out=p_acc[tt][:, nh * 512:(nh + 1) * 512],
                                lhsT=s_MT[:, sb * 512 + tt * 128:sb * 512 + (tt + 1) * 128],
                                rhs=s_hfb[buf][:, nh * 512:(nh + 1) * 512],
                                start=(sb == 0), stop=(sb == NSB - 1)), after=aft)
                        lastmm[tt] = mm
                    g_hfb[buf].r(mm)
                    if dmH == 1:
                        g_MT[sb].r(mm)
                for tt in range(4):
                    ev = R.op("act", lambda E_, tt=tt, dmH=dmH: E_.activation(
                        out=s_ffn4[:, tt * D + dmH * 1024:tt * D + (dmH + 1) * 1024],
                        in_=p_acc[tt][:], func=AF.Copy),
                        after=[lastmm[tt]] + (g_ffn[tt].w() if dmH == 0 else []))
                    g_pacc[tt].r(ev)
                    evq[dmH][tt] = ev
            for tt in range(4):
                m_ = quad * 4 + tt
                sl = s_ffn4[:, tt * D:(tt + 1) * D]
                xl = R.dma("qS", "xr5", lambda E_, m_=m_: E_.dma_start(
                    out=s_xr5[:], in_=xf32o_h.ap()[m_ * 128:(m_ + 1) * 128, :]),
                    after=g_xr5.w())
                sv = R.op("dve", lambda E_, sl=sl: E_.tensor_tensor(
                    out=sl, in0=sl, in1=s_xr5[:], op=ALU.add),
                    after=[xl, evq[1][tt]])
                g_xr5.r(sv)
                mu = R.op("dve", lambda E_, sl=sl: E_.tensor_reduce(
                    out=s_mu5[:], in_=sl, axis=AX.X, op=ALU.add), after=[sv])
                mub = R.op("dve", lambda E_: E_.tensor_scalar(
                    out=s_mu5[:], in0=s_mu5[:], scalar1=1.0 / D, scalar2=None, op0=ALU.mult))
                sq = R.op("act", lambda E_, sl=sl: E_.activation(
                    out=s_sq5[:], in_=sl, func=AF.Square), after=[sv] + g_sq5.w())
                m2a = R.op("dve", lambda E_: E_.tensor_reduce(
                    out=s_m25[:], in_=s_sq5[:], axis=AX.X, op=ALU.add), after=[sq])
                g_sq5.r(m2a)
                m2b = R.op("dve", lambda E_: E_.tensor_scalar(
                    out=s_m25[:], in0=s_m25[:], scalar1=1.0 / D, scalar2=None, op0=ALU.mult))
                mu2v = R.op("dve", lambda E_: E_.tensor_scalar(
                    out=s_mu25[:], in0=s_mu5[:], scalar1=s_mu5[:, 0:1], scalar2=None,
                    op0=ALU.mult), after=[mub])
                vv = R.op("dve", lambda E_: E_.tensor_tensor(
                    out=s_m25[:], in0=s_m25[:], in1=s_mu25[:], op=ALU.subtract),
                    after=[m2b, mu2v])
                sdv = R.op("act", lambda E_: E_.activation(
                    out=s_sd5[:], in_=s_m25[:], func=AF.Sqrt, bias=s_eps[:, 0:1]), after=[vv])
                rst = R.op("dve", lambda E_: E_.reciprocal(out=s_rstd5[:], in_=s_sd5[:]),
                           after=[sdv])
                nmu = R.op("dve", lambda E_: E_.tensor_scalar(
                    out=s_mu5[:], in0=s_mu5[:], scalar1=-1.0, scalar2=None, op0=ALU.mult),
                    after=[mu2v])
                ov = R.op("dve", lambda E_, sl=sl: E_.tensor_scalar(
                    out=s_oout[:], in0=sl, scalar1=s_mu5[:, 0:1], scalar2=s_rstd5[:, 0:1],
                    op0=ALU.add, op1=ALU.mult), after=[nmu, rst] + g_oout.w())
                g_ffn[tt].r(ov)
                ost = R.dma("qS", "out", lambda E_, m_=m_: E_.dma_start(
                    out=outown[m_ * 128:(m_ + 1) * 128, :], in_=s_oout[:]), after=[ov])
                g_oout.r(ost)
                laststore = ost

    for _fam in list(R.famstream):
        _eng = "qS" if R.famstream[_fam] == "sync" else "gp"
        R.raw(_eng, lambda E_, s=_fam, v=R.cnt[_fam]: E_.wait_ge(R.semh[s], v))

    with nc.Block() as block:
        block.sync(lambda e: [f(e) for f in R.streams["sync"]])
        block.scalar(lambda e: [f(e) for f in R.streams["scalar"]])
        block.vector(lambda e: [f(e) for f in R.streams["vector"]])
        block.tensor(lambda e: [f(e) for f in R.streams["tensor"]])
        block.gpsimd(lambda e: [f(e) for f in R.streams["gpsimd"]])
    ctx.close()
    return nc


def _dynoff(c, h):
    d = np.zeros((1, 16), np.int32)
    for k in range(NCH):
        d[0, k] = k * T + h * 256
    for r_i in range(4):
        d[0, 8 + r_i] = r_i * NTOK + c * TOKPC
    d[0, 12] = c * CAP * 3
    d[0, 13] = c * CAP
    d[0, 14] = c * CAP * 3 + CAP
    return d


def make_inmaps(inputs):
    x = np.asarray(inputs["L0_attn_x"], np.float32)
    ipw = np.asarray(inputs["L0_attn_in_proj_w"], np.float32)
    cwf = np.asarray(inputs["L0_attn_conv_w"], np.float32)
    cbf = np.asarray(inputs["L0_attn_conv_b"], np.float32)
    xpw = np.asarray(inputs["L0_attn_x_proj_w"], np.float32)
    dtw = np.asarray(inputs["L0_attn_dt_proj_w"], np.float32)
    dtbf = np.asarray(inputs["L0_attn_dt_proj_b"], np.float32)
    alogf = np.asarray(inputs["L0_attn_A_log"], np.float32)
    dv = np.asarray(inputs["L0_attn_D"], np.float32)
    opw = np.asarray(inputs["L0_attn_out_proj_w"], np.float32)
    rw = np.asarray(inputs["L0_ffn_router_w"], np.float32)
    w1 = np.asarray(inputs["L0_ffn_w1"], np.float32)
    b1 = np.asarray(inputs["L0_ffn_b1"], np.float32)
    w2 = np.asarray(inputs["L0_ffn_w2"], np.float32)
    b2 = np.asarray(inputs["L0_ffn_b2"], np.float32)

    C = np.ascontiguousarray
    in_maps = []
    own_rows_all = []
    for c in range(8):
        b, h = c // 2, c % 2
        hs = slice(h * HC, (h + 1) * HC)
        rows = np.concatenate([np.arange(k * 512 + h * 256, k * 512 + h * 256 + 256)
                               for k in range(NCH)])
        own_rows_all.append(b * L + rows)
        pack = lambda a: C(a.reshape(NB, 128, -1).transpose(1, 0, 2).reshape(128, -1))
        m = {
            "xb": C(x[b]),
            "xown": C(x[b][rows]),
            "wuT": C(ipw[hs, :].T),
            "wzT": C(ipw[INNER + h * HC: INNER + (h + 1) * HC, :].T),
            "cw": pack(cwf[hs, 0, :]),
            "cb": pack(cbf[hs].reshape(-1, 1)),
            "xpT": C(xpw[:, hs].T),
            "dtT": C(dtw[hs, :].T),
            "dtb": pack(dtbf[hs].reshape(-1, 1)),
            "alog": pack(alogf[hs, :]),
            "dvec": pack(dv[hs].reshape(-1, 1)),
            "opT": C(opw.T),
            "rwT": C(rw.T),
            "w1T": C(w1.T),
            "b1r": C(b1.reshape(F // 128, 128).T),
            "w2T": C(w2.T),
            "b2r": C(b2.reshape(D // 128, 128).T),
            "ident": np.eye(128, dtype=np.float32),
            "sids": C((c * CAP + np.arange(CAP, dtype=np.int32)).reshape(4, 128).T),
            "hoff": np.array([[h * 256]], np.int32),
            "dynoff": _dynoff(c, h),
            "coff": np.array([[c * TOKPC]], np.int32),
            "epst": np.full((128, 1), 1e-5, np.float32),
        }
        in_maps.append(m)
    return in_maps, own_rows_all


def kernel(**inputs):
    in_maps, own_rows_all = make_inmaps(inputs)
    nc = build_program()
    res = run_bass_kernel_spmd(nc, in_maps, core_ids=list(range(8)))
    out = np.empty((B, L, D), np.float32)
    flat = out.reshape(NTOK, D)
    for c in range(8):
        flat[own_rows_all[c]] = res.results[c]["outown"]
    return out


# version 13 (final)
# speedup vs baseline: 1.3037x; 1.0631x over previous
import numpy as np
import ml_dtypes
import concourse.bass as bass
import concourse.mybir as mybir
from concourse.bass_utils import run_bass_kernel_spmd
from concourse.bass_core_rust import IndirectOffsetOnAxis

F32 = mybir.dt.float32
BF16 = mybir.dt.bfloat16
I32 = mybir.dt.int32
U32 = mybir.dt.uint32
AF = mybir.ActivationFunctionType
ALU = mybir.AluOpType
AX = mybir.AxisListType

B, L, D = 4, 4096, 2048
INNER, HC, NB = 4096, 2048, 16
NST, DTR = 16, 128
T = 512
NCH = L // T
E, CAP, F = 8, 512, 8192
NEXP = 8
NTOK = B * L
TOKPC = NTOK // 8
TBL = E * CAP + 128
OOBV = 100000.0

ENG = {"pe": "tensor", "act": "scalar", "dve": "vector", "gp": "gpsimd", "qS": "sync",
       "qG": "gpsimd", "qA": "scalar"}


class Rec:
    def __init__(self, nc):
        self.nc = nc
        self.streams = {k: [] for k in ["sync", "scalar", "vector", "tensor", "gpsimd"]}
        self.cnt = {"pe": 0, "act": 0, "dve": 0, "gp": 0, "qS": 0, "qG": 0}
        self.semh = {}
        self.ccn = 0
        self._ctx = None
        self.famstream = {}
        self.hwm = {k: {} for k in self.streams}

    def _filt(self, stream, aft):
        best = {}
        for (s, v) in aft:
            if v > best.get(s, 0):
                best[s] = v
        h = self.hwm[stream]
        out = []
        for s, v in best.items():
            if v > h.get(s, 0):
                h[s] = v
                out.append((s, v))
        return out

    def init_sems(self, ctx):
        self._ctx = ctx
        for name in ["pe", "act", "dve", "gp", "qS", "qG"]:
            self.semh[name] = ctx.enter_context(self.nc.semaphore("sem_" + name))

    def op(self, eng, fn, after=()):
        self.cnt[eng] += 1
        tgt = self.cnt[eng]
        aft = [x for x in after if x is not None]
        if eng in ("act", "dve", "gp"):
            aft.append((eng, tgt - 1))
        aft = self._filt(ENG[eng], aft)
        sem = self.semh

        def emit(E, aft=tuple(aft), fn=fn, eng=eng):
            for (s, v) in aft:
                if v > 0:
                    E.wait_ge(sem[s], v)
            fn(E).then_inc(sem[eng], 1)
        self.streams[ENG[eng]].append(emit)
        return (eng, tgt)

    def raw(self, eng, fn, after=()):
        aft = self._filt(ENG[eng], [x for x in after if x is not None])
        sem = self.semh

        def emit(E, aft=tuple(aft), fn=fn):
            for (s, v) in aft:
                if v > 0:
                    E.wait_ge(sem[s], v)
            fn(E)
        self.streams[ENG[eng]].append(emit)

    def dma(self, q, fam, fn, after=()):
        if fam not in self.cnt:
            self.cnt[fam] = 0
            self.semh[fam] = self._ctx.enter_context(self.nc.semaphore("sem_" + fam))
            self.famstream[fam] = ENG[q]
        assert self.famstream[fam] == ENG[q], fam
        self.cnt[fam] += 16
        tgt = self.cnt[fam]
        aft = self._filt(ENG[q], [x for x in after if x is not None])
        sem = self.semh

        def emit(E, aft=tuple(aft), fn=fn, fam=fam):
            for (s, v) in aft:
                if v > 0:
                    E.wait_ge(sem[s], v)
            fn(E).then_inc(sem[fam], 16)
        self.streams[ENG[q]].append(emit)
        return (fam, tgt)

    def cc(self, fn, after=()):
        name = "cc%d" % self.ccn
        self.ccn += 1
        self.semh[name] = self._ctx.enter_context(self.nc.semaphore("sem_" + name))
        aft = self._filt("gpsimd", [x for x in after if x is not None])
        sem = self.semh

        def emit(E, aft=tuple(aft), fn=fn, name=name):
            for (s, v) in aft:
                if v > 0:
                    E.wait_ge(sem[s], v)
            fn(E).then_inc(sem[name])
        self.streams["gpsimd"].append(emit)
        return (name, 1)

    def all_cur(self):
        return [(e, v) for (e, v) in self.cnt.items() if v > 0]


class Guard:
    def __init__(self):
        self.evs = []

    def r(self, ev):
        if ev is not None:
            self.evs.append(ev)
        return ev

    def w(self):
        evs, self.evs = self.evs, []
        return evs


def build_program():
    nc = bass.Bass("TRN2", target_bir_lowering=False, dynamic_dma_scratch_size=65536)
    dt = nc.dram_tensor

    xb_h = dt("xb", [L, D], F32, kind="ExternalInput")
    wuT_h = dt("wuT", [D, HC], F32, kind="ExternalInput")
    wzT_h = dt("wzT", [D, HC], F32, kind="ExternalInput")
    cw = dt("cw", [128, NB * 4], F32, kind="ExternalInput").ap()
    cb = dt("cb", [128, NB], F32, kind="ExternalInput").ap()
    xpT_h = dt("xpT", [HC, 160], F32, kind="ExternalInput")
    dtT_h = dt("dtT", [DTR, HC], F32, kind="ExternalInput")
    dtb = dt("dtb", [128, NB], F32, kind="ExternalInput").ap()
    alog = dt("alog", [128, NB * NST], F32, kind="ExternalInput").ap()
    dvec = dt("dvec", [128, NB], F32, kind="ExternalInput").ap()
    opT_h = dt("opT", [INNER, D], F32, kind="ExternalInput")
    rwT_h = dt("rwT", [D, E], F32, kind="ExternalInput")
    w1T_h = dt("w1T", [D, F], F32, kind="ExternalInput")
    b1r = dt("b1r", [128, F // 128], F32, kind="ExternalInput").ap()
    w2T_h = dt("w2T", [F, D], F32, kind="ExternalInput")
    b2r = dt("b2r", [128, D // 128], F32, kind="ExternalInput").ap()
    ident = dt("ident", [128, 128], F32, kind="ExternalInput").ap()
    xown_h = dt("xown", [TOKPC, D], F32, kind="ExternalInput")
    sids = dt("sids", [128, 4], I32, kind="ExternalInput").ap()
    hoff = dt("hoff", [1, 1], I32, kind="ExternalInput").ap()
    coff = dt("coff", [1, 1], I32, kind="ExternalInput").ap()
    epst = dt("epst", [128, 1], F32, kind="ExternalInput").ap()
    dynoff = dt("dynoff", [1, 32], I32, kind="ExternalInput").ap()

    outown = dt("outown", [TOKPC, D], F32, kind="ExternalOutput").ap()

    u16d_h = dt("u16d", [HC, L], BF16)
    z16d_h = dt("z16d", [HC, L], BF16)
    xdblp_h = dt("xdblp", [160, L], F32)
    xdblr_h = dt("xdblr", [160, L], F32)
    bflat_h = dt("bflat", [NST, T], F32)
    cflat_h = dt("cflat", [NST, T], F32)
    ys16d_h = dt("ys16d", [NCH * HC, T], BF16)
    ysf0_h = dt("ysf0", [2 * 4 * HC, T], BF16)
    ysf1_h = dt("ysf1", [2 * 4 * HC, T], BF16)
    opT16_h = dt("opT16", [INNER, D], BF16)
    xf32o_h = dt("xf32o", [TOKPC, D], F32)
    xf16o_h = dt("xf16o", [TOKPC, D], BF16)
    xffull_h = dt("xffull", [NTOK, D], BF16)
    top2o_h = dt("top2o", [TOKPC, 4], F32)
    top2full_h = dt("top2full", [NTOK, 4], F32)
    gtscr_h = dt("gtscr", [1, E], F32)
    ptab_h = dt("ptab", [3, CAP], F32)
    rtab_h = dt("rtab", [E * 3, CAP], F32)
    hfino_h = dt("hfino", [CAP, D], BF16)
    hfull_h = dt("hfull", [E * CAP, D], BF16)

    from contextlib import ExitStack
    ctx = ExitStack()
    R = Rec(nc)
    R.init_sems(ctx)

    s_id16 = ctx.enter_context(nc.sbuf_tensor("s_id16", [128, 128], BF16))
    s_cw = ctx.enter_context(nc.sbuf_tensor("s_cw", [128, NB * 4], F32))
    s_cb = ctx.enter_context(nc.sbuf_tensor("s_cb", [128, NB], F32))
    s_dtb = ctx.enter_context(nc.sbuf_tensor("s_dtb", [128, NB], F32))
    s_negA = ctx.enter_context(nc.sbuf_tensor("s_negA", [128, NB * NST], F32))
    s_dvec = ctx.enter_context(nc.sbuf_tensor("s_dvec", [128, NB], F32))
    s_eps = ctx.enter_context(nc.sbuf_tensor("s_eps", [128, 1], F32))
    s_dyn = ctx.enter_context(nc.sbuf_tensor("s_dyn", [1, 32], I32))
    s_ones = ctx.enter_context(nc.sbuf_tensor("s_ones", [128, 1], F32))

    R.dma("qS", "pre", lambda E: E.dma_start(out=s_cw[:], in_=cw))
    R.dma("qS", "pre", lambda E: E.dma_start(out=s_cb[:], in_=cb))
    R.dma("qS", "pre", lambda E: E.dma_start(out=s_dtb[:], in_=dtb))
    R.dma("qS", "pre", lambda E: E.dma_start(out=s_negA[:], in_=alog))
    R.dma("qS", "pre", lambda E: E.dma_start(out=s_dvec[:], in_=dvec))
    R.dma("qS", "pre", lambda E: E.dma_start(out=s_eps[:], in_=epst))
    lN = R.dma("qS", "pre", lambda E: E.dma_start(out=s_dyn[:], in_=dynoff))
    idld = R.dma("qG", "idg", lambda E: E.dma_start(out=s_id16[:], in_=ident))
    idc = idld
    e1 = R.op("act", lambda E: E.activation(out=s_negA[:], in_=s_negA[:], func=AF.Exp),
               after=[lN])
    negA_done = R.op("dve", lambda E: E.tensor_scalar(out=s_negA[:], in0=s_negA[:], scalar1=-1.0,
                                                      scalar2=None, op0=ALU.mult), after=[e1])
    ones1 = R.op("dve", lambda E: E.memset(s_ones[:], 1.0))
    r_t = nc.gpsimd.alloc_register("r_t")
    for c8 in range(8):
        opc = R.dma("qG", "opc", lambda E, c8=c8: E.dma_start(
            out=opT16_h.ap()[c8 * 512:(c8 + 1) * 512, :],
            in_=opT_h.ap()[c8 * 512:(c8 + 1) * 512, :]))
    opc_done = opc

    with ExitStack() as P:
        EE = P.enter_context
        s_wuT = [EE(nc.sbuf_tensor("s_wuT%d" % k, [128, HC], BF16)) for k in range(16)]
        s_wzT = [EE(nc.sbuf_tensor("s_wzT%d" % k, [128, HC], BF16)) for k in range(16)]
        s_xpT = [EE(nc.sbuf_tensor("s_xpT%d" % i, [128, 160], BF16)) for i in range(NB)]
        s_xc = [EE(nc.sbuf_tensor("s_xc%d" % q, [128, D], BF16)) for q in range(1)]
        s_xT = [EE(nc.sbuf_tensor("s_xTT%d" % k, [128, T], BF16)) for k in range(16)]
        s_upre = EE(nc.sbuf_tensor("s_upre", [128, T + 3], BF16))
        s_halo = EE(nc.sbuf_tensor("s_halo", [128, 3 * NB], BF16))
        s_cvt = EE(nc.sbuf_tensor("s_cvt", [128, T], F32))
        s_u16t = EE(nc.sbuf_tensor("s_u16t", [128, T], BF16))
        s_z16t = s_u16t
        p_tr = [EE(nc.psum_tensor("p_tr%d" % s, [128, 128], BF16)) for s in range(2)]
        p_mm = [EE(nc.psum_tensor("p_mm%d" % s, [128, T], F32)) for s in range(2)]
        p_dt = EE(nc.psum_tensor("p_dt", [128, T], F32))
        p_bc = EE(nc.psum_tensor("p_bc", [32, T], F32))

        for k in range(16):
            R.dma("qG", "wp1", lambda E, k=k: E.dma_start(out=s_wuT[k][:], in_=wuT_h.ap()[k * 128:(k + 1) * 128, :]))
            R.dma("qG", "wp1", lambda E, k=k: E.dma_start(out=s_wzT[k][:], in_=wzT_h.ap()[k * 128:(k + 1) * 128, :]))
        for i in range(NB):
            wl = R.dma("qG", "wp1", lambda E, i=i: E.dma_start(out=s_xpT[i][:], in_=xpT_h.ap()[i * 128:(i + 1) * 128, :]))
        w_done = wl
        halo0 = R.op("dve", lambda E: E.memset(s_halo[:], 0.0))

        g_xc = [Guard() for _ in range(1)]
        g_xT = [Guard() for _ in range(16)]
        g_tr = [Guard() for _ in range(2)]
        g_mm = [Guard() for _ in range(2)]
        g_u16t = Guard(); g_z16t = g_u16t
        g_upre = Guard(); g_halo = Guard()
        g_dtp = Guard(); g_cvt = Guard()

        for k in range(NCH):
            wEs = [g_xT[i].w() for i in range(16)]
            for q in range(4):
                xld = R.dma("qG", "xld", lambda E, q=q, k=k: E.dma_start(
                    out=s_xc[0][:], in_=xb_h.ap()[k * T + q * 128: k * T + (q + 1) * 128, :]),
                    after=g_xc[0].w())
                for i in range(16):
                    slot = i % 2
                    tr = R.op("pe", lambda E, i=i, q=q, slot=slot: E.transpose(
                        out=p_tr[slot][:], in_=s_xc[0][:, i * 128:(i + 1) * 128],
                        identity=s_id16[:]),
                        after=[xld, idc] + g_tr[slot].w()
                        + (wEs[i] if q == 0 else []))
                    g_xc[0].r(tr)
                    ev = R.op("act", lambda E, i=i, q=q, slot=slot: E.activation(
                        out=s_xT[i][:, q * 128:(q + 1) * 128], in_=p_tr[slot][:], func=AF.Copy),
                        after=[tr])
                    g_tr[slot].r(ev)
            xT_all = ("act", R.cnt["act"])

            for j in range(32):
                isu = j < 16
                i = j if isu else j - 16
                wt = s_wuT if isu else s_wzT
                bank = j % 2
                mm = None
                for kk in range(16):
                    aft = []
                    if kk == 0:
                        aft = [xT_all, w_done] + g_mm[bank].w()
                        if j == 0 and k > 0:
                            aft += g_dtp.w()
                    mm = R.op("pe", lambda E, kk=kk, i=i, wt=wt, bank=bank: E.matmul(
                        out=p_mm[bank][:], lhsT=wt[kk][:, i * 128:(i + 1) * 128], rhs=s_xT[kk][:],
                        start=(kk == 0), stop=(kk == 15)), after=aft)
                for kk in range(16):
                    g_xT[kk].r(mm)
                if isu:
                    ev = R.op("act", lambda E, i=i, bank=bank: E.activation(
                        out=s_upre[:, 3:T + 3], in_=p_mm[bank][:], func=AF.Copy),
                        after=[mm] + g_upre.w())
                    g_mm[bank].r(ev)
                    hcp = R.op("dve", lambda E, i=i: E.tensor_copy(
                        out=s_upre[:, 0:3], in_=s_halo[:, 3 * i:3 * i + 3]),
                        after=[halo0] + g_halo.w())
                    c1 = R.op("dve", lambda E, i=i: E.tensor_scalar(
                        out=s_cvt[:], in0=s_upre[:, 0:T], scalar1=s_cw[:, 4 * i:4 * i + 1],
                        scalar2=None, op0=ALU.mult), after=[ev, hcp] + g_cvt.w())
                    for kk in range(1, 4):
                        c1 = R.op("dve", lambda E, i=i, kk=kk: E.scalar_tensor_tensor(
                            out=s_cvt[:], in0=s_upre[:, kk:kk + T],
                            scalar=s_cw[:, 4 * i + kk:4 * i + kk + 1],
                            in1=s_cvt[:], op0=ALU.mult, op1=ALU.add), after=[c1])
                    hal = R.op("dve", lambda E, i=i: E.tensor_copy(
                        out=s_halo[:, 3 * i:3 * i + 3], in_=s_upre[:, T:T + 3]), after=[c1])
                    g_upre.r(hal)
                    g_halo.r(hal)
                    sil = R.op("act", lambda E, i=i: E.activation(
                        out=s_u16t[:], in_=s_cvt[:], func=AF.Silu, bias=s_cb[:, i:i + 1]),
                        after=[c1] + g_u16t.w())
                    g_cvt.r(sil)
                    st_u = R.dma("qS", "stuz", lambda E, i=i, k=k: E.dma_start(
                        out=u16d_h.ap()[i * 128:(i + 1) * 128, k * T:(k + 1) * T], in_=s_u16t[:]),
                        after=[sil])
                    m1 = R.op("pe", lambda E, i=i: E.matmul(
                        out=p_dt[:], lhsT=s_xpT[i][:, 0:128], rhs=s_u16t[:],
                        start=(i == 0), stop=(i == 15)), after=[sil])
                    m2 = R.op("pe", lambda E, i=i: E.matmul(
                        out=p_bc[:], lhsT=s_xpT[i][:, 128:160], rhs=s_u16t[:],
                        start=(i == 0), stop=(i == 15)), after=[sil])
                    g_u16t.r(st_u); g_u16t.r(m1); g_u16t.r(m2)
                else:
                    ev = R.op("act", lambda E, bank=bank: E.activation(
                        out=s_z16t[:], in_=p_mm[bank][:], func=AF.Copy),
                        after=[mm] + g_z16t.w())
                    g_mm[bank].r(ev)
                    st_z = R.dma("qS", "stuz", lambda E, i=i, k=k: E.dma_start(
                        out=z16d_h.ap()[i * 128:(i + 1) * 128, k * T:(k + 1) * T], in_=s_z16t[:]),
                        after=[ev])
                    g_z16t.r(st_z)

            dtev = R.op("dve", lambda E: E.tensor_copy(out=s_cvt[:], in_=p_dt[:]),
                        after=[m1] + g_cvt.w())
            g_dtp.r(dtev)
            st1 = R.dma("qS", "cvt", lambda E, k=k: E.dma_start(
                out=xdblp_h.ap()[0:128, k * T:(k + 1) * T], in_=s_cvt[:]), after=[dtev])
            g_cvt.r(st1)
            bcev = R.op("dve", lambda E: E.tensor_copy(out=s_cvt[0:32, :], in_=p_bc[:]),
                        after=[m2, st1] + g_cvt.w())
            g_dtp.r(bcev)
            st2 = R.dma("qS", "cvt", lambda E, k=k: E.dma_start(
                out=xdblp_h.ap()[128:160, k * T:(k + 1) * T], in_=s_cvt[0:32, :]), after=[bcev])
            g_cvt.r(st2)

        p1_tail = R.all_cur()

    c_xdbl = R.cc(lambda E: E.collective_compute(
        "AllReduce", ALU.add, replica_groups=[[0, 1], [2, 3], [4, 5], [6, 7]],
        ins=[xdblp_h.ap().opt()], outs=[xdblr_h.ap().opt()]), after=p1_tail)

    with ExitStack() as P:
        EE = P.enter_context
        s_dtT = EE(nc.sbuf_tensor("s_dtT", [128, HC], BF16))
        s_carry = EE(nc.sbuf_tensor("s_carry", [128, NB * NST], F32))
        s_dt16 = EE(nc.sbuf_tensor("s_dt16", [128, T], BF16))
        s_Brep = EE(nc.sbuf_tensor("s_Brep", [128, NST * T], BF16))
        s_Crep = EE(nc.sbuf_tensor("s_Crep", [128, NST * T], BF16))
        s_bcfl = EE(nc.sbuf_tensor("s_bcfl", [33, 2048], BF16))
        s_on1 = EE(nc.sbuf_tensor("s_on1", [33, 128], BF16))
        s_dA = EE(nc.sbuf_tensor("s_dA", [128, NST * T], F32))
        s_bt = EE(nc.sbuf_tensor("s_bt", [128, NST * T], BF16))
        s_h = EE(nc.sbuf_tensor("s_h", [128, NST * T], BF16))
        s_del = EE(nc.sbuf_tensor("s_del", [128, T], F32))
        s_t1 = EE(nc.sbuf_tensor("s_t1", [128, T], F32))
        s_u16 = EE(nc.sbuf_tensor("s_u16", [128, T], BF16))
        s_du16 = EE(nc.sbuf_tensor("s_du16", [128, T], BF16))
        s_z16 = EE(nc.sbuf_tensor("s_z16", [128, T], BF16))
        s_y = EE(nc.sbuf_tensor("s_y", [128, T], F32))
        s_sz = EE(nc.sbuf_tensor("s_sz", [128, T], BF16))
        s_ys16 = EE(nc.sbuf_tensor("s_ys16", [128, T], BF16))
        p_d = EE(nc.psum_tensor("p_d", [128, T], F32))
        p_rep = [EE(nc.psum_tensor("p_rep%d" % s, [128, 1024], F32)) for s in range(2)]

        dtT_ld = R.dma("qG", "dtt", lambda E: E.dma_start(out=s_dtT[:], in_=dtT_h.ap()), after=[c_xdbl])
        on1v = R.op("dve", lambda E: E.memset(s_on1[:], 1.0))
        carry0 = R.op("dve", lambda E: E.memset(s_carry[:], 0.0))

        g_dt16 = Guard()
        g_Brep = Guard(); g_Crep = Guard(); g_bflat = Guard(); g_cflat = Guard()
        g_pd = Guard(); g_t1 = Guard(); g_del = Guard()
        g_u16 = Guard(); g_z16 = Guard(); g_dA = Guard()
        g_bt = Guard(); g_h = Guard(); g_y = Guard(); g_sz = Guard(); g_ys = Guard()
        g_du16 = Guard()

        for k in range(NCH):
            dtl = R.dma("qG", "dtl", lambda E, k=k: E.dma_start(
                out=s_dt16[:], in_=xdblr_h.ap()[0:128, k * T:(k + 1) * T]),
                after=[c_xdbl] + g_dt16.w())
            cast_dt = dtl
            bfd = R.dma("qS", "bfd", lambda E, k=k: E.dma_start(
                out=bflat_h.ap(), in_=xdblr_h.ap()[128:144, k * T:(k + 1) * T]),
                after=[c_xdbl] + g_bflat.w())
            cfd = R.dma("qS", "cfd", lambda E, k=k: E.dma_start(
                out=cflat_h.ap(), in_=xdblr_h.ap()[144:160, k * T:(k + 1) * T]),
                after=[c_xdbl] + g_cflat.w())
            g_prep = getattr(R, "_g_prep", None)
            if g_prep is None:
                g_prep = R._g_prep = [Guard(), Guard()]
                R._g_bcfl = [Guard(), Guard()]
            g_bcfl = R._g_bcfl
            brl = crl = None
            for rep_i, (row0, s_rep, flat_ap, dep) in enumerate(
                    [(0, s_Brep, bflat_h, bfd), (32, s_Crep, cflat_h, cfd)]):
                last = None
                for seg in range(NST * T // 1024):
                    par = seg % 2
                    fl_ld = R.dma("qG", "fl%d" % par, lambda E, row0=row0, seg=seg, par=par, flat_ap=flat_ap: E.dma_start(
                        out=s_bcfl[row0:row0 + 1, par * 1024:(par + 1) * 1024],
                        in_=flat_ap.ap().rearrange("n t -> (n t)").unsqueeze(0)
                        [:, seg * 1024:(seg + 1) * 1024]),
                        after=[dep] + g_bcfl[par].w())
                    mmr0 = R.op("pe", lambda E, row0=row0, par=par: E.matmul(
                        out=p_rep[par][:, 0:512], lhsT=s_on1[row0:row0 + 1, :],
                        rhs=s_bcfl[row0:row0 + 1, par * 1024:par * 1024 + 512],
                        start=True, stop=True),
                        after=[fl_ld, on1v] + g_prep[par].w())
                    mmr = R.op("pe", lambda E, row0=row0, par=par: E.matmul(
                        out=p_rep[par][:, 512:1024], lhsT=s_on1[row0:row0 + 1, :],
                        rhs=s_bcfl[row0:row0 + 1, par * 1024 + 512:(par + 1) * 1024],
                        start=True, stop=True))
                    g_bcfl[par].r(mmr)
                    evr = R.op("act", lambda E, s_rep=s_rep, seg=seg, par=par: E.activation(
                        out=s_rep[:, seg * 1024:(seg + 1) * 1024], in_=p_rep[par][:],
                        func=AF.Copy), after=[mmr])
                    g_prep[par].r(evr)
                    last = evr
                if rep_i == 0:
                    brl = last
                    g_bflat.r(fl_ld)
                else:
                    crl = last
                    g_cflat.r(fl_ld)

            for i in range(NB):
                mmd = R.op("pe", lambda E, i=i: E.matmul(
                    out=p_d[:], lhsT=s_dtT[:, i * 128:(i + 1) * 128], rhs=s_dt16[:],
                    start=True, stop=True),
                    after=[cast_dt, dtT_ld] + g_pd.w())
                g_dt16.r(mmd)
                a1 = R.op("act", lambda E, i=i: E.activation(
                    out=s_t1[:], in_=p_d[:], func=AF.Abs, bias=s_dtb[:, i:i + 1]),
                    after=[mmd] + g_t1.w())
                a2 = R.op("act", lambda E: E.activation(out=s_t1[:], in_=s_t1[:], func=AF.Exp, scale=-1.0))
                a3 = R.op("act", lambda E: E.activation(out=s_t1[:], in_=s_t1[:], func=AF.Ln, bias=1.0))
                a4 = R.op("dve", lambda E, i=i: E.tensor_scalar(
                    out=s_del[:], in0=p_d[:], scalar1=s_dtb[:, i:i + 1], scalar2=0.0,
                    op0=ALU.add, op1=ALU.max),
                    after=[mmd] + g_del.w())
                g_pd.r(a1); g_pd.r(a4)
                dl = R.op("dve", lambda E: E.tensor_tensor(out=s_del[:], in0=s_del[:], in1=s_t1[:],
                                                           op=ALU.add), after=[a3, a4])
                g_t1.r(dl)
                ul = R.dma("qS", "ul", lambda E, i=i, k=k: E.dma_start(
                    out=s_u16[:], in_=u16d_h.ap()[i * 128:(i + 1) * 128, k * T:(k + 1) * T]),
                    after=g_u16.w())
                du = R.op("dve", lambda E: E.tensor_tensor(out=s_du16[:], in0=s_del[:], in1=s_u16[:],
                                                           op=ALU.mult),
                          after=[dl, ul] + g_du16.w())
                exps = []
                dAw = g_dA.w()
                for n in range(NST):
                    exps.append(R.op("act", lambda E, i=i, n=n: E.activation(
                        out=s_dA[:, n * T:(n + 1) * T], in_=s_del[:], func=AF.Exp,
                        scale=s_negA[:, i * NST + n: i * NST + n + 1]),
                        after=[dl, negA_done] + (dAw if n == 0 else [])))
                    g_del.r(exps[n])
                du_bc = s_du16[:].unsqueeze(1).broadcast_to([128, NST, T])
                bt = R.op("dve", lambda E, du_bc=du_bc: E.tensor_tensor(
                    out=s_bt.rearrange("p (n t) -> p n t", n=NST), in0=du_bc,
                    in1=s_Brep.rearrange("p (n t) -> p n t", n=NST), op=ALU.mult),
                    after=[du, brl] + g_bt.w())
                g_du16.r(bt); g_Brep.r(bt)
                sc = bt
                hw = g_h.w()
                for n in range(NST):
                    sc = R.op("dve", lambda E, i=i, n=n: E.tensor_tensor_scan(
                        out=s_h[:, n * T:(n + 1) * T], data0=s_dA[:, n * T:(n + 1) * T],
                        data1=s_bt[:, n * T:(n + 1) * T],
                        initial=s_carry[:, i * NST + n: i * NST + n + 1],
                        op0=ALU.mult, op1=ALU.add),
                        after=[exps[n], carry0, bt] + (hw if n == 0 else []))
                    g_dA.r(sc)
                cu = R.op("dve", lambda E, i=i: E.tensor_copy(
                    out=s_carry[:, i * NST:(i + 1) * NST],
                    in_=s_h[:].rearrange("p (n t) -> p n t", n=NST)[:, :, T - 1:T].squeeze(2)),
                    after=[sc])
                hc = R.op("dve", lambda E: E.tensor_tensor(out=s_bt[:], in0=s_h[:], in1=s_Crep[:],
                                                           op=ALU.mult), after=[cu, crl])
                g_h.r(hc); g_Crep.r(hc)
                tr1 = R.op("dve", lambda E: E.tensor_tensor(
                    out=s_bt[:, 0:8 * T], in0=s_bt[:, 0:8 * T], in1=s_bt[:, 8 * T:16 * T],
                    op=ALU.add), after=[hc])
                tr2 = R.op("dve", lambda E: E.tensor_tensor(
                    out=s_bt[:, 0:4 * T], in0=s_bt[:, 0:4 * T], in1=s_bt[:, 4 * T:8 * T],
                    op=ALU.add))
                tr3 = R.op("dve", lambda E: E.tensor_tensor(
                    out=s_bt[:, 0:2 * T], in0=s_bt[:, 0:2 * T], in1=s_bt[:, 2 * T:4 * T],
                    op=ALU.add))
                yr = R.op("dve", lambda E: E.tensor_tensor(
                    out=s_y[:], in0=s_bt[:, 0:T], in1=s_bt[:, T:2 * T],
                    op=ALU.add), after=g_y.w())
                g_bt.r(yr)
                zl = R.dma("qS", "zl", lambda E, i=i, k=k: E.dma_start(
                    out=s_z16[:], in_=z16d_h.ap()[i * 128:(i + 1) * 128, k * T:(k + 1) * T]),
                    after=g_z16.w())
                szv = R.op("act", lambda E: E.activation(out=s_sz[:], in_=s_z16[:], func=AF.Silu),
                           after=[zl] + g_sz.w())
                g_z16.r(szv)
                y2 = R.op("dve", lambda E, i=i: E.scalar_tensor_tensor(
                    out=s_y[:], in0=s_u16[:], scalar=s_dvec[:, i:i + 1], in1=s_y[:],
                    op0=ALU.mult, op1=ALU.add), after=[yr])
                g_u16.r(y2)
                ysv = R.op("dve", lambda E: E.tensor_tensor(out=s_ys16[:], in0=s_y[:], in1=s_sz[:],
                                                            op=ALU.mult), after=[y2, szv] + g_ys.w())
                g_y.r(ysv); g_sz.r(ysv)
                yst = R.dma("qS", "yst", lambda E, i=i, k=k: E.dma_start(
                    out=ys16d_h.ap()[k * HC + i * 128:k * HC + (i + 1) * 128, :], in_=s_ys16[:]),
                    after=[ysv])
                g_ys.r(yst)
            if k == 3:
                p2_mid = R.all_cur()
        p2_tail = R.all_cur()

    c_ys1 = R.cc(lambda E: E.collective_compute(
        "AllGather", ALU.bypass, replica_groups=[[0, 1], [2, 3], [4, 5], [6, 7]],
        ins=[ys16d_h.ap()[0:4 * HC, :].opt()], outs=[ysf0_h.ap().opt()]), after=p2_mid)
    c_ys = R.cc(lambda E: E.collective_compute(
        "AllGather", ALU.bypass, replica_groups=[[0, 1], [2, 3], [4, 5], [6, 7]],
        ins=[ys16d_h.ap()[4 * HC:8 * HC, :].opt()], outs=[ysf1_h.ap().opt()]), after=p2_tail)

    with ExitStack() as P:
        EE = P.enter_context
        s_opk = [EE(nc.sbuf_tensor("s_opk%d" % s, [128, D], BF16)) for s in range(4)]
        s_id32_3 = EE(nc.sbuf_tensor("s_id32p3", [128, 128], F32))
        s_ysall = EE(nc.sbuf_tensor("s_ysall", [128, 32 * 256], BF16))
        s_sT = [EE(nc.sbuf_tensor("s_sTq%d" % q, [128, D], F32)) for q in range(2)]
        s_xrow = EE(nc.sbuf_tensor("s_xrow", [128, D], F32))
        s_mu = EE(nc.sbuf_tensor("s_mu", [128, 1], F32))
        s_mu2 = EE(nc.sbuf_tensor("s_mu2", [128, 1], F32))
        s_m2 = EE(nc.sbuf_tensor("s_m2", [128, 1], F32))
        s_sd = EE(nc.sbuf_tensor("s_sd", [128, 1], F32))
        s_rstd = EE(nc.sbuf_tensor("s_rstd", [128, 1], F32))
        s_xf = EE(nc.sbuf_tensor("s_xf", [128, D], F32))
        s_sq = EE(nc.sbuf_tensor("s_sq", [128, D], F32))
        s_xf16 = EE(nc.sbuf_tensor("s_xf16", [128, D], BF16))
        s_xft = [EE(nc.sbuf_tensor("s_xft%d" % kk, [128, 128], F32)) for kk in range(16)]
        s_rwT = EE(nc.sbuf_tensor("s_rwT", [128, 16 * NEXP], F32))
        s_lg = EE(nc.sbuf_tensor("s_lg", [8, 128], F32))
        s_lgt = EE(nc.sbuf_tensor("s_lgt", [128, 8], F32))
        s_mx = EE(nc.sbuf_tensor("s_mx", [128, 8], F32))
        s_mi = EE(nc.sbuf_tensor("s_mi", [128, 8], U32))
        s_nm = EE(nc.sbuf_tensor("s_nm", [128, 1], F32))
        s_ex = EE(nc.sbuf_tensor("s_ex", [128, 8], F32))
        s_ssum = EE(nc.sbuf_tensor("s_ssum", [128, 1], F32))
        s_rs = EE(nc.sbuf_tensor("s_rs", [128, 1], F32))
        s_t2p = EE(nc.sbuf_tensor("s_t2p", [128, 4], F32))
        p_sw = [EE(nc.psum_tensor("p_sw%d" % s, [128, 2048], F32)) for s in range(2)]

        id32l = R.dma("qS", "id3", lambda E: E.dma_start(out=s_id32_3[:], in_=ident),
                      after=[c_ys1] + p2_tail)
        rwld = R.dma("qS", "rw", lambda E: E.dma_start(
            out=s_rwT[:].rearrange("p (kk e) -> p kk e", e=NEXP),
            in_=bass.AP(rwT_h, 0, [[NEXP, 128], [128 * NEXP, 16], [1, NEXP]])),
            after=[c_ys1] + p2_tail)

        g_ysall = Guard()
        g_opk = [Guard() for _ in range(4)]
        g_psw = [Guard(), Guard()]
        g_sT = [Guard() for _ in range(2)]
        g_xrow = Guard(); g_xf = Guard(); g_sq = Guard(); g_xf16 = Guard()
        g_ptr = [Guard(), Guard()]; g_plg = Guard(); g_ptl = Guard()
        g_xft = [Guard() for _ in range(16)]
        g_lg = Guard(); g_lgt = Guard()
        g_mx = Guard(); g_mi = Guard(); g_nm = Guard(); g_ex = Guard()
        g_ssum = Guard(); g_rs = Guard(); g_t2p = Guard()
        g_mu = Guard(); g_mu2 = Guard(); g_m2 = Guard(); g_sd = Guard(); g_rstd = Guard()

        for k in range(NCH):
            jofs = k
            g_w = g_ysall.w()

            def do_regs(E, jofs=jofs):
                E.reg_load(r_t, s_dyn[0:1, jofs:jofs + 1])
            R.raw("gp", do_regs, after=[lN])
            ysfk = ysf0_h if k < 4 else ysf1_h
            for rk in range(2):
                if rk == 1:
                    def do_regs2(E, jofs=16 + k):
                        E.reg_load(r_t, s_dyn[0:1, jofs:jofs + 1])
                    R.raw("gp", do_regs2)
                ysld = R.dma("qG", "ys", lambda E, ysfk=ysfk, rk=rk: E.dma_start(
                    out=s_ysall[:, rk * 4096:(rk + 1) * 4096]
                    .rearrange("p (blk t) -> p blk t", t=256),
                    in_=bass.AP(ysfk, r_t, [[T, 128], [128 * T, 16], [1, 256]])),
                    after=[c_ys1 if k < 4 else c_ys] + (g_w if rk == 0 else [])
                    + (p2_tail if k == 0 and rk == 0 else []))
            ys_all = ysld

            mm = None
            lastq = [None, None]
            for kk in range(32):
                buf = kk % 4
                opl = R.dma("qG" if buf % 2 == 0 else "qA", "op%d" % buf,
                            lambda E, kk=kk, buf=buf: E.dma_start(
                    out=s_opk[buf][:], in_=opT16_h.ap()[kk * 128:(kk + 1) * 128, :]),
                    after=g_opk[buf].w()
                    + ([c_ys1, opc_done] + p2_tail if k == 0 and kk < 4 else []))
                for q in range(2):
                    for dq in range(4):
                        aft = []
                        if dq == 0:
                            aft = [ys_all, opl]
                            if kk == 0:
                                aft = aft + g_psw[q].w()
                        mm = R.op("pe", lambda E, kk=kk, q=q, dq=dq, buf=buf: E.matmul(
                            out=p_sw[q][:, dq * 512:(dq + 1) * 512],
                            lhsT=s_ysall[:, kk * 256 + q * 128:kk * 256 + (q + 1) * 128],
                            rhs=s_opk[buf][:, dq * 512:(dq + 1) * 512],
                            start=(kk == 0), stop=(kk == 31)), after=aft)
                    lastq[q] = mm
                g_opk[buf].r(mm)
            g_ysall.r(mm)
            for q in range(2):
                ev = R.op("act", lambda E, q=q: E.activation(
                    out=s_sT[q][:], in_=p_sw[q][:], func=AF.Copy),
                    after=[lastq[q]] + g_sT[q].w())
                g_psw[q].r(ev)
            sT_done = ("act", R.cnt["act"])

            for q in range(2):
                row0 = k * 256 + q * 128
                xl = R.dma("qS", "xr3", lambda E, row0=row0: E.dma_start(
                    out=s_xrow[:], in_=xown_h.ap()[row0:row0 + 128, :]),
                    after=g_xrow.w() + (p2_tail if k == 0 and q == 0 else []))
                add = R.op("dve", lambda E, q=q: E.tensor_tensor(
                    out=s_sT[q][:], in0=s_sT[q][:], in1=s_xrow[:], op=ALU.add),
                    after=[sT_done, xl])
                g_xrow.r(add)
                mu = R.op("dve", lambda E, q=q: E.tensor_reduce(
                    out=s_mu[:], in_=s_sT[q][:], axis=AX.X, op=ALU.add), after=[add] + g_mu.w())
                mub = R.op("dve", lambda E: E.tensor_scalar(
                    out=s_mu[:], in0=s_mu[:], scalar1=1.0 / D, scalar2=None, op0=ALU.mult))
                sq = R.op("act", lambda E, q=q: E.activation(
                    out=s_sq[:], in_=s_sT[q][:], func=AF.Square), after=[add] + g_sq.w())
                m2a = R.op("dve", lambda E: E.tensor_reduce(
                    out=s_m2[:], in_=s_sq[:], axis=AX.X, op=ALU.add), after=[sq] + g_m2.w())
                g_sq.r(m2a)
                m2b = R.op("dve", lambda E: E.tensor_scalar(
                    out=s_m2[:], in0=s_m2[:], scalar1=1.0 / D, scalar2=None, op0=ALU.mult))
                mu2v = R.op("dve", lambda E: E.tensor_scalar(
                    out=s_mu2[:], in0=s_mu[:], scalar1=s_mu[:, 0:1], scalar2=None, op0=ALU.mult),
                    after=[mub] + g_mu2.w())
                vv = R.op("dve", lambda E: E.tensor_tensor(
                    out=s_m2[:], in0=s_m2[:], in1=s_mu2[:], op=ALU.subtract), after=[m2b, mu2v])
                g_mu2.r(vv)
                sdv = R.op("act", lambda E: E.activation(
                    out=s_sd[:], in_=s_m2[:], func=AF.Sqrt, bias=s_eps[:, 0:1]),
                    after=[vv] + g_sd.w())
                g_m2.r(sdv)
                rst = R.op("dve", lambda E: E.reciprocal(out=s_rstd[:], in_=s_sd[:]),
                           after=[sdv] + g_rstd.w())
                g_sd.r(rst)
                nmu = R.op("dve", lambda E: E.tensor_scalar(
                    out=s_mu[:], in0=s_mu[:], scalar1=-1.0, scalar2=None, op0=ALU.mult),
                    after=[mu2v])
                g_mu.r(nmu)
                xfv = R.op("dve", lambda E, q=q: E.tensor_scalar(
                    out=s_xf[:], in0=s_sT[q][:], scalar1=s_mu[:, 0:1], scalar2=s_rstd[:, 0:1],
                    op0=ALU.add, op1=ALU.mult), after=[nmu, rst] + g_xf.w())
                g_sT[q].r(xfv); g_mu.r(xfv); g_rstd.r(xfv)
                xf16v = R.op("dve", lambda E: E.tensor_copy(out=s_xf16[:], in_=s_xf[:]),
                             after=[xfv] + g_xf16.w())
                st32 = R.dma("qS", "xf32s", lambda E, row0=row0: E.dma_start(
                    out=xf32o_h.ap()[row0:row0 + 128, :], in_=s_xf[:]), after=[xfv])
                st16 = R.dma("qS", "xf16s", lambda E, row0=row0: E.dma_start(
                    out=xf16o_h.ap()[row0:row0 + 128, :], in_=s_xf16[:]), after=[xf16v])
                g_xf16.r(st16)
                g_xf.r(st32)
                for kk in range(16):
                    slot = kk % 2
                    tr = R.op("pe", lambda E, kk=kk, slot=slot: E.transpose(
                        out=p_sw[0][:, slot * 512:slot * 512 + 128],
                        in_=s_xf[:, kk * 128:(kk + 1) * 128],
                        identity=s_id32_3[:]),
                        after=[xfv, id32l] + g_ptr[slot].w())
                    g_xf.r(tr)
                    tev = R.op("act", lambda E, kk=kk, slot=slot: E.activation(
                        out=s_xft[kk][:], in_=p_sw[0][:, slot * 512:slot * 512 + 128],
                        func=AF.Copy),
                        after=[tr] + g_xft[kk].w())
                    g_ptr[slot].r(tev)
                    g_psw[0].r(tev)
                xft_done = ("act", R.cnt["act"])
                mm2 = None
                for kk in range(16):
                    aft = ([xft_done, rwld] + g_plg.w()) if kk == 0 else []
                    mm2 = R.op("pe", lambda E, kk=kk: E.matmul(
                        out=p_sw[1][0:8, 0:128], lhsT=s_rwT[:, kk * NEXP:(kk + 1) * NEXP],
                        rhs=s_xft[kk][:],
                        start=(kk == 0), stop=(kk == 15)), after=aft)
                for kk in range(16):
                    g_xft[kk].r(mm2)
                lgev = R.op("act", lambda E: E.activation(out=s_lg[:], in_=p_sw[1][0:8, 0:128],
                                                          func=AF.Copy),
                            after=[mm2] + g_lg.w())
                g_plg.r(lgev)
                g_psw[1].r(lgev)
                trl = R.op("pe", lambda E: E.transpose(
                    out=p_sw[1][:, 512:520], in_=s_lg[0:8, :], identity=s_id32_3[0:8, 0:8]),
                    after=[lgev] + g_ptl.w())
                g_lg.r(trl)
                ltv = R.op("act", lambda E: E.activation(out=s_lgt[:], in_=p_sw[1][:, 512:520],
                                                         func=AF.Copy), after=[trl] + g_lgt.w())
                g_ptl.r(ltv)
                g_psw[1].r(ltv)
                mx = R.op("dve", lambda E: E.max(out=s_mx[:], in_=s_lgt[:]),
                          after=[ltv] + g_mx.w())
                g_lgt.r(mx)
                mi = R.op("dve", lambda E: E.max_index(out=s_mi[:], in_max=s_mx[:], in_values=s_lgt[:]),
                          after=[mx] + g_mi.w())
                g_lgt.r(mi)
                nm = R.op("dve", lambda E: E.tensor_scalar(
                    out=s_nm[:], in0=s_mx[:, 0:1], scalar1=-1.0, scalar2=None, op0=ALU.mult),
                    after=[mx] + g_nm.w())
                exv = R.op("act", lambda E: E.activation(out=s_ex[:], in_=s_mx[:], func=AF.Exp,
                                                         bias=s_nm[:, 0:1]),
                           after=[nm] + g_ex.w())
                g_mx.r(exv); g_nm.r(exv)
                ssv = R.op("dve", lambda E: E.tensor_reduce(
                    out=s_ssum[:], in_=s_ex[:], axis=AX.X, op=ALU.add), after=[exv] + g_ssum.w())
                rsv = R.op("dve", lambda E: E.reciprocal(out=s_rs[:], in_=s_ssum[:]),
                           after=[ssv] + g_rs.w())
                g_ssum.r(rsv)
                c0 = R.op("dve", lambda E: E.tensor_copy(out=s_t2p[:, 0:1], in_=s_mi[:, 0:1]),
                          after=[mi] + g_t2p.w())
                c1v = R.op("dve", lambda E: E.tensor_copy(out=s_t2p[:, 1:2], in_=s_mi[:, 1:2]))
                g_mi.r(c1v)
                c2 = R.op("dve", lambda E: E.tensor_tensor(out=s_t2p[:, 2:3], in0=s_ex[:, 0:1],
                                                           in1=s_rs[:, 0:1], op=ALU.mult),
                          after=[rsv])
                c3 = R.op("dve", lambda E: E.tensor_tensor(out=s_t2p[:, 3:4], in0=s_ex[:, 1:2],
                                                           in1=s_rs[:, 0:1], op=ALU.mult))
                g_ex.r(c3); g_rs.r(c3)
                stt2 = R.dma("qS", "t2s", lambda E, row0=row0: E.dma_start(
                    out=top2o_h.ap()[row0:row0 + 128, :], in_=s_t2p[:]), after=[c3, c0, c1v, c2])
                g_t2p.r(stt2)
        p3_tail = R.all_cur()

    c_xf = R.cc(lambda E: E.collective_compute(
        "AllGather", ALU.bypass, replica_groups=[list(range(8))],
        ins=[xf16o_h.ap().opt()], outs=[xffull_h.ap().opt()]), after=p3_tail)
    c_t2 = R.cc(lambda E: E.collective_compute(
        "AllGather", ALU.bypass, replica_groups=[list(range(8))],
        ins=[top2o_h.ap().opt()], outs=[top2full_h.ap().opt()]), after=[c_xf])

    with ExitStack() as P:
        EE = P.enter_context
        s_id32_4 = EE(nc.sbuf_tensor("s_id32p4", [128, 128], F32))
        s_b1r = EE(nc.sbuf_tensor("s_b1r", [128, F // 128], F32))
        s_b2r = EE(nc.sbuf_tensor("s_b2r", [128, D // 128], F32))
        s_ssf = EE(nc.sbuf_tensor("s_ssf", [128, 4], F32))
        s_ssi = EE(nc.sbuf_tensor("s_ssi", [128, 4], I32))
        P4r = ExitStack()
        ER = P4r.enter_context
        s_r4 = ER(nc.sbuf_tensor("s_r4", [128, 512], F32))
        s_E1 = ER(nc.sbuf_tensor("s_E1", [128, 128], F32))
        s_E2 = ER(nc.sbuf_tensor("s_E2", [128, 128], F32))
        s_P1 = ER(nc.sbuf_tensor("s_P1", [128, 128], F32))
        s_P2 = ER(nc.sbuf_tensor("s_P2", [128, 128], F32))
        s_oh = ER(nc.sbuf_tensor("s_oh", [128, 8 * 128], F32))
        s_pos = ER(nc.sbuf_tensor("s_pos", [128, 8 * 128], F32))
        s_keep = ER(nc.sbuf_tensor("s_keep", [128, 8 * 128], F32))
        s_cum = ER(nc.sbuf_tensor("s_cum", [128, 128], F32))
        s_tot = ER(nc.sbuf_tensor("s_tot", [128, 8], F32))
        s_TB = ER(nc.sbuf_tensor("s_TB", [8, 128], F32))
        s_TBc = ER(nc.sbuf_tensor("s_TBc", [8, 128], F32))
        s_prefP = ER(nc.sbuf_tensor("s_prefP", [128, 8], F32))
        s_gt8 = ER(nc.sbuf_tensor("s_gt8", [8, 1], F32))
        s_gtb = ER(nc.sbuf_tensor("s_gtb", [128, 8], F32))
        s_ts1 = ER(nc.sbuf_tensor("s_ts1", [128, 128], F32))
        s_ts2v = ER(nc.sbuf_tensor("s_ts2v", [128, 128], F32))
        s_kp = ER(nc.sbuf_tensor("s_kp", [128, 128], F32))
        s_w1t = ER(nc.sbuf_tensor("s_w1t", [128, 128], F32))
        s_w2t = ER(nc.sbuf_tensor("s_w2t", [128, 128], F32))
        s_tmp = ER(nc.sbuf_tensor("s_tmpP4", [128, 128], F32))
        s_iotac = ER(nc.sbuf_tensor("s_iotac", [128, CAP], F32))
        s_cc4 = ER(nc.sbuf_tensor("s_cc4", [128, 1], I32))
        s_cc4f = ER(nc.sbuf_tensor("s_cc4f", [128, 1], F32))
        s_pcol = ER(nc.sbuf_tensor("s_pcol", [128, 1], F32))
        s_jrow = ER(nc.sbuf_tensor("s_jrow", [128, 128], F32))
        s_lhsA = ER(nc.sbuf_tensor("s_lhsA", [128, 384], BF16))
        s_lhsB = ER(nc.sbuf_tensor("s_lhsB", [128, 384], BF16))
        s_ohj = [ER(nc.sbuf_tensor("s_ohj%d" % s, [128, CAP], BF16)) for s in range(2)]
        s_tab4 = ER(nc.sbuf_tensor("s_tab4", [4, CAP], F32))
        s_ssfh = ER(nc.sbuf_tensor("s_ssfh", [128, 4], F32))
        s_ssfl = ER(nc.sbuf_tensor("s_ssfl", [128, 4], F32))
        p_h1 = [EE(nc.psum_tensor("p_h1%d" % s, [128, CAP], F32)) for s in range(2)]
        p_h2 = [EE(nc.psum_tensor("p_h2%d" % s, [128, CAP], F32)) for s in range(2)]
        p_g16 = [EE(nc.psum_tensor("p_g16%d" % s, [128, 128], BF16)) for s in range(2)]
        p_r = EE(nc.psum_tensor("p_r", [128, 128], F32))
        p_tab4 = EE(nc.psum_tensor("p_tab4", [4, CAP], F32))

        id32l4 = R.dma("qS", "id4", lambda E: E.dma_start(out=s_id32_4[:], in_=ident), after=[c_t2])
        b1l = R.dma("qS", "b1", lambda E: E.dma_start(out=s_b1r[:], in_=b1r), after=[c_t2])
        b2l = R.dma("qS", "b2", lambda E: E.dma_start(out=s_b2r[:], in_=b2r))
        r4l = R.dma("qS", "r4", lambda E: E.dma_start(
            out=s_r4[:], in_=top2full_h.ap().rearrange("(p j) c -> p (j c)", p=128)),
            after=[c_t2])
        exE1 = R.op("dve", lambda E: E.tensor_copy(
            out=s_E1[:], in_=s_r4[:].rearrange("p (j c) -> p j c", c=4)[:, :, 0:1].squeeze(2)),
            after=[r4l])
        exE2 = R.op("dve", lambda E: E.tensor_copy(
            out=s_E2[:], in_=s_r4[:].rearrange("p (j c) -> p j c", c=4)[:, :, 1:2].squeeze(2)))
        exP1 = R.op("dve", lambda E: E.tensor_copy(
            out=s_P1[:], in_=s_r4[:].rearrange("p (j c) -> p j c", c=4)[:, :, 2:3].squeeze(2)))
        exP2 = R.op("dve", lambda E: E.tensor_copy(
            out=s_P2[:], in_=s_r4[:].rearrange("p (j c) -> p j c", c=4)[:, :, 3:4].squeeze(2)))

        ones_bc = s_ones[:].unsqueeze(2).broadcast_to([128, 1, 128]).squeeze(1)
        ones8_bc = s_ones[0:8, :].unsqueeze(2).broadcast_to([8, 1, 128]).squeeze(1)
        gtb_done = None
        agg_done = []
        g_tot = Guard(); g_TB = Guard(); g_TBc = Guard(); g_prefP = Guard(); g_pr = Guard()
        g_cumb = Guard()
        for ph in range(2):
            Ed = s_E1 if ph == 0 else s_E2
            Edev = exE1 if ph == 0 else exE2
            for e in range(E):
                oh = R.op("dve", lambda E_, e=e, Ed=Ed: E_.tensor_single_scalar(
                    out=s_oh[:, e * 128:(e + 1) * 128], in_=Ed[:], scalar=float(e), op=ALU.is_equal),
                    after=[Edev])
                cm = R.op("dve", lambda E_, e=e: E_.tensor_tensor_scan(
                    out=s_cum[:], data0=ones_bc, data1=s_oh[:, e * 128:(e + 1) * 128],
                    initial=0.0, op0=ALU.mult, op1=ALU.add), after=[oh, ones1] + g_cumb.w())
                ps = R.op("dve", lambda E_, e=e: E_.tensor_tensor(
                    out=s_pos[:, e * 128:(e + 1) * 128], in0=s_cum[:],
                    in1=s_oh[:, e * 128:(e + 1) * 128], op=ALU.subtract), after=[cm])
                tc = R.op("dve", lambda E_, e=e: E_.tensor_copy(
                    out=s_tot[:, e:e + 1], in_=s_cum[:, 127:128]),
                    after=[cm] + (g_tot.w() if e == 0 else []))
                g_cumb.r(tc)
            tot_done = ("dve", R.cnt["dve"])
            trt = R.op("pe", lambda E_: E_.transpose(out=p_r[0:8, :], in_=s_tot[:, 0:8],
                                                     identity=s_id32_4[:]),
                       after=[tot_done, id32l4] + g_pr.w())
            g_tot.r(trt)
            tbv = R.op("act", lambda E_: E_.activation(out=s_TB[:], in_=p_r[0:8, :], func=AF.Copy),
                       after=[trt] + g_TB.w())
            g_pr.r(tbv)
            cm8 = R.op("dve", lambda E_: E_.tensor_tensor_scan(
                out=s_TBc[:], data0=ones8_bc, data1=s_TB[:], initial=0.0,
                op0=ALU.mult, op1=ALU.add), after=[tbv, ones1] + g_TBc.w())
            if ph == 0:
                gtc = R.op("dve", lambda E_: E_.tensor_copy(out=s_gt8[:], in_=s_TBc[:, 127:128]),
                           after=[cm8])
                gtd = R.dma("qS", "gt", lambda E_: E_.dma_start(
                    out=gtscr_h.ap().rearrange("1 e -> e 1"), in_=s_gt8[:]), after=[gtc])
                gtb_done = R.dma("qG", "gtb", lambda E_: E_.dma_start(
                    out=s_gtb[:], in_=gtscr_h.ap().to_broadcast([128, NEXP])), after=[gtd])
            prf = R.op("dve", lambda E_: E_.tensor_tensor(out=s_TB[:], in0=s_TBc[:], in1=s_TB[:],
                                                          op=ALU.subtract), after=[cm8])
            trb = R.op("pe", lambda E_: E_.transpose(out=p_r[:, 0:8], in_=s_TB[:],
                                                     identity=s_id32_4[0:8, 0:8]),
                       after=[prf] + g_pr.w())
            g_TB.r(trb)
            prp = R.op("act", lambda E_: E_.activation(out=s_prefP[:], in_=p_r[:, 0:8], func=AF.Copy),
                       after=[trb] + g_prefP.w())
            g_pr.r(prp)
            g_TBc.r(prf)
            for e in range(E):
                if ph == 0:
                    pg = R.op("dve", lambda E_, e=e: E_.tensor_scalar(
                        out=s_pos[:, e * 128:(e + 1) * 128], in0=s_pos[:, e * 128:(e + 1) * 128],
                        scalar1=s_prefP[:, e:e + 1], scalar2=None, op0=ALU.add), after=[prp])
                else:
                    pg = R.op("dve", lambda E_, e=e: E_.tensor_scalar(
                        out=s_pos[:, e * 128:(e + 1) * 128], in0=s_pos[:, e * 128:(e + 1) * 128],
                        scalar1=s_prefP[:, e:e + 1], scalar2=s_gtb[:, e:e + 1],
                        op0=ALU.add, op1=ALU.add), after=[prp, gtb_done])
                g_prefP.r(pg)
                kp = R.op("dve", lambda E_, e=e: E_.tensor_single_scalar(
                    out=s_keep[:, e * 128:(e + 1) * 128], in_=s_pos[:, e * 128:(e + 1) * 128],
                    scalar=float(CAP), op=ALU.is_lt), after=[pg])
                t1_ = R.op("dve", lambda E_, e=e: E_.tensor_scalar(
                    out=s_pos[:, e * 128:(e + 1) * 128], in0=s_pos[:, e * 128:(e + 1) * 128],
                    scalar1=float(e * CAP) - OOBV, scalar2=None, op0=ALU.add), after=[kp])
                t2_ = R.op("dve", lambda E_, e=e: E_.tensor_tensor(
                    out=s_pos[:, e * 128:(e + 1) * 128], in0=s_pos[:, e * 128:(e + 1) * 128],
                    in1=s_keep[:, e * 128:(e + 1) * 128], op=ALU.mult), after=[t1_])
                R.op("dve", lambda E_, e=e: E_.tensor_scalar(
                    out=s_pos[:, e * 128:(e + 1) * 128], in0=s_pos[:, e * 128:(e + 1) * 128],
                    scalar1=OOBV, scalar2=None, op0=ALU.add), after=[t2_])
            slotm_done = ("dve", R.cnt["dve"])
            ts_dst = s_ts1 if ph == 0 else s_ts2v
            z1 = R.op("dve", lambda E_, ts_dst=ts_dst: E_.memset(ts_dst[:], 0.0), after=[slotm_done])
            z2 = R.op("dve", lambda E_: E_.memset(s_kp[:], 0.0))
            for e in range(E):
                tm = R.op("dve", lambda E_, e=e: E_.tensor_tensor(
                    out=s_tmp[:], in0=s_oh[:, e * 128:(e + 1) * 128],
                    in1=s_pos[:, e * 128:(e + 1) * 128], op=ALU.mult))
                R.op("dve", lambda E_, ts_dst=ts_dst: E_.tensor_tensor(
                    out=ts_dst[:], in0=ts_dst[:], in1=s_tmp[:], op=ALU.add))
                tm2 = R.op("dve", lambda E_, e=e: E_.tensor_tensor(
                    out=s_tmp[:], in0=s_oh[:, e * 128:(e + 1) * 128],
                    in1=s_keep[:, e * 128:(e + 1) * 128], op=ALU.mult))
                R.op("dve", lambda E_: E_.tensor_tensor(
                    out=s_kp[:], in0=s_kp[:], in1=s_tmp[:], op=ALU.add))
            wdst = s_w1t if ph == 0 else s_w2t
            Pv = s_P1 if ph == 0 else s_P2
            Pev = exP1 if ph == 0 else exP2
            wv = R.op("dve", lambda E_, wdst=wdst, Pv=Pv: E_.tensor_tensor(
                out=wdst[:], in0=Pv[:], in1=s_kp[:], op=ALU.mult), after=[Pev])
            agg_done.append(("dve", R.cnt["dve"]))

        io0 = R.op("gp", lambda E_: E_.iota(s_iotac[:], [[1, CAP]], 0, 0,
                                            allow_small_or_imprecise_dtypes=True))
        cc4l = R.dma("qG", "ccb", lambda E_: E_.dma_start(
            out=s_cc4[:], in_=dynoff[0:1, 13:14].to_broadcast([128, 1])))
        cc4c = R.op("dve", lambda E_: E_.tensor_copy(out=s_cc4f[:], in_=s_cc4[:]),
                    after=[cc4l])
        iov = R.op("dve", lambda E_: E_.tensor_scalar(
            out=s_iotac[:], in0=s_iotac[:], scalar1=s_cc4f[:, 0:1], scalar2=None,
            op0=ALU.add), after=[io0, cc4c])
        iop = R.op("gp", lambda E_: E_.iota(s_pcol[:], [[1, 1]], 0, 1,
                                            allow_small_or_imprecise_dtypes=True))
        ioj = R.op("gp", lambda E_: E_.iota(s_jrow[:], [[1, 128]], 0, 0,
                                            allow_small_or_imprecise_dtypes=True))
        for (dst, col, srcv, aftev) in (
                (s_lhsA, 0, None, [iop]), (s_lhsA, 1, s_jrow, [ioj]),
                (s_lhsA, 2, s_w1t, [agg_done[0]]),
                (s_lhsB, 0, None, [iop]), (s_lhsB, 1, s_jrow, [ioj]),
                (s_lhsB, 2, s_w2t, [agg_done[1]])):
            R.op("dve", lambda E_, dst=dst, col=col, srcv=srcv: E_.tensor_copy(
                out=dst[:].rearrange("p (j c) -> p j c", c=3)[:, :, col:col + 1].squeeze(2),
                in_=(s_pcol[:].broadcast_to([128, 128]) if srcv is None else srcv[:])),
                after=aftev)
        lhs_done = ("dve", R.cnt["dve"])
        g_ohj = [Guard(), Guard()]
        mmt = None
        for ph in range(2):
            slt = s_ts1 if ph == 0 else s_ts2v
            lhs = s_lhsA if ph == 0 else s_lhsB
            for j in range(128):
                buf = j % 2
                oh = R.op("dve", lambda E_, slt=slt, j=j, buf=buf: E_.tensor_scalar(
                    out=s_ohj[buf][:], in0=s_iotac[:], scalar1=slt[:, j:j + 1], scalar2=None,
                    op0=ALU.is_equal),
                    after=[iov, agg_done[ph]] + g_ohj[buf].w())
                mmt = R.op("pe", lambda E_, ph=ph, j=j, buf=buf, lhs=lhs: E_.matmul(
                    out=p_tab4[0:3, :], lhsT=lhs[:, 3 * j:3 * j + 3], rhs=s_ohj[buf][:],
                    start=(ph == 0 and j == 0), stop=(ph == 1 and j == 127)),
                    after=[oh] + ([lhs_done] if ph == 0 and j == 0 else []))
                g_ohj[buf].r(mmt)
        tabev = R.op("act", lambda E_: E_.activation(out=s_tab4[0:3, :], in_=p_tab4[0:3, :],
                                                     func=AF.Copy), after=[mmt])
        tabst = R.dma("qS", "tab", lambda E_: E_.dma_start(
            out=ptab_h.ap(), in_=s_tab4[0:3, :]), after=[tabev])
        c_tab = R.cc(lambda E_: E_.collective_compute(
            "AllGather", ALU.bypass, replica_groups=[list(range(8))],
            ins=[ptab_h.ap().opt()], outs=[rtab_h.ap().opt()]),
            after=[tabst])

        def do_regs_ss(E_):
            E_.reg_load(r_t, s_dyn[0:1, 12:13])
        R.raw("gp", do_regs_ss, after=[lN])
        ssgh = R.dma("qG", "ssg", lambda E_: E_.dma_start(
            out=s_ssfh[:].unsqueeze(2),
            in_=bass.AP(rtab_h, r_t, [[1, 128], [128, 4], [1, 1]]),
            allow_slow_non_contiguous=True),
            after=[c_tab])

        def do_regs_ss2(E_):
            E_.reg_load(r_t, s_dyn[0:1, 14:15])
        R.raw("gp", do_regs_ss2)
        ssgl = R.dma("qG", "ssg2", lambda E_: E_.dma_start(
            out=s_ssfl[:].unsqueeze(2),
            in_=bass.AP(rtab_h, r_t, [[1, 128], [128, 4], [1, 1]]),
            allow_slow_non_contiguous=True))
        scmb = R.op("dve", lambda E_: E_.scalar_tensor_tensor(
            out=s_ssf[:], in0=s_ssfh[:], scalar=128.0, in1=s_ssfl[:],
            op0=ALU.mult, op1=ALU.add), after=[ssgh, ssgl])
        cssi = R.op("dve", lambda E_: E_.tensor_copy(out=s_ssi[:], in_=s_ssf[:]),
                    after=[scmb])
        P4r.close()
        s_xg = [EE(nc.sbuf_tensor("s_xg%d" % m, [128, D], BF16)) for m in range(4)]
        s_xgT = [EE(nc.sbuf_tensor("s_xgT%d" % kk, [128, CAP], BF16)) for kk in range(16)]
        s_w1f = [EE(nc.sbuf_tensor("s_w1f%d" % s, [128, D], BF16)) for s in range(2)]
        s_h116 = EE(nc.sbuf_tensor("s_h116", [128, 64 * CAP], BF16))
        s_w2d = [EE(nc.sbuf_tensor("s_w2d%d" % s, [128, 32 * 128], BF16)) for s in range(2)]
        s_hf = EE(nc.sbuf_tensor("s_hf", [128, CAP], F32))
        s_hf16 = EE(nc.sbuf_tensor("s_hf16", [128, CAP], BF16))
        s_hT = [EE(nc.sbuf_tensor("s_hT%d" % qq, [128, D], BF16)) for qq in range(4)]
        g_pg = [Guard() for _ in range(2)]
        xgld = []
        for m in range(4):
            xgld.append(R.dma("qG", "xg%d" % m, lambda E_, m=m: E_.indirect_dma_start(
                out=s_xg[m][:], out_offset=None,
                in_=xffull_h.ap(), in_offset=IndirectOffsetOnAxis(s_ssi[:, m:m + 1], 0),
                bounds_check=NTOK - 1, oob_is_err=False),
                after=[cssi, c_xf]))
        for m in range(4):
            for kk in range(16):
                slot = (m * 16 + kk) % 2
                tr = R.op("pe", lambda E_, m=m, kk=kk, slot=slot: E_.transpose(
                    out=p_g16[slot][:], in_=s_xg[m][:, kk * 128:(kk + 1) * 128],
                    identity=s_id16[:]),
                    after=[xgld[m], idc] + g_pg[slot].w())
                tev = R.op("act", lambda E_, m=m, kk=kk, slot=slot: E_.activation(
                    out=s_xgT[kk][:, m * 128:(m + 1) * 128], in_=p_g16[slot][:], func=AF.Copy),
                    after=[tr])
                g_pg[slot].r(tev)
        xgT_done = ("act", R.cnt["act"])

        g_w1f = [Guard() for _ in range(2)]
        g_ph1 = [Guard() for _ in range(2)]
        for f in range(64):
            buf = f % 2
            wld = R.dma("qG", "w1f%d" % buf, lambda E_, f=f, buf=buf: E_.dma_start(
                out=s_w1f[buf][:], in_=bass.AP(w1T_h, f * 128, [[F, 128], [128 * F, 16], [1, 128]])),
                after=g_w1f[buf].w())
            mm = None
            for kk in range(16):
                aft = []
                if kk == 0:
                    aft = [xgT_done, wld] + g_ph1[buf].w()
                mm = R.op("pe", lambda E_, kk=kk, buf=buf: E_.matmul(
                    out=p_h1[buf][:], lhsT=s_w1f[buf][:, kk * 128:(kk + 1) * 128], rhs=s_xgT[kk][:],
                    start=(kk == 0), stop=(kk == 15)), after=aft)
            g_w1f[buf].r(mm)
            gel = R.op("act", lambda E_, f=f, buf=buf: E_.activation(
                out=s_h116[:, f * CAP:(f + 1) * CAP], in_=p_h1[buf][:], func=AF.Gelu,
                bias=s_b1r[:, f:f + 1]), after=[mm, b1l])
            g_ph1[buf].r(gel)
        h1_done = ("act", R.cnt["act"])


        g_w2d = [Guard() for _ in range(2)]
        g_ph2 = [Guard() for _ in range(2)]
        g_hf = Guard(); g_hf16 = Guard()
        g_hT = [Guard() for _ in range(4)]
        hstores = []
        for jd in range(16):
            buf = jd % 2
            mm = None
            for half in range(2):
                wbuf = (jd * 2 + half) % 2
                wld = R.dma("qG", "w2d%d" % wbuf, lambda E_, jd=jd, half=half, wbuf=wbuf: E_.dma_start(
                    out=s_w2d[wbuf][:],
                    in_=bass.AP(w2T_h, jd * 128 + half * 32 * 128 * D,
                                [[D, 128], [128 * D, 32], [1, 128]])),
                    after=g_w2d[wbuf].w())
                for fi in range(32):
                    ff = half * 32 + fi
                    aft = [wld]
                    if ff == 0:
                        aft = [h1_done, wld] + g_ph2[buf].w()
                    mm = R.op("pe", lambda E_, fi=fi, ff=ff, buf=buf, wbuf=wbuf: E_.matmul(
                        out=p_h2[buf][:], lhsT=s_w2d[wbuf][:, fi * 128:(fi + 1) * 128],
                        rhs=s_h116[:, ff * CAP:(ff + 1) * CAP],
                        start=(ff == 0), stop=(ff == 63)),
                        after=(aft if fi == 0 else []))
                g_w2d[wbuf].r(mm)
            hev = R.op("act", lambda E_, buf=buf: E_.activation(
                out=s_hf[:], in_=p_h2[buf][:], func=AF.Copy),
                after=[mm] + g_hf.w())
            g_ph2[buf].r(hev)
            hev = R.op("dve", lambda E_, jd=jd: E_.tensor_scalar(
                out=s_hf[:], in0=s_hf[:], scalar1=s_b2r[:, jd:jd + 1], scalar2=None,
                op0=ALU.add), after=[hev, b2l])
            hc16 = R.op("dve", lambda E_: E_.tensor_copy(out=s_hf16[:], in_=s_hf[:]),
                        after=[hev] + g_hf16.w())
            g_hf.r(hc16)
            for qq in range(4):
                slot = qq % 2
                tr = R.op("pe", lambda E_, qq=qq, slot=slot: E_.transpose(
                    out=p_g16[slot][:], in_=s_hf16[:, qq * 128:(qq + 1) * 128], identity=s_id16[:]),
                    after=[hc16] + g_pg[slot].w())
                g_hf16.r(tr)
                tev = R.op("act", lambda E_, qq=qq, jd=jd, slot=slot: E_.activation(
                    out=s_hT[qq][:, jd * 128:(jd + 1) * 128], in_=p_g16[slot][:], func=AF.Copy),
                    after=[tr] + (g_hT[qq].w() if jd == 0 else []))
                g_pg[slot].r(tev)
        hT_done = ("act", R.cnt["act"])
        for qq in range(4):
            hstores.append(R.dma("qS", "hst", lambda E_, qq=qq: E_.dma_start(
                out=hfino_h.ap()[qq * 128:(qq + 1) * 128, :], in_=s_hT[qq][:]), after=[hT_done]))
        p4_tail = R.all_cur()

    c_h = R.cc(lambda E_: E_.collective_compute(
        "AllGather", ALU.bypass, replica_groups=[list(range(8))],
        ins=[hfino_h.ap().opt()], outs=[hfull_h.ap().opt()]), after=p4_tail)

    NSB = NEXP * CAP // 128
    with ExitStack() as P:
        EE = P.enter_context
        s_st2 = EE(nc.sbuf_tensor("s_st2", [128, 2 * NSB], F32))
        s_st4 = EE(nc.sbuf_tensor("s_st4", [128, 3 * NSB], F32))
        s_cb5 = EE(nc.sbuf_tensor("s_cb5", [128, 1], I32))
        s_cbf = EE(nc.sbuf_tensor("s_cbf", [128, 1], F32))
        s_gidb = EE(nc.sbuf_tensor("s_gidb", [128, TOKPC], F32))
        s_MT = EE(nc.sbuf_tensor("s_MT", [128, NSB * 512], BF16))
        s_hfb = [EE(nc.sbuf_tensor("s_hfb%d" % s, [128, 1024], BF16)) for s in range(4)]
        s_ffn4 = EE(nc.sbuf_tensor("s_ffn4", [128, 4 * D], F32))
        s_xr5 = EE(nc.sbuf_tensor("s_xr5", [128, D], F32))
        s_sq5 = EE(nc.sbuf_tensor("s_sq5", [128, D], F32))
        s_oout = EE(nc.sbuf_tensor("s_oout", [128, D], F32))
        s_mu5 = EE(nc.sbuf_tensor("s_mu5", [128, 1], F32))
        s_mu25 = EE(nc.sbuf_tensor("s_mu25", [128, 1], F32))
        s_m25 = EE(nc.sbuf_tensor("s_m25", [128, 1], F32))
        s_sd5 = EE(nc.sbuf_tensor("s_sd5", [128, 1], F32))
        s_rstd5 = EE(nc.sbuf_tensor("s_rstd5", [128, 1], F32))
        p_acc = [EE(nc.psum_tensor("p_acc%d" % t, [128, 1024], F32)) for t in range(4)]

        stl = R.dma("qS", "stl", lambda E_: E_.dma_start(
            out=s_st4[:].rearrange("p (c r q) -> p c r q", r=3, q=4),
            in_=bass.AP(rtab_h, 0, [[1, 128], [1536, 8], [512, 3], [128, 4]]),
            allow_slow_non_contiguous=True), after=[c_tab])
        _v4 = s_st4[:].rearrange("p (c r q) -> p c r q", r=3, q=4)
        _hi = _v4[:, :, 0:1, :].squeeze(2)
        _lo = _v4[:, :, 1:2, :].squeeze(2)
        _wv = _v4[:, :, 2:3, :].squeeze(2)
        _s2 = s_st2[:].rearrange("p (c q c2) -> p c q c2", q=4, c2=2)
        _ido = _s2[:, :, :, 0:1].squeeze(3)
        _wo = _s2[:, :, :, 1:2].squeeze(3)
        cmb1 = R.op("dve", lambda E_: E_.scalar_tensor_tensor(
            out=_ido, in0=_hi, scalar=128.0, in1=_lo, op0=ALU.mult, op1=ALU.add),
            after=[stl])
        cmb2 = R.op("dve", lambda E_: E_.tensor_copy(out=_wo, in_=_wv))
        st2_done = cmb2
        cbl = R.dma("qG", "cbl", lambda E_: E_.dma_start(
            out=s_cb5[:], in_=dynoff[0:1, 8:9].to_broadcast([128, 1])))
        cbc = R.op("dve", lambda E_: E_.tensor_copy(out=s_cbf[:], in_=s_cb5[:]),
                   after=[cbl])
        gio = R.op("gp", lambda E_: E_.iota(s_gidb[:], [[1, TOKPC]], 0, 0,
                                            allow_small_or_imprecise_dtypes=True))
        gid = R.op("dve", lambda E_: E_.tensor_scalar(
            out=s_gidb[:], in0=s_gidb[:], scalar1=s_cbf[:, 0:1], scalar2=None,
            op0=ALU.add), after=[cbc, gio])

        g_MT = [Guard() for _ in range(NSB)]
        g_hfb = [Guard() for _ in range(4)]
        g_pacc = [Guard() for _ in range(4)]
        g_ffn = [Guard() for _ in range(4)]
        g_xr5 = Guard(); g_sq5 = Guard(); g_oout = Guard()
        ihf = 0
        laststore = None
        for quad in range(4):
            mtev = []
            for sb in range(NSB):
                ev = R.op("dve", lambda E_, sb=sb, quad=quad: E_.tensor_scalar(
                    out=s_MT[:, sb * 512:(sb + 1) * 512],
                    in0=s_gidb[:, quad * 512:(quad + 1) * 512],
                    scalar1=s_st2[:, 2 * sb:2 * sb + 1],
                    scalar2=s_st2[:, 2 * sb + 1:2 * sb + 2],
                    op0=ALU.is_equal, op1=ALU.mult),
                    after=[gid, st2_done] + g_MT[sb].w())
                mtev.append(ev)
            evq = [[None] * 4, [None] * 4]
            for dmH in range(2):
                lastmm = [None] * 4
                for sb in range(NSB):
                    buf = ihf % 4
                    ihf += 1
                    ld = R.dma("qS" if buf % 2 == 0 else "qA", "hf%d" % buf,
                               lambda E_, sb=sb, dmH=dmH, buf=buf: E_.dma_start(
                        out=s_hfb[buf][:],
                        in_=hfull_h.ap()[sb * 128:(sb + 1) * 128,
                                         dmH * 1024:(dmH + 1) * 1024]),
                        after=[c_h] + g_hfb[buf].w())
                    mm = None
                    for tt in range(4):
                        for nh in range(2):
                            aft = []
                            if tt == 0 and nh == 0:
                                aft = [mtev[sb], ld]
                                if sb == 0:
                                    for t2 in range(4):
                                        aft += g_pacc[t2].w()
                            mm = R.op("pe", lambda E_, sb=sb, tt=tt, nh=nh, buf=buf: E_.matmul(
                                out=p_acc[tt][:, nh * 512:(nh + 1) * 512],
                                lhsT=s_MT[:, sb * 512 + tt * 128:sb * 512 + (tt + 1) * 128],
                                rhs=s_hfb[buf][:, nh * 512:(nh + 1) * 512],
                                start=(sb == 0), stop=(sb == NSB - 1)), after=aft)
                        lastmm[tt] = mm
                    g_hfb[buf].r(mm)
                    if dmH == 1:
                        g_MT[sb].r(mm)
                for tt in range(4):
                    ev = R.op("act", lambda E_, tt=tt, dmH=dmH: E_.activation(
                        out=s_ffn4[:, tt * D + dmH * 1024:tt * D + (dmH + 1) * 1024],
                        in_=p_acc[tt][:], func=AF.Copy),
                        after=[lastmm[tt]] + (g_ffn[tt].w() if dmH == 0 else []))
                    g_pacc[tt].r(ev)
                    evq[dmH][tt] = ev
            for tt in range(4):
                m_ = quad * 4 + tt
                sl = s_ffn4[:, tt * D:(tt + 1) * D]
                xl = R.dma("qS", "xr5", lambda E_, m_=m_: E_.dma_start(
                    out=s_xr5[:], in_=xf32o_h.ap()[m_ * 128:(m_ + 1) * 128, :]),
                    after=g_xr5.w())
                sv = R.op("dve", lambda E_, sl=sl: E_.tensor_tensor(
                    out=sl, in0=sl, in1=s_xr5[:], op=ALU.add),
                    after=[xl, evq[1][tt]])
                g_xr5.r(sv)
                mu = R.op("dve", lambda E_, sl=sl: E_.tensor_reduce(
                    out=s_mu5[:], in_=sl, axis=AX.X, op=ALU.add), after=[sv])
                mub = R.op("dve", lambda E_: E_.tensor_scalar(
                    out=s_mu5[:], in0=s_mu5[:], scalar1=1.0 / D, scalar2=None, op0=ALU.mult))
                sq = R.op("act", lambda E_, sl=sl: E_.activation(
                    out=s_sq5[:], in_=sl, func=AF.Square), after=[sv] + g_sq5.w())
                m2a = R.op("dve", lambda E_: E_.tensor_reduce(
                    out=s_m25[:], in_=s_sq5[:], axis=AX.X, op=ALU.add), after=[sq])
                g_sq5.r(m2a)
                m2b = R.op("dve", lambda E_: E_.tensor_scalar(
                    out=s_m25[:], in0=s_m25[:], scalar1=1.0 / D, scalar2=None, op0=ALU.mult))
                mu2v = R.op("dve", lambda E_: E_.tensor_scalar(
                    out=s_mu25[:], in0=s_mu5[:], scalar1=s_mu5[:, 0:1], scalar2=None,
                    op0=ALU.mult), after=[mub])
                vv = R.op("dve", lambda E_: E_.tensor_tensor(
                    out=s_m25[:], in0=s_m25[:], in1=s_mu25[:], op=ALU.subtract),
                    after=[m2b, mu2v])
                sdv = R.op("act", lambda E_: E_.activation(
                    out=s_sd5[:], in_=s_m25[:], func=AF.Sqrt, bias=s_eps[:, 0:1]), after=[vv])
                rst = R.op("dve", lambda E_: E_.reciprocal(out=s_rstd5[:], in_=s_sd5[:]),
                           after=[sdv])
                nmu = R.op("dve", lambda E_: E_.tensor_scalar(
                    out=s_mu5[:], in0=s_mu5[:], scalar1=-1.0, scalar2=None, op0=ALU.mult),
                    after=[mu2v])
                ov = R.op("dve", lambda E_, sl=sl: E_.tensor_scalar(
                    out=s_oout[:], in0=sl, scalar1=s_mu5[:, 0:1], scalar2=s_rstd5[:, 0:1],
                    op0=ALU.add, op1=ALU.mult), after=[nmu, rst] + g_oout.w())
                g_ffn[tt].r(ov)
                ost = R.dma("qS", "out", lambda E_, m_=m_: E_.dma_start(
                    out=outown[m_ * 128:(m_ + 1) * 128, :], in_=s_oout[:]), after=[ov])
                g_oout.r(ost)
                laststore = ost

    for _fam in list(R.famstream):
        _eng = "qS" if R.famstream[_fam] == "sync" else "gp"
        R.raw(_eng, lambda E_, s=_fam, v=R.cnt[_fam]: E_.wait_ge(R.semh[s], v))

    with nc.Block() as block:
        block.sync(lambda e: [f(e) for f in R.streams["sync"]])
        block.scalar(lambda e: [f(e) for f in R.streams["scalar"]])
        block.vector(lambda e: [f(e) for f in R.streams["vector"]])
        block.tensor(lambda e: [f(e) for f in R.streams["tensor"]])
        block.gpsimd(lambda e: [f(e) for f in R.streams["gpsimd"]])
    ctx.close()
    return nc


def _dynoff(c, h):
    d = np.zeros((1, 32), np.int32)
    for k in range(NCH):
        d[0, k] = (k % 4) * HC * T + h * 256
        d[0, 16 + k] = (k % 4) * HC * T + h * 256 + 4 * HC * T
    for r_i in range(4):
        d[0, 8 + r_i] = r_i * NTOK + c * TOKPC
    d[0, 12] = c * CAP * 3
    d[0, 13] = c * CAP
    d[0, 14] = c * CAP * 3 + CAP
    return d


def make_inmaps(inputs):
    x = np.asarray(inputs["L0_attn_x"], np.float32)
    ipw = np.asarray(inputs["L0_attn_in_proj_w"], np.float32)
    cwf = np.asarray(inputs["L0_attn_conv_w"], np.float32)
    cbf = np.asarray(inputs["L0_attn_conv_b"], np.float32)
    xpw = np.asarray(inputs["L0_attn_x_proj_w"], np.float32)
    dtw = np.asarray(inputs["L0_attn_dt_proj_w"], np.float32)
    dtbf = np.asarray(inputs["L0_attn_dt_proj_b"], np.float32)
    alogf = np.asarray(inputs["L0_attn_A_log"], np.float32)
    dv = np.asarray(inputs["L0_attn_D"], np.float32)
    opw = np.asarray(inputs["L0_attn_out_proj_w"], np.float32)
    rw = np.asarray(inputs["L0_ffn_router_w"], np.float32)
    w1 = np.asarray(inputs["L0_ffn_w1"], np.float32)
    b1 = np.asarray(inputs["L0_ffn_b1"], np.float32)
    w2 = np.asarray(inputs["L0_ffn_w2"], np.float32)
    b2 = np.asarray(inputs["L0_ffn_b2"], np.float32)

    C = np.ascontiguousarray
    in_maps = []
    own_rows_all = []
    for c in range(8):
        b, h = c // 2, c % 2
        hs = slice(h * HC, (h + 1) * HC)
        rows = np.concatenate([np.arange(k * 512 + h * 256, k * 512 + h * 256 + 256)
                               for k in range(NCH)])
        own_rows_all.append(b * L + rows)
        pack = lambda a: C(a.reshape(NB, 128, -1).transpose(1, 0, 2).reshape(128, -1))
        m = {
            "xb": C(x[b]),
            "xown": C(x[b][rows]),
            "wuT": C(ipw[hs, :].T),
            "wzT": C(ipw[INNER + h * HC: INNER + (h + 1) * HC, :].T),
            "cw": pack(cwf[hs, 0, :]),
            "cb": pack(cbf[hs].reshape(-1, 1)),
            "xpT": C(xpw[:, hs].T),
            "dtT": C(dtw[hs, :].T),
            "dtb": pack(dtbf[hs].reshape(-1, 1)),
            "alog": pack(alogf[hs, :]),
            "dvec": pack(dv[hs].reshape(-1, 1)),
            "opT": C(opw.T),
            "rwT": C(rw.T),
            "w1T": C(w1.T),
            "b1r": C(b1.reshape(F // 128, 128).T),
            "w2T": C(w2.T),
            "b2r": C(b2.reshape(D // 128, 128).T),
            "ident": np.eye(128, dtype=np.float32),
            "sids": C((c * CAP + np.arange(CAP, dtype=np.int32)).reshape(4, 128).T),
            "hoff": np.array([[h * 256]], np.int32),
            "dynoff": _dynoff(c, h),
            "coff": np.array([[c * TOKPC]], np.int32),
            "epst": np.full((128, 1), 1e-5, np.float32),
        }
        in_maps.append(m)
    return in_maps, own_rows_all


def kernel(**inputs):
    in_maps, own_rows_all = make_inmaps(inputs)
    nc = build_program()
    res = run_bass_kernel_spmd(nc, in_maps, core_ids=list(range(8)))
    out = np.empty((B, L, D), np.float32)
    flat = out.reshape(NTOK, D)
    for c in range(8):
        flat[own_rows_all[c]] = res.results[c]["outown"]
    return out
